# version 06
# baseline (speedup 1.0000x reference)
import numpy as np
import ml_dtypes
import concourse.bass as bass
import concourse.mybir as mybir
from concourse.bass_utils import run_bass_kernel_spmd

F32 = mybir.dt.float32
BF16 = mybir.dt.bfloat16
I32 = mybir.dt.int32
AF = mybir.ActivationFunctionType
ALU = mybir.AluOpType
AX = mybir.AxisListType
IOA = bass.IndirectOffsetOnAxis

NCORE = 8
D = 1024
T_OWN = 2048
NT = 16
E = 8
CAP = 1280
NSLOT = E * 2 * CAP
SLOT_OWN = 2 * CAP
HID = 4096
LN_EPS = 1e-5

ENGINES = ("sync", "gpsimd", "scalar", "vector", "tensor")
UPTO = [99]
A5PARTS = [True, True, True, True, True, True, True]

from contextlib import contextmanager
from concourse.ordered_engine_set import OrderedEngineSet

_DYN_ID = [0]


@contextmanager
def dynloop(nc, eng, creg, end_sv):
    _DYN_ID[0] += 1
    lname = f"dyn{_DYN_ID[0]}_loop"
    ename = f"dyn{_DYN_ID[0]}_end"
    engset = OrderedEngineSet([eng.engine])
    eng.reg_mov(creg, 0)
    eng.br_lt(creg, end_sv, on_true=lname, on_false=ename)
    with nc.body(lname, valid_engines=engset):
        yield
        eng.reg_add(creg, creg, 1)
        eng.br_lt(creg, end_sv, on_true=lname, on_false=ename)
    nc.switch_bb(ename)



class Sem:
    def __init__(self, h):
        self.h = h
        self.n = 0

    def inc(self, k=1):
        self.n += k
        return self.n

    @property
    def now(self):
        return (self, self.n)


class Prog:
    def __init__(self):
        self.ops = {e: [] for e in ENGINES}

    def add(self, eng, fn):
        self.ops[eng].append(fn)


def _w(e, waits):
    for s, v in waits:
        if v > 0:
            e.wait_ge(s.h, v)


_DVE_COMPUTE = {
    "tensor_copy", "tensor_tensor", "tensor_add", "tensor_sub", "tensor_mul",
    "tensor_max", "tensor_scalar", "tensor_single_scalar", "tensor_scalar_add",
    "tensor_scalar_sub", "tensor_scalar_mul", "tensor_scalar_max",
    "tensor_scalar_min", "tensor_relu", "tensor_reduce", "reduce_max",
    "reduce_sum", "scalar_tensor_tensor", "tensor_tensor_scan", "memset",
    "memzero", "reciprocal", "max", "max_index", "select", "copy_predicated",
    "bn_stats", "bn_aggr", "transpose", "match_replace",
}


class AutoDrain:

    def __init__(self, v):
        object.__setattr__(self, "_v", v)
        object.__setattr__(self, "_n", 0)

    def __getattr__(self, name):
        v = object.__getattribute__(self, "_v")
        fn = getattr(v, name)
        if name in _DVE_COMPUTE:
            def call(*a, **k):
                n = object.__getattribute__(self, "_n")
                if n > 0:
                    v.drain()
                object.__setattr__(self, "_n", n + 1)
                return fn(*a, **k)
            return call
        return fn


def op(P, eng, waits=(), inc=None, inck=1):
    if inc is not None:
        inc.inc(inck)

    def deco(fn):
        def run(e):
            if eng == "vector":
                e = AutoDrain(e)
            _w(e, waits)
            r = fn(e)
            if inc is not None:
                r.then_inc(inc.h, inck)
        P.add(eng, run)
        return None
    return deco


def dma(P, eng, out, in_, sem, waits=(), **kw):
    sem.inc(16)

    def run(e):
        _w(e, waits)
        e.dma_start(out=out, in_=in_, **kw).then_inc(sem.h, 16)
    P.add(eng, run)


def build_program(phases=("ffn1",), debug=False, adbg=False):
    nc = bass.Bass("TRN2", target_bir_lowering=False)

    x_my = nc.dram_tensor("x_my", [T_OWN, D], F32, kind="ExternalInput")
    wr_in = nc.dram_tensor("wr_in", [D, E], F32, kind="ExternalInput")
    w1_in = nc.dram_tensor("w1_in", [D, HID], F32, kind="ExternalInput")
    w2_in = nc.dram_tensor("w2_in", [HID, D], F32, kind="ExternalInput")
    b1c_in = nc.dram_tensor("b1c_in", [128, 32], F32, kind="ExternalInput")
    b2r_in = nc.dram_tensor("b2r_in", [1, D], F32, kind="ExternalInput")
    brb_in = nc.dram_tensor("brb_in", [128, E], F32, kind="ExternalInput")
    ident_in = nc.dram_tensor("ident_in", [128, 128], F32, kind="ExternalInput")
    iota8_in = nc.dram_tensor("iota8_in", [128, E], F32, kind="ExternalInput")
    selpref_in = nc.dram_tensor("selpref_in", [16, 2], F32, kind="ExternalInput")
    consts_in = nc.dram_tensor("consts_in", [128, 4], F32, kind="ExternalInput")
    wblk_in = nc.dram_tensor("wblk_in", [65, 128, 1024], F32, kind="ExternalInput")
    wv_in = nc.dram_tensor("wv_in", [8, 128, 2048], F32, kind="ExternalInput")
    binc_in = nc.dram_tensor("binc_in", [128, 65], F32, kind="ExternalInput")
    binv_in = nc.dram_tensor("binv_in", [1, 2048], F32, kind="ExternalInput")
    woutT_in = nc.dram_tensor("woutT_in", [2048, 1024], F32, kind="ExternalInput")
    lnw4_in = nc.dram_tensor("lnw4_in", [128, 4 * 1024], F32, kind="ExternalInput")
    hpar_in = nc.dram_tensor("hpar_in", [128, 32], F32, kind="ExternalInput")
    masks_in = nc.dram_tensor("masks_in", [128, 4 * 512], F32, kind="ExternalInput")
    icon_in = nc.dram_tensor("icon_in", [1, 16], I32, kind="ExternalInput")
    y_my = nc.dram_tensor("y_my", [T_OWN, D], F32, kind="ExternalOutput")
    if debug:
        d_ln = nc.dram_tensor("d_ln", [NCORE * T_OWN, D], BF16, kind="ExternalOutput")
        d_logits = nc.dram_tensor("d_logits", [128, NT * E], F32, kind="ExternalOutput")
        d_posg = nc.dram_tensor("d_posg", [E, T_OWN], F32, kind="ExternalOutput")
        d_cnt = nc.dram_tensor("d_cnt", [16, E], F32, kind="ExternalOutput")
        d_offs = nc.dram_tensor("d_offs", [E, 4], F32, kind="ExternalOutput")
        d_map = nc.dram_tensor("d_map", [SLOT_OWN, 1], F32, kind="ExternalOutput")
        d_ybin = nc.dram_tensor("d_ybin", [SLOT_OWN, D], BF16, kind="ExternalOutput")
        d_earr = nc.dram_tensor("d_earr", [128, 2 * NT], F32, kind="ExternalOutput")
        d_parr = nc.dram_tensor("d_parr", [128, 2 * NT], F32, kind="ExternalOutput")
        d_warr = nc.dram_tensor("d_warr", [128, 2 * NT], F32, kind="ExternalOutput")
        d_ggix = nc.dram_tensor("d_ggix", [128, 2 * NT], I32, kind="ExternalOutput")
        d_postm = nc.dram_tensor("d_postm", [128, NT * E], F32, kind="ExternalOutput")
        d_buf = nc.dram_tensor("d_buf", [128, 8 * SLOT_OWN], BF16, kind="ExternalOutput")
        d_hfm = nc.dram_tensor("d_hfm", [128, 32 * 512], BF16, kind="ExternalOutput")
        d_w1 = nc.dram_tensor("d_w1", [128, 8 * HID], BF16, kind="ExternalOutput")

    ln_ag_in = nc.dram_tensor("ln_ag_in", [T_OWN, D], BF16)
    ln_ag_out = nc.dram_tensor("ln_ag_out", [NCORE * T_OWN, D], BF16)
    cnt_ag_in = nc.dram_tensor("cnt_ag_in", [2, E], F32)
    cnt_ag_out = nc.dram_tensor("cnt_ag_out", [16, E], F32)
    rs_in = nc.dram_tensor("rs_in", [NSLOT, 1], F32)
    rs_out = nc.dram_tensor("rs_out", [SLOT_OWN, 1], F32)
    yb_ag_in = nc.dram_tensor("yb_ag_in", [SLOT_OWN, D], BF16)
    yb_ag_out = nc.dram_tensor("yb_ag_out", [NSLOT, D], BF16)
    w2bf = nc.dram_tensor("w2bf", [HID, D], BF16)
    x_res = nc.dram_tensor("x_res", [T_OWN, D], F32)
    kvy_ag_in = nc.dram_tensor("kvy_ag_in", [4224, 2048], BF16)
    kvy_ag_out = nc.dram_tensor("kvy_ag_out", [4 * 4224, 2048], BF16)
    q_fm = nc.dram_tensor("q_fm", [2048, T_OWN], BF16)
    p_fm = nc.dram_tensor("p_fm", [2048, T_OWN], BF16)
    o_fm = nc.dram_tensor("o_fm", [2048, T_OWN], BF16)
    pos_dram = nc.dram_tensor("pos_dram", [16, 8192], F32)
    mneg_dram = nc.dram_tensor("mneg_dram", [16, 2048], BF16)

    ctxs = []

    def S(name):
        h = nc.semaphore(name)
        ctxs.append(h)
        return Sem(h.__enter__())

    def SB(name, shape, dt):
        h = nc.sbuf_tensor(name, shape, dt)
        ctxs.append(h)
        return h.__enter__()

    def PSUM(name, shape, dt):
        h = nc.psum_tensor(name, shape, dt)
        ctxs.append(h)
        return h.__enter__()

    ident = SB("ident", [128, 128], F32)
    identb = SB("identb", [128, 128], BF16)
    iota8 = SB("iota8", [128, E], F32)
    brb = SB("brb", [128, E], F32)
    b1c = SB("b1c", [128, 32], F32)
    b2r = SB("b2r", [1, D], F32)
    selpref = SB("selpref", [16, 2], F32)
    cconst = SB("cconst", [128, 4], F32)
    ones_f = SB("ones_f", [1, 128], F32)
    epsc = SB("epsc", [128, 1], F32)
    wrb = SB("wrb", [128, 8 * E], BF16)
    rowval = SB("rowval", [128, NT], F32)
    zed = SB("zed", [128, 160], F32)
    xt = SB("xt", [128, 2 * D], F32)
    sqt = SB("sqt", [128, D], F32)
    lnt = SB("lnt", [128, 2 * D], BF16)
    stat = SB("stat", [128, 16], F32)
    small = SB("small", [128, 16], F32)
    logits = SB("logits", [128, NT * E], F32)
    parr = SB("parr", [128, 2 * NT], F32)
    warr = SB("warr", [128, 2 * NT], F32)
    earr = SB("earr", [128, 2 * NT], F32)
    ohs1 = SB("ohs1", [128, NT * E], F32)
    ohsb = SB("ohsb", [128, E], BF16)
    cagb = SB("cagb", [16, E], F32)
    offs = SB("offs", [E, 4], F32)
    cnt_sb = SB("cnt_sb", [2, E], F32)
    postm = SB("postm", [128, NT * E], F32)
    gidx = SB("gidx", [128, 2 * NT], I32)
    ggix = SB("ggix", [128, 2 * NT], I32)
    mapcol = SB("mapcol", [128, 20], F32)
    mapi = SB("mapi", [128, 20], I32)
    w2sb = SB("w2sb", [128, 2 * 512], BF16)
    ysb = SB("ysb", [128, 4 * D], BF16)
    ygb = SB("ygb", [128, 2 * D], BF16)
    accv = SB("accv", [128, D], F32)
    binc = SB("binc", [128, 65], F32)
    hpar = SB("hpar", [128, 32], F32)
    qscale = SB("qscale", [128, 16], F32)
    smsig = SB("smsig", [128, 16], F32)
    binqs = SB("binqs", [128, 16], F32)
    icon = SB("icon", [1, 16], I32)
    onescol = SB("onescol", [128, 1], BF16)
    zrow = SB("zrow", [1, 512], F32)
    onesb = SB("onesb", [1, 128], BF16)
    zrowb = SB("zrowb", [1, 512], BF16)
    carry = SB("carry", [16, 2], F32)
    poscol = SB("poscol", [128, 128], F32)
    pc64 = SB("pc64", [64, 128], F32)
    pcstg = SB("pcstg", [128, 8], F32)
    densb = SB("densb", [1, 512], F32)
    recsb = SB("recsb", [1, 512], F32)
    recb = SB("recb", [128, 512], F32)

    arena = SB("arena", [128, 73728], BF16)
    O_LNFM, O_BUFFM, O_W1, O_HFM, O_BUFTM = 0, 0, 20480, 53248, 69632
    ohs_fm = arena[0:E, 53248:55296]
    cinc = arena[0:E, 55296:59392].bitcast(F32)
    posg = arena[0:E, 59392:63488].bitcast(F32)

    tps = PSUM("tps", [128, 128], BF16)
    psA = PSUM("psA", [128, 512], F32)
    psB = PSUM("psB", [128, 512], F32)
    psY = [PSUM(f"psY{i}", [128, 512], F32) for i in range(4)]
    psY0, psY1, psY2, psY3 = psY

    ld = S("ld")
    lx = S("lx")
    gp = S("gp")
    cc = S("cc")
    mm = S("mm")
    dv = S("dv")
    ac = S("ac")
    st = S("st")
    wst = S("wst")
    asg = S("asg")
    amm = S("amm")
    aac = S("aac")
    aen = S("aen")
    adm = S("adm")
    aep = S("aep")
    abc = S("abc")
    aev = S("aev")
    aib = S("aib")
    ahd = S("ahd")
    aos = S("aos")
    akv = S("akv")
    ldp = S("ldp")
    ldm = S("ldm")
    ldq2 = S("ldq2")
    lxx = S("lxx")
    wvs = S("wvs")

    P = Prog()
    regs = {}

    def _initregs(g):
        regs["bc_slot"] = g.to_reg(NSLOT - 1)
        regs["bc_row"] = g.to_reg(NCORE * T_OWN - 1)
    P.add("gpsimd", _initregs)

    dma(P, "sync", ident[:, :], ident_in.ap(), ld)
    dma(P, "sync", iota8[:, :], iota8_in.ap(), ld)
    dma(P, "sync", brb[:, :], brb_in.ap(), ld)
    dma(P, "sync", b1c[:, :], b1c_in.ap(), ld)
    dma(P, "sync", b2r[:, :], b2r_in.ap(), ld)
    dma(P, "sync", selpref[:, :], selpref_in.ap(), ld)
    dma(P, "sync", cconst[:, :], consts_in.ap(), ld)
    dma(P, "gpsimd", wrb[:, :].rearrange("p (c e) -> p c e", e=E),
        wr_in.ap().rearrange("(c p) e -> p c e", p=128), gp)

    @op(P, "gpsimd", inc=gp)
    def _(g):
        g.memset(ones_f[:, :], 1.0)
        g.memset(epsc[:, :], LN_EPS)
        g.memset(zed[:, :], 0.0)
        return g.iota(rowval[:, :], [[128, NT]], channel_multiplier=1,
                      allow_small_or_imprecise_dtypes=True)

    @op(P, "vector", waits=[ld.now, gp.now], inc=dv)
    def _(v):
        v.tensor_copy(identb[:, :], ident[:, :])
        return v.tensor_scalar_add(rowval[:, :], rowval[:, :], cconst[:, 1:2])
    dv_setup = dv.n

    def emit_ffn(src_dram, dst_dram, first_call, entry=()):
        lx0 = lx.n
        ac_ln = [None] * NT
        st_ln = [None] * NT
        mm_tp = [None] * NT
        for t in range(NT):
            s0 = (t % 2) * D
            sl = slice(s0, s0 + D)
            wx = [(ac, ac_ln[t - 2])] if t >= 2 else list(entry)
            dma(P, "scalar", xt[:, sl], src_dram[t * 128:(t + 1) * 128, :],
                lx, waits=wx)

            @op(P, "vector", waits=[(lx, lx0 + 16 * (t + 1)), (dv, dv_setup)], inc=dv)
            def _(v, sl=sl):
                v.reduce_sum(stat[:, 0:1], xt[:, sl], axis=AX.X)
                return v.tensor_scalar_mul(stat[:, 1:2], stat[:, 0:1], 1.0 / D)

            @op(P, "scalar", waits=[dv.now], inc=ac)
            def _(s, sl=sl):
                return s.activation(sqt[:, :], xt[:, sl], AF.Square)

            @op(P, "vector", waits=[ac.now], inc=dv)
            def _(v):
                v.reduce_sum(stat[:, 2:3], sqt[:, :], axis=AX.X)
                v.tensor_scalar_mul(stat[:, 3:4], stat[:, 2:3], 1.0 / D)
                v.tensor_mul(stat[:, 4:5], stat[:, 1:2], stat[:, 1:2])
                return v.tensor_sub(stat[:, 5:6], stat[:, 3:4], stat[:, 4:5])

            @op(P, "scalar", waits=[dv.now], inc=ac)
            def _(s):
                return s.activation(stat[:, 6:7], stat[:, 5:6], AF.Sqrt, bias=epsc[:, :])

            @op(P, "vector", waits=[ac.now], inc=dv)
            def _(v):
                v.reciprocal(stat[:, 7:8], stat[:, 6:7])
                v.tensor_mul(stat[:, 8:9], stat[:, 1:2], stat[:, 7:8])
                return v.tensor_scalar_mul(stat[:, 9:10], stat[:, 8:9], -1.0)

            w = [dv.now]
            if t >= 2:
                w += [(st, st_ln[t - 2]), (mm, mm_tp[t - 2])]

            @op(P, "scalar", waits=w, inc=ac)
            def _(s, sl=sl):
                return s.activation(lnt[:, sl], xt[:, sl], AF.Identity,
                                    bias=stat[:, 9:10], scale=stat[:, 7:8])
            ac_ln[t] = ac.n

            dma(P, "sync", ln_ag_in[t * 128:(t + 1) * 128, :], lnt[:, sl], st,
                waits=[ac.now])
            st_ln[t] = st.n

            for c in range(8):
                @op(P, "tensor", waits=[ac.now, dv.now], inc=mm)
                def _(te, s0=s0, c=c):
                    return te.transpose(
                        tps[:, :], lnt[:, s0 + c * 128: s0 + (c + 1) * 128],
                        identb[:, :])

                @op(P, "vector", waits=[mm.now], inc=dv)
                def _(v, t=t, c=c):
                    return v.tensor_copy(
                        arena[:, O_LNFM + c * T_OWN + t * 128:
                              O_LNFM + c * T_OWN + (t + 1) * 128], tps[:, :])
            mm_tp[t] = mm.n

            @op(P, "tensor", waits=[dv.now], inc=mm)
            def _(te, t=t):
                r = None
                for c in range(8):
                    r = te.matmul(
                        psA[:, 0:E],
                        arena[:, O_LNFM + c * T_OWN + t * 128:
                              O_LNFM + c * T_OWN + (t + 1) * 128],
                        wrb[:, c * E:(c + 1) * E], start=(c == 0), stop=(c == 7))
                return r

            @op(P, "vector", waits=[mm.now], inc=dv)
            def _(v, t=t):
                return v.tensor_add(logits[:, t * E:(t + 1) * E], psA[:, 0:E],
                                    brb[:, :])
        dv_logits_all = dv.n
        st_ln_all = st.n

        mm_ohs = [None] * NT
        for t in range(NT):
            tE = slice(t * E, (t + 1) * E)
            w = [(mm, mm_ohs[t - 1])] if t >= 1 else []

            @op(P, "vector", waits=w, inc=dv)
            def _(v, tE=tE):
                v.max(out=small[:, 0:8], in_=logits[:, tE])
                v.tensor_scalar_mul(small[:, 8:9], small[:, 0:1], -1.0)
                return v.tensor_sub(small[:, 9:10], small[:, 1:2], small[:, 0:1])

            @op(P, "scalar", waits=[dv.now], inc=ac)
            def _(s, tE=tE):
                s.activation(sqt[:, 0:E], logits[:, tE], AF.Exp, bias=small[:, 8:9])
                return s.activation(sqt[:, E:E + 1], small[:, 9:10], AF.Exp)

            @op(P, "vector", waits=[ac.now], inc=dv)
            def _(v, tE=tE, t=t):
                v.reduce_sum(small[:, 10:11], sqt[:, 0:E], axis=AX.X)
                v.reciprocal(parr[:, t:t + 1], small[:, 10:11])
                v.tensor_mul(parr[:, NT + t:NT + t + 1], sqt[:, E:E + 1],
                             parr[:, t:t + 1])
                v.tensor_scalar(small[:, 0:8], logits[:, tE], small[:, 0:1], None,
                                ALU.is_equal)
                v.tensor_scalar(small[:, 0:8], small[:, 0:8], -1.0, 1.0,
                                ALU.mult, ALU.add)
                v.scalar_tensor_tensor(small[:, 0:8], small[:, 0:8], 1e9,
                                       iota8[:, :], ALU.mult, ALU.add)
                v.tensor_reduce(earr[:, t:t + 1], small[:, 0:8], AX.X, ALU.min)
                v.tensor_scalar(ohs1[:, tE], iota8[:, :], earr[:, t:t + 1], None,
                                ALU.is_equal)
                v.max(out=small[:, 0:8], in_=logits[:, tE])
                v.tensor_scalar(small[:, 8:16], logits[:, tE], small[:, 1:2], None,
                                ALU.is_equal)
                v.tensor_scalar(small[:, 8:16], small[:, 8:16], -1.0, 1.0,
                                ALU.mult, ALU.add)
                v.scalar_tensor_tensor(small[:, 8:16], small[:, 8:16], 1e9,
                                       iota8[:, :], ALU.mult, ALU.add)
                v.scalar_tensor_tensor(small[:, 8:16], ohs1[:, tE], 1e9,
                                       small[:, 8:16], ALU.mult, ALU.add)
                v.tensor_reduce(earr[:, NT + t:NT + t + 1], small[:, 8:16],
                                AX.X, ALU.min)
                v.tensor_scalar(small[:, 0:8], iota8[:, :],
                                earr[:, NT + t:NT + t + 1], None, ALU.is_equal)
                v.tensor_add(ohs1[:, tE], ohs1[:, tE], small[:, 0:8])
                return v.tensor_copy(ohsb[:, :], ohs1[:, tE])

            @op(P, "tensor", waits=[dv.now], inc=mm)
            def _(te):
                return te.transpose(tps[0:E, 0:128], ohsb[:, :], identb[:, :])
            mm_ohs[t] = mm.n

            @op(P, "vector", waits=[mm.now], inc=dv)
            def _(v, t=t):
                return v.tensor_copy(ohs_fm[:, t * 128:(t + 1) * 128],
                                     tps[0:E, 0:128])

        @op(P, "vector", inc=dv)
        def _(v):
            v.tensor_tensor_scan(cinc[:, :], ohs_fm[:, :], ohs_fm[:, :], 0.0,
                                 ALU.add, ALU.bypass)
            v.tensor_copy(offs[:, 2:3], cinc[:, 1023:1024])
            return v.tensor_sub(offs[:, 3:4], cinc[:, 2047:2048],
                                cinc[:, 1023:1024])

        @op(P, "tensor", waits=[dv.now], inc=mm)
        def _(te):
            return te.transpose(psB[0:2, 0:E], offs[:, 2:4], ident[0:E, 0:E])

        @op(P, "vector", waits=[mm.now], inc=dv)
        def _(v):
            return v.tensor_copy(cnt_sb[:, :], psB[0:2, 0:E])

        dma(P, "sync", cnt_ag_in.ap(), cnt_sb[:, :], st, waits=[dv.now])
        st_cnt = st.n

        dma(P, "gpsimd", rs_in.ap().rearrange("(a b) o -> a (b o)", b=160),
            zed[:, :], gp)
        gp_zero = gp.n

        @op(P, "gpsimd", waits=[(st, st_cnt)], inc=cc)
        def _(g):
            return g.collective_compute(
                "AllGather", ALU.bypass, replica_groups=[list(range(NCORE))],
                ins=[cnt_ag_in.ap().opt()], outs=[cnt_ag_out.ap().opt()])
        cc_cnt = cc.n

        @op(P, "gpsimd", waits=[(st, st_ln_all)], inc=cc)
        def _(g):
            return g.collective_compute(
                "AllGather", ALU.bypass, replica_groups=[list(range(NCORE))],
                ins=[ln_ag_in.ap().opt()], outs=[ln_ag_out.ap().opt()])
        cc_lnag = cc.n

        dma(P, "sync", cagb[:, :], cnt_ag_out.ap(), ld, waits=[(cc, cc_cnt)])

        @op(P, "tensor", waits=[ld.now], inc=mm)
        def _(te):
            return te.matmul(psA[0:E, 0:2], cagb[:, :], selpref[:, :],
                             start=True, stop=True)

        @op(P, "vector", waits=[mm.now], inc=dv)
        def _(v):
            v.tensor_copy(offs[:, 0:2], psA[0:E, 0:2])
            v.tensor_add(offs[:, 1:2], offs[:, 0:1], offs[:, 1:2])
            v.tensor_sub(cinc[:, :], cinc[:, :], ohs_fm[:, :])
            v.tensor_scalar_add(posg[:, 0:1024], cinc[:, 0:1024], offs[:, 0:1])
            return v.tensor_scalar_add(posg[:, 1024:2048], cinc[:, 1024:2048],
                                       offs[:, 1:2])

        for t in range(NT):
            @op(P, "tensor", waits=[dv.now], inc=mm)
            def _(te, t=t):
                return te.transpose(psB[:, 0:E], posg[:, t * 128:(t + 1) * 128],
                                    ident[0:E, 0:E])

            @op(P, "vector", waits=[mm.now], inc=dv)
            def _(v, t=t):
                return v.tensor_copy(postm[:, t * E:(t + 1) * E], psB[:, 0:E])

        for t in range(NT):
            tE = slice(t * E, (t + 1) * E)

            @op(P, "vector", inc=dv)
            def _(v, tE=tE, t=t):
                r = None
                for k in range(2):
                    col = k * NT + t
                    ek = earr[:, col:col + 1]
                    v.tensor_scalar(small[:, 0:8], iota8[:, :], ek, None,
                                    ALU.is_equal)
                    v.tensor_mul(small[:, 0:8], small[:, 0:8], postm[:, tE])
                    v.reduce_sum(small[:, 8:9], small[:, 0:8], axis=AX.X)
                    v.tensor_scalar(small[:, 9:10], small[:, 8:9], 1280.0, None,
                                    ALU.is_ge)
                    v.tensor_scalar_min(small[:, 10:11], small[:, 8:9], 1279.0)
                    v.tensor_scalar(small[:, 11:12], ek, 2560.0, None, ALU.mult)
                    v.tensor_scalar_add(small[:, 11:12], small[:, 11:12],
                                        cconst[:, 0:1])
                    v.tensor_add(small[:, 11:12], small[:, 11:12], small[:, 10:11])
                    v.tensor_copy(ggix[:, col:col + 1], small[:, 11:12])
                    v.scalar_tensor_tensor(small[:, 12:13], small[:, 9:10], 1e9,
                                           small[:, 11:12], ALU.mult, ALU.add)
                    v.tensor_copy(gidx[:, col:col + 1], small[:, 12:13])
                    v.tensor_scalar(small[:, 13:14], small[:, 9:10], -0.5, 0.5,
                                    ALU.mult, ALU.add)
                    r = v.tensor_mul(warr[:, col:col + 1], parr[:, col:col + 1],
                                     small[:, 13:14])
                return r

            gp.inc(32)
            dv_t = dv.n

            def scat(g, t=t, dv_t=dv_t, gpz=gp_zero):
                _w(g, [(dv, dv_t), (gp, gpz)])
                for k in range(2):
                    col = k * NT + t
                    g.indirect_dma_start(
                        rs_in[:, :], IOA(gidx[:, col:col + 1], 0),
                        rowval[:, t:t + 1], None,
                        bounds_check=regs["bc_slot"], oob_is_err=False,
                    ).then_inc(gp.h, 16)
            P.add("gpsimd", scat)

        @op(P, "gpsimd", waits=[gp.now], inc=cc)
        def _(g):
            return g.collective_compute(
                "ReduceScatter", ALU.add, replica_groups=[list(range(NCORE))],
                ins=[rs_in.ap().opt()], outs=[rs_out.ap().opt()])
        cc_rs = cc.n

        for k in range(20):
            dma(P, "sync", mapcol[:, k:k + 1], rs_out[k * 128:(k + 1) * 128, :],
                ld, waits=[(cc, cc_rs)] if k == 0 else ())

        @op(P, "vector", waits=[ld.now], inc=dv)
        def _(v):
            v.tensor_scalar_add(mapcol[:, :], mapcol[:, :], -1.0)
            return v.tensor_copy(mapi[:, :], mapcol[:, :])
        dv_map = dv.n

        for c in range(8):
            dma(P, "gpsimd", arena[:, O_W1 + c * HID: O_W1 + (c + 1) * HID],
                w1_in[c * 128:(c + 1) * 128, :], wst,
                waits=list(entry) if c == 0 else ())
        wst_w1 = wst.n
        if first_call:
            dma(P, "gpsimd", w2bf.ap(), w2_in.ap(), wst)
        wst_w2 = wst.n

        mm_prev_tp = 0
        st_prev_y = st.n
        for s in range(5):
            @op(P, "gpsimd",
                waits=[(dv, dv_map)] + ([(mm, mm_prev_tp)] if s else []), inc=gp)
            def _(g):
                return g.memset(arena[:, O_BUFTM:O_BUFTM + 4 * D], 0.0)

            for j in range(4):
                gp.inc(16)

                def gath(g, s=s, j=j):
                    g.indirect_dma_start(
                        arena[:, O_BUFTM + j * D: O_BUFTM + (j + 1) * D], None,
                        ln_ag_out[:, :], IOA(mapi[:, 4 * s + j:4 * s + j + 1], 0),
                        bounds_check=regs["bc_row"], oob_is_err=False,
                    ).then_inc(gp.h, 16)
                P.add("gpsimd", gath)
            gp_gath = gp.n

            for j in range(4):
                for c in range(8):
                    @op(P, "tensor",
                        waits=[(gp, gp_gath), (dv, dv_logits_all), (dv, dv.n)],
                        inc=mm)
                    def _(te, j=j, c=c):
                        return te.transpose(
                            tps[:, :],
                            arena[:, O_BUFTM + j * D + c * 128:
                                  O_BUFTM + j * D + (c + 1) * 128],
                            identb[:, :])

                    @op(P, "vector", waits=[mm.now], inc=dv)
                    def _(v, s=s, j=j, c=c):
                        dst = arena[:, O_BUFFM + c * SLOT_OWN + s * 512 + j * 128:
                                    O_BUFFM + c * SLOT_OWN + s * 512 + (j + 1) * 128]
                        return v.tensor_copy(dst, tps[:, :])
            mm_prev_tp = mm.n
            dv_tp_end = dv.n

            ac_h0 = ac.n
            for hc in range(32):
                hp = psA if hc % 2 == 0 else psB
                w = [(dv, dv_tp_end), (wst, wst_w1)]
                if hc >= 2:
                    w.append((ac, ac_h0 + hc - 1))

                @op(P, "tensor", waits=w, inc=mm)
                def _(te, s=s, hc=hc, hp=hp):
                    r = None
                    for c in range(8):
                        r = te.matmul(
                            hp[:, :],
                            arena[:, O_W1 + c * HID + hc * 128:
                                  O_W1 + c * HID + (hc + 1) * 128],
                            arena[:, O_BUFFM + c * SLOT_OWN + s * 512:
                                  O_BUFFM + c * SLOT_OWN + (s + 1) * 512],
                            start=(c == 0), stop=(c == 7))
                    return r

                @op(P, "scalar", waits=[mm.now], inc=ac)
                def _(sc, hc=hc, hp=hp):
                    return sc.activation(
                        arena[:, O_HFM + hc * 512: O_HFM + (hc + 1) * 512],
                        hp[:, :], AF.Relu, bias=b1c[:, hc:hc + 1])
            ac_h_end = ac.n

            for half in range(2):
                wst_h0 = wst.n
                mm_y0 = mm.n
                for hc in range(32):
                    buf = (hc % 2) * 512
                    w = [(mm, mm_y0 + hc)] if hc >= 2 else [(wst, wst_w2), (mm, mm_y0)]
                    dma(P, "sync", w2sb[:, buf:buf + 512],
                        w2bf[hc * 128:(hc + 1) * 128, half * 512:(half + 1) * 512],
                        wst, waits=w)

                @op(P, "tensor",
                    waits=[(ac, ac_h_end), (dv, dv.n), (st, st_prev_y)], inc=mm)
                def _(te, half=half):
                    r = None
                    for st4 in range(4):
                        r = te.matmul(psY[st4][:, :], ones_f[:, :],
                                      b2r[:, half * 512:(half + 1) * 512],
                                      start=True, stop=False)
                    return r

                for hc in range(32):
                    buf = (hc % 2) * 512

                    @op(P, "tensor", waits=[(wst, wst_h0 + 16 * (hc + 1))], inc=mm)
                    def _(te, hc=hc, buf=buf):
                        r = None
                        for st4 in range(4):
                            r = te.matmul(
                                psY[st4][:, :],
                                arena[:, O_HFM + hc * 512 + st4 * 128:
                                      O_HFM + hc * 512 + (st4 + 1) * 128],
                                w2sb[:, buf:buf + 512],
                                start=False, stop=(hc == 31))
                        return r

                for st4 in range(4):
                    @op(P, "vector", waits=[mm.now], inc=dv)
                    def _(v, st4=st4, half=half):
                        return v.tensor_copy(
                            ysb[:, st4 * D + half * 512: st4 * D + (half + 1) * 512],
                            psY[st4][:, :])

            for st4 in range(4):
                dma(P, "sync",
                    yb_ag_in[s * 512 + st4 * 128: s * 512 + (st4 + 1) * 128, :],
                    ysb[:, st4 * D:(st4 + 1) * D], st, waits=[dv.now])
            st_prev_y = st.n
        st_y_end = st.n

        @op(P, "gpsimd", waits=[(st, st_y_end)], inc=cc)
        def _(g):
            return g.collective_compute(
                "AllGather", ALU.bypass, replica_groups=[list(range(NCORE))],
                ins=[yb_ag_in.ap().opt()], outs=[yb_ag_out.ap().opt()])
        cc_yag = cc.n

        lxc = lx.n
        st_comb = [None] * NT
        for t in range(NT):
            wg = [(cc, cc_yag)]
            if t >= 1:
                wg.append((st, st_comb[t - 1]))
            gp.inc(32)

            def gath2(g, t=t, wg=wg, gpn=gp.n):
                _w(g, wg)
                for k in range(2):
                    col = k * NT + t
                    g.indirect_dma_start(
                        ygb[:, k * D:(k + 1) * D], None,
                        yb_ag_out[:, :], IOA(ggix[:, col:col + 1], 0),
                        bounds_check=regs["bc_slot"], oob_is_err=False,
                    ).then_inc(gp.h, 16)
            P.add("gpsimd", gath2)
            gp_g = gp.n

            w = [(st, st_comb[t - 2])] if t >= 2 and st_comb[t - 2] else []
            dma(P, "scalar", xt[:, (t % 2) * D:(t % 2 + 1) * D],
                src_dram[t * 128:(t + 1) * 128, :], lx, waits=w)

            wv = [(gp, gp_g), lx.now]
            if t >= 1:
                wv.append((st, st_comb[t - 1]))

            @op(P, "vector", waits=wv, inc=dv)
            def _(v, t=t):
                a = slice((t % 2) * D, (t % 2 + 1) * D)
                v.scalar_tensor_tensor(accv[:, 0:D], ygb[:, 0:D], warr[:, t:t + 1],
                                       xt[:, a], ALU.mult, ALU.add)
                return v.scalar_tensor_tensor(accv[:, 0:D], ygb[:, D:2 * D],
                                              warr[:, NT + t:NT + t + 1],
                                              accv[:, 0:D], ALU.mult, ALU.add)

            dma(P, "sync", dst_dram[t * 128:(t + 1) * 128, :],
                accv[:, 0:D], st, waits=[dv.now])
            st_comb[t] = st.n
        return st.n


    def emit_attention(src_dram, dst_dram, ffn1_final, upto=99):
        O_XNFM = 0
        O_KH = [0, 8192]
        O_VH = [16384, 24576]
        O_WV = 32768
        O_WOUT = 32768
        O_WBLK = 49152
        O_LNW = 49152
        O_KRAW = 51200
        O_KSM = 53248
        O_VSB = 55296
        O_YSM = 56320
        O_KSTG = 51200
        O_PT = 53248
        O_QB = 55296
        O_SIG = 51200
        O_POSQ = 55296
        O_BINV = 59392
        O_MROW = 56320
        O_MNEG = 60416
        O_MASK = 68608
        O_KLR = 70656
        O_KLC = 72704
        O_KLT = 72704
        O_GFM = 0
        O_OSB = 51200
        O_OLNW = 53248

        def A(a, b):
            return arena[:, a:b]

        def Af(a, n):
            return arena[:, a:a + 2 * n].bitcast(F32)

        dma(P, "sync", binc[:, :], binc_in.ap(), ld)
        dma(P, "sync", hpar[:, :], hpar_in.ap(), ld)
        dma(P, "sync", icon[:, :], icon_in.ap(), ld)
        dma(P, "gpsimd", A(O_MASK, O_MASK + 2048), masks_in.ap(), gp)

        @op(P, "gpsimd", inc=gp)
        def _(g):
            g.memset(onescol[:, :], 1.0)
            g.memset(onesb[:, :], 1.0)
            g.memset(zrowb[:, :], 0.0)
            return g.memset(zrow[:, :], 0.0)

        @op(P, "scalar", waits=[ld.now, gp.now], inc=ac)
        def _(sc):
            sc.activation(qscale[:, :], hpar[:, 16:32], AF.Exp, scale=-2.0)
            return sc.activation(smsig[:, :], hpar[:, 0:16], AF.Sigmoid)

        @op(P, "vector", waits=[ac.now], inc=dv)
        def _(v):
            v.tensor_scalar_mul(qscale[:, :], qscale[:, :], float(1.0 / np.sqrt(128.0)))
            return v.tensor_mul(binqs[:, :], binc[:, 0:16], qscale[:, :])
        dv_aconst = dv.n

        dma(P, "gpsimd", A(O_LNW, O_LNW + 1024), lnw4_in[:, 0:1024], gp)
        dma(P, "gpsimd", A(O_LNW + 1024, O_LNW + 2048), lnw4_in[:, 1024:2048], gp)
        gp_lnw = gp.n

        tmpb = ysb[:, 0:1024]
        ac_lnA = [None] * NT
        mm_tpA = [None] * NT
        lx0a = lx.n

        def ln_stats(srcap):
            @op(P, "vector", inc=dv)
            def _(v):
                v.reduce_sum(stat[:, 0:1], srcap, axis=AX.X)
                return v.tensor_scalar_mul(stat[:, 1:2], stat[:, 0:1], 1.0 / D)

            @op(P, "scalar", waits=[dv.now], inc=ac)
            def _(s_):
                return s_.activation(sqt[:, :], srcap, AF.Square)

            @op(P, "vector", waits=[ac.now], inc=dv)
            def _(v):
                v.reduce_sum(stat[:, 2:3], sqt[:, :], axis=AX.X)
                v.tensor_scalar_mul(stat[:, 3:4], stat[:, 2:3], 1.0 / D)
                v.tensor_mul(stat[:, 4:5], stat[:, 1:2], stat[:, 1:2])
                return v.tensor_sub(stat[:, 5:6], stat[:, 3:4], stat[:, 4:5])

            @op(P, "scalar", waits=[dv.now], inc=ac)
            def _(s_):
                return s_.activation(stat[:, 6:7], stat[:, 5:6], AF.Sqrt, bias=epsc[:, :])

            @op(P, "vector", waits=[ac.now], inc=dv)
            def _(v):
                v.reciprocal(stat[:, 7:8], stat[:, 6:7])
                v.tensor_mul(stat[:, 8:9], stat[:, 1:2], stat[:, 7:8])
                return v.tensor_scalar_mul(stat[:, 9:10], stat[:, 8:9], -1.0)

        for t in range(NT):
            s0 = (t % 2) * D
            sl = slice(s0, s0 + D)
            wx = [(ac, ac_lnA[t - 2])] if t >= 2 else [(st, ffn1_final)]
            dma(P, "scalar", xt[:, sl], src_dram[t * 128:(t + 1) * 128, :],
                lx, waits=wx)

            @op(P, "vector", waits=[(lx, lx0a + 16 * (t + 1)), (dv, dv_aconst)], inc=dv)
            def _(v, sl=sl):
                v.reduce_sum(stat[:, 0:1], xt[:, sl], axis=AX.X)
                return v.tensor_scalar_mul(stat[:, 1:2], stat[:, 0:1], 1.0 / D)

            @op(P, "scalar", waits=[dv.now], inc=ac)
            def _(s_, sl=sl):
                return s_.activation(sqt[:, :], xt[:, sl], AF.Square)

            @op(P, "vector", waits=[ac.now], inc=dv)
            def _(v):
                v.reduce_sum(stat[:, 2:3], sqt[:, :], axis=AX.X)
                v.tensor_scalar_mul(stat[:, 3:4], stat[:, 2:3], 1.0 / D)
                v.tensor_mul(stat[:, 4:5], stat[:, 1:2], stat[:, 1:2])
                return v.tensor_sub(stat[:, 5:6], stat[:, 3:4], stat[:, 4:5])

            @op(P, "scalar", waits=[dv.now], inc=ac)
            def _(s_):
                return s_.activation(stat[:, 6:7], stat[:, 5:6], AF.Sqrt, bias=epsc[:, :])

            @op(P, "vector", waits=[ac.now], inc=dv)
            def _(v):
                v.reciprocal(stat[:, 7:8], stat[:, 6:7])
                v.tensor_mul(stat[:, 8:9], stat[:, 1:2], stat[:, 7:8])
                return v.tensor_scalar_mul(stat[:, 9:10], stat[:, 8:9], -1.0)

            @op(P, "scalar", waits=[dv.now], inc=ac)
            def _(s_, sl=sl):
                return s_.activation(accv[:, :], xt[:, sl], AF.Identity,
                                     bias=stat[:, 9:10], scale=stat[:, 7:8])
            ac_lnA[t] = ac.n

            ln_stats(accv[:, :])

            @op(P, "scalar", waits=[dv.now], inc=ac)
            def _(s_):
                return s_.activation(sqt[:, :], accv[:, :], AF.Identity,
                                     bias=stat[:, 9:10], scale=stat[:, 7:8])

            w = [ac.now, (gp, gp_lnw)]
            if t >= 2:
                w.append((mm, mm_tpA[t - 2]))

            @op(P, "vector", waits=w, inc=dv)
            def _(v, sl=sl):
                v.tensor_mul(tmpb, sqt[:, :], A(O_LNW, O_LNW + 1024))
                return v.tensor_add(lnt[:, sl], tmpb, A(O_LNW + 1024, O_LNW + 2048))

            for c in range(8):
                @op(P, "tensor", waits=[dv.now], inc=mm)
                def _(te, s0=s0, c=c):
                    return te.transpose(
                        tps[:, :], lnt[:, s0 + c * 128: s0 + (c + 1) * 128],
                        identb[:, :])

                @op(P, "vector", waits=[mm.now], inc=dv)
                def _(v, t=t, c=c):
                    return v.tensor_copy(
                        A(O_XNFM + c * T_OWN + t * 128,
                          O_XNFM + c * T_OWN + (t + 1) * 128), tps[:, :])
            mm_tpA[t] = mm.n
        dv_xnfm = dv.n

        if upto < 2:
            return st.n
        for c in range(8):
            dma(P, "gpsimd", A(O_WV + c * 2048, O_WV + (c + 1) * 2048), wv_in[c],
                wvs, waits=[(dv, dv_xnfm)] if c == 0 else ())
        dma(P, "gpsimd", Af(O_BINV, 2048)[0:1, :], binv_in.ap(), wvs)
        wvs_wv = wvs.n
        VBASE = 2048 * 2048
        gp_prev_vscat = 0
        for tt in range(NT):
            dv_ev = []
            for vc in range(4):
                @op(P, "tensor", waits=[(wvs, wvs_wv), (dv, dv.n)], inc=mm)
                def _(te, tt=tt, vc=vc):
                    te.matmul(psY[vc][:, :], ones_f[:, :],
                              Af(O_BINV, 2048)[0:1, vc * 512:(vc + 1) * 512],
                              start=True, stop=False)
                    r = None
                    for c in range(8):
                        r = te.matmul(
                            psY[vc][:, :],
                            A(O_XNFM + c * T_OWN + tt * 128,
                              O_XNFM + c * T_OWN + (tt + 1) * 128),
                            A(O_WV + c * 2048 + vc * 512,
                              O_WV + c * 2048 + (vc + 1) * 512),
                            start=False, stop=(c == 7))
                    return r

                @op(P, "vector", waits=[mm.now] +
                    ([(gp, gp_prev_vscat)] if gp_prev_vscat and vc == 0 else []),
                    inc=dv)
                def _(v, vc=vc):
                    return v.tensor_copy(
                        arena[:, O_VSB + vc * 512: O_VSB + (vc + 1) * 512],
                        psY[vc][:, :])
            gp_prev_vscat = gp.n + 16
            gp.inc(16)

            def vscat(g, tt=tt, dvn=dv.n):
                _w(g, [(dv, dvn)])
                g.dma_start(
                    out=bass.AP(kvy_ag_in, VBASE + tt * 128,
                                [[16 * 128, 128], [128 * 16 * 128, 16], [1, 128]]),
                    in_=arena[:, O_VSB:O_VSB + 2048].rearrange("p (h d) -> p h d", d=128),
                    allow_slow_non_contiguous=True,
                ).then_inc(gp.h, 16)
            P.add("gpsimd", vscat)
        gp_vscat = gp.n

        order = list(range(16, 32)) + [64] + list(range(0, 16)) + list(range(48, 64))
        wblk_free = [0, 0]
        evb = ysb[:, 1024:2048]
        st_q = st.n
        wbsem = [S("wbA"), S("wbB")]
        for oi, o in enumerate(order):
            buf = oi % 2
            wload_waits = [(mm, wblk_free[buf])] if wblk_free[buf] else [(dv, dv_xnfm)]
            dma(P, "gpsimd", A(O_WBLK + buf * 1024, O_WBLK + (buf + 1) * 1024),
                wblk_in[o], wbsem[buf], waits=wload_waits)
            wst_o = wbsem[buf].n
            for tb in range(4):
                hp = psA if tb % 2 == 0 else psB

                @op(P, "tensor", waits=[(wbsem[buf], wst_o), (ac, ac.n), (dv, dv.n),
                                        (gp, gp_vscat)], inc=mm)
                def _(te, o=o, tb=tb, buf=buf, hp=hp):
                    r = None
                    for c in range(8):
                        r = te.matmul(
                            hp[:, :],
                            A(O_WBLK + buf * 1024 + c * 128,
                              O_WBLK + buf * 1024 + (c + 1) * 128),
                            A(O_XNFM + c * T_OWN + tb * 512,
                              O_XNFM + c * T_OWN + (tb + 1) * 512),
                            start=(c == 0), stop=(c == 7))
                    return r
                if tb == 3:
                    wblk_free[buf] = mm.n

                if 16 <= o < 32:
                    @op(P, "scalar", waits=[mm.now], inc=ac)
                    def _(s_, o=o, tb=tb, hp=hp):
                        return s_.activation(
                            arena[:, O_KRAW + tb * 512: O_KRAW + (tb + 1) * 512],
                            hp[:, :], AF.Identity, bias=binc[:, o:o + 1])
                elif o == 64:
                    @op(P, "scalar", waits=[mm.now], inc=ac)
                    def _(s_, hp=hp):
                        return s_.activation(arena[0:16, O_YSM:O_YSM + 512],
                                             hp[0:16, :], AF.Identity,
                                             bias=binc[0:16, 64:65])
                    dma(P, "sync", kvy_ag_in[4096:4112, tb * 512:(tb + 1) * 512],
                        arena[0:16, O_YSM:O_YSM + 512], st, waits=[ac.now])
                elif o < 16:
                    eoff = 1024 + (tb % 2) * 512

                    @op(P, "scalar", waits=[mm.now] +
                        ([(st, st.n - 16)] if tb >= 2 else []), inc=ac)
                    def _(s_, o=o, hp=hp, eoff=eoff):
                        return s_.activation(ysb[:, eoff:eoff + 512], hp[:, :],
                                             AF.Identity, bias=binqs[:, o:o + 1],
                                             scale=qscale[:, o:o + 1])
                    dma(P, "sync", q_fm[o * 128:(o + 1) * 128, tb * 512:(tb + 1) * 512],
                        ysb[:, eoff:eoff + 512], st, waits=[ac.now])
                else:
                    eoff = 1024 + (tb % 2) * 512

                    @op(P, "scalar", waits=[mm.now] +
                        ([(st, st.n - 16)] if tb >= 2 else []), inc=ac)
                    def _(s_, o=o, hp=hp, eoff=eoff):
                        return s_.activation(ysb[:, eoff:eoff + 512], hp[:, :],
                                             AF.Identity, bias=binc[:, o:o + 1])
                    dma(P, "sync",
                        p_fm[(o - 48) * 128:(o - 47) * 128, tb * 512:(tb + 1) * 512],
                        ysb[:, eoff:eoff + 512], st, waits=[ac.now])

            if 16 <= o < 32:
                h = o - 16
                @op(P, "vector", waits=[ac.now, (st, st.n)], inc=dv)
                def _(v, h=h):
                    v.tensor_copy(arena[:, O_KLC + h: O_KLC + h + 1],
                                  arena[:, O_KRAW + 1023: O_KRAW + 1024])
                    v.tensor_copy(arena[:, O_KLC + 16 + h: O_KLC + 17 + h],
                                  arena[:, O_KRAW + 2047: O_KRAW + 2048])
                    for hb in range(2):
                        b0 = O_KRAW + hb * 1024
                        s0 = O_KSM + hb * 1024
                        v.tensor_sub(tmpb[:, 0:1023],
                                     arena[:, b0:b0 + 1023],
                                     arena[:, b0 + 1:b0 + 1024])
                        v.scalar_tensor_tensor(
                            arena[:, s0 + 1:s0 + 1024], tmpb[:, 0:1023],
                            smsig[:, h:h + 1], arena[:, b0 + 1:b0 + 1024],
                            ALU.mult, ALU.add)
                        v.tensor_scalar(stat[:, 10:11], smsig[:, h:h + 1], -1.0, 1.0,
                                        ALU.mult, ALU.add)
                        v.tensor_mul(arena[:, s0:s0 + 1],
                                     arena[:, b0:b0 + 1], stat[:, 10:11])
                    return v.tensor_copy(stat[:, 11:12], stat[:, 10:11])
                dma(P, "sync", kvy_ag_in[h * 128:(h + 1) * 128, :],
                    arena[:, O_KSM:O_KSM + 2048], st, waits=[dv.now])
        for sdx in range(2):
            @op(P, "tensor", waits=[(dv, dv.n)], inc=mm)
            def _(te, sdx=sdx):
                return te.transpose(tps[0:16, 0:128],
                                    arena[:, O_KLC + sdx * 16: O_KLC + (sdx + 1) * 16],
                                    identb[:, :])

            @op(P, "vector", waits=[mm.now], inc=dv)
            def _(v, sdx=sdx):
                return v.tensor_copy(ysb[0:16, sdx * 128:(sdx + 1) * 128],
                                     tps[0:16, 0:128])

            def _klst(e, sdx=sdx, w=dv.now):
                _w(e, [w])
                e.dma_start(out=bass.AP(kvy_ag_in, (4112 + sdx) * 2048,
                                        [[128, 16], [1, 128]]),
                            in_=ysb[0:16, sdx * 128:(sdx + 1) * 128]).then_inc(st.h, 16)
            st.inc(16)
            P.add("sync", _klst)
        st_fm_all = st.n

        if upto < 3:
            return st.n
        @op(P, "gpsimd", waits=[(st, st_fm_all), (gp, gp_vscat)], inc=cc)
        def _(g):
            return g.collective_compute(
                "AllGather", ALU.bypass,
                replica_groups=[[0, 1, 2, 3], [4, 5, 6, 7]],
                ins=[kvy_ag_in.ap().opt()], outs=[kvy_ag_out.ap().opt()])
        cc_kvy = cc.n

        if upto < 4:
            return st.n
        @op(P, "vector", inc=dv)
        def _(v):
            return v.memset(carry[:, :], 0.0)
        for qt in range(4):
            for ci in range(2):
                c = 2 * qt + ci
                rl, sl_ = (min(c, 7 - c)), (0 if c < 4 else 1)
                dma(P, "gpsimd",
                    Af(O_SIG, 2048)[0:16, ci * 1024:(ci + 1) * 1024],
                    kvy_ag_out[rl * 4224 + 4096: rl * 4224 + 4112,
                               sl_ * 1024:(sl_ + 1) * 1024],
                    gp, waits=[(cc, cc_kvy)] + ([(st, st.n)] if qt else []))

            @op(P, "scalar", waits=[gp.now], inc=ac)
            def _(s_):
                return s_.activation(Af(O_POSQ, 2048)[0:16, :],
                                     Af(O_SIG, 2048)[0:16, :], AF.Sigmoid)

            @op(P, "vector", waits=[ac.now], inc=dv)
            def _(v, qt=qt):
                v.tensor_tensor_scan(Af(O_SIG, 2048)[0:16, :],
                                     Af(O_POSQ, 2048)[0:16, :],
                                     Af(O_POSQ, 2048)[0:16, :],
                                     0.0 if qt == 0 else carry[:, 0:1],
                                     ALU.add, ALU.bypass)
                v.drain()
                return v.tensor_copy(carry[:, 0:1], Af(O_SIG, 2048)[0:16, 2047:2048])

            dma(P, "sync", pos_dram[:, qt * 2048:(qt + 1) * 2048],
                Af(O_SIG, 2048)[0:16, :], st, waits=[dv.now])
        st_pos = st.n
        st_q = st_pos

        def a45_init_sy(sy):
            R["sy_cA"] = sy.to_reg(sy.value_load(icon[0:1, 4:5]))
            R["sy_cB"] = sy.to_reg(sy.value_load(icon[0:1, 5:6]))
            _w(sy, [(st, st_pos)])
            sy.dma_start(out=Af(O_MROW, 2048)[0:16, 0:1024],
                         in_=bass.AP(pos_dram, R["sy_cA"], [[8192, 16], [1, 1024]])
                         ).then_inc(ldm.h, 16)
            sy.dma_start(out=Af(O_MROW, 2048)[0:16, 1024:2048],
                         in_=bass.AP(pos_dram, R["sy_cB"], [[8192, 16], [1, 1024]])
                         ).then_inc(ldm.h, 16)
        R = {}
        ldm.inc(32)
        P.add("sync", a45_init_sy)

        @op(P, "vector", waits=[ldm.now], inc=dv)
        def _(v):
            return v.tensor_scalar(ysb[0:16, 0:2048],
                                   Af(O_MROW, 2048)[0:16, :], -1.0, -24.0,
                                   ALU.mult, ALU.add)

        dma(P, "sync", mneg_dram.ap(), ysb[0:16, 0:2048], st,
            waits=[dv.now])
        st_mneg = st.n

        if upto < 5:
            return st.n
        KSTG = [arena[:, O_KSTG + i * 256: O_KSTG + i * 256 + 128] for i in range(8)]
        VSTG = [arena[:, O_KSTG + i * 256 + 128: O_KSTG + (i + 1) * 256]
                for i in range(8)]
        PT = [arena[:, O_PT + i * 512: O_PT + (i + 1) * 512] for i in range(4)]
        SB4 = [psA, psB, psY3, psY2]
        O_MNEGB = 60416

        def a5_init_pe(te):
            R["pe_rb"] = te.alloc_register(name="pe_rb")
            R["pe_t1"] = te.alloc_register(name="pe_t1")
            R["pe_c"] = te.alloc_register(name="pe_c")
            R["pe_psp"] = te.alloc_register(name="pe_psp")
            te.reg_mov(R["pe_rb"], 0)
            R["pe_tr"] = [te.value_load(icon[0:1, i:i + 1]) for i in range(4)]
        P.add("tensor", a5_init_pe)

        def a5_init_ac(sc):
            R["ac_ra"] = sc.alloc_register(name="ac_ra")
            R["ac_t1"] = sc.alloc_register(name="ac_t1")
            R["ac_c"] = sc.alloc_register(name="ac_c")
            R["ac_psp"] = sc.alloc_register(name="ac_psp")
            sc.reg_mov(R["ac_ra"], 0)
            R["ac_tr"] = [sc.value_load(icon[0:1, i:i + 1]) for i in range(4)]
        P.add("scalar", a5_init_ac)

        def dv_mega(v):
            rj = v.alloc_register(name="m_rj")
            rp = v.alloc_register(name="m_rp")
            rs = v.alloc_register(name="m_rs")
            bl = v.alloc_register(name="m_bl")
            bk1 = v.alloc_register(name="m_bk1")
            ibc = v.alloc_register(name="m_ibc")
            hpar = v.alloc_register(name="m_hp")
            t1 = v.alloc_register(name="m_t1")
            cr = v.alloc_register(name="m_cr")
            sp = v.alloc_register(name="m_sp")
            trips = v.value_load(icon[0:1, 6:7])
            v.reg_mov(rs, 4)
            v.reg_mov(bl, 4)
            v.reg_mov(bk1, 0)
            v.reg_mov(ibc, 3)
            v.reg_mov(hpar, 1)
            v.reg_mov(sp, 0)
            with dynloop(nc, v, cr, trips):
                with v.If_eq(bl, 4):
                    with v.If_ne(bk1, 0):
                        v.reg_add(t1, rs, -4)
                        v.wait_ge(amm.h, t1)
                        for dt in range(3):
                            v.tensor_tensor(
                                SB4[dt][:, :], SB4[dt][:, :],
                                A(O_MASK + dt * 512, O_MASK + (dt + 1) * 512),
                                ALU.add)
                        v.drain()
                        v.tensor_tensor(
                            SB4[3][:, :], SB4[3][:, :],
                            A(O_MASK + 3 * 512, O_MASK + 4 * 512),
                            ALU.add).then_inc(adm.h, 4)
                        v.reg_add(t1, rs, -4)
                        v.wait_ge(aen.h, t1)
                        v.tensor_copy(densb[:, :], psY1[0:1, :])
                        v.drain()
                        v.reciprocal(recsb[:, :], densb[:, :]).then_inc(aep.h, 1)
                        v.wait_ge(abc.h, bk1)
                        v.tensor_copy(recb[:, :], psY2[:, :])
                        v.drain()
                        v.reg_mul(t1, bk1, 16)
                        v.reg_add(t1, t1, -16)
                        v.wait_ge(aos.h, t1)
                        v.tensor_tensor(ysb[:, 0:512], psY0[:, :], recb[:, :],
                                        ALU.mult).then_inc(aev.h, 1)
                    v.reg_add(ibc, ibc, 1)
                    with v.If_eq(ibc, 4):
                        v.reg_mov(ibc, 0)
                        v.reg_mov(t1, 1)
                        v.reg_sub(hpar, t1, hpar)
                    v.reg_add(t1, ibc, 8)
                    v.tensor_copy(mapi[0:1, 19:20],
                                  bass.AP(icon, t1, [[16, 1], [1, 1]]))
                    v.drain()
                    nj_sv = v.value_load(mapi[0:1, 19:20])
                    v.reg_add(bl, nj_sv, 4)
                    v.reg_mul(t1, hpar, 8192)
                    v.reg_add(rj, t1, 0)
                    v.reg_mul(rp, hpar, 64)
                    v.reg_mov(sp, 0)
                    v.reg_add(bk1, bk1, 1)
                    v.reg_add(t1, bk1, 3)
                    v.reg_alu(t1, t1, 2, op=ALU.logical_shift_right)
                    v.wait_ge(akv.h, t1)
                with v.Else():
                    v.reg_add(t1, rs, -8)
                    v.reg_alu(t1, t1, 0, op=ALU.max)
                    v.wait_ge(aen.h, t1)
                    v.reg_add(t1, rs, -4)
                    v.reg_alu(t1, t1, 0, op=ALU.max)
                    v.wait_ge(aac.h, t1)
                    v.tensor_copy(pcstg[:, 0:4],
                                  bass.AP(poscol, rp, [[128, 128], [1, 4]]))
                    v.drain()
                    with v.If_eq(sp, 0):
                        v.tensor_copy(
                            arena[:, O_KSTG: O_KSTG + 1024]
                            .rearrange("p (s kv d) -> p s kv d", s=4, kv=2, d=128),
                            bass.AP(arena, rj,
                                    [[73728, 128], [128, 4], [16384, 2], [1, 128]])
                        ).then_inc(asg.h, 4)
                        v.reg_mov(sp, 1)
                    with v.Else():
                        v.tensor_copy(
                            arena[:, O_KSTG + 1024: O_KSTG + 2048]
                            .rearrange("p (s kv d) -> p s kv d", s=4, kv=2, d=128),
                            bass.AP(arena, rj,
                                    [[73728, 128], [128, 4], [16384, 2], [1, 128]])
                        ).then_inc(asg.h, 4)
                        v.reg_mov(sp, 0)
                    v.reg_add(rj, rj, 512)
                    v.reg_add(rp, rp, 4)
                    v.reg_add(rs, rs, 4)
                    v.reg_add(bl, bl, -4)
        P.add("vector", dv_mega) if A5PARTS[0] else None
        aep.inc(64)
        aev.inc(64)

        mm_tp_prev = 0
        for h in range(16):
            hb = h % 2
            wh = [(ahd, h - 1), (cc, cc_kvy), (st, st_pos)]
            for c in range(8):
                rl, slt = min(c, 7 - c), (0 if c < 4 else 1)
                dma(P, "sync", A(O_KH[hb] + c * 1024, O_KH[hb] + (c + 1) * 1024),
                    kvy_ag_out[rl * 4224 + h * 128: rl * 4224 + (h + 1) * 128,
                               slt * 1024:(slt + 1) * 1024], ld,
                    waits=wh if c == 0 else ())
                voff = (rl * 4224 + 2048) * 2048 + h * 262144 + slt * 8 * 128
                dma(P, "sync", A(O_VH[hb] + c * 1024, O_VH[hb] + (c + 1) * 1024),
                    bass.AP(kvy_ag_out, voff, [[2048, 128], [128, 8], [1, 128]]), ld)
            dma(P, "sync", pc64[:, :],
                bass.AP(pos_dram, h * 8192, [[128, 64], [1, 128]]), ldp,
                waits=[(st, st_pos)] + ([(mm, mm_tp_prev)] if h else []))
            if h == 0:
                for rl in range(4):
                    dma(P, "sync", arena[rl * 2:(rl + 1) * 2, O_KLR:O_KLR + 2048],
                        kvy_ag_out[rl * 4224 + 4112: rl * 4224 + 4114, :], ld)
            ld_h = ld.n

            dma(P, "sync", arena[0:1, O_MNEGB + hb * 2048: O_MNEGB + (hb + 1) * 2048],
                mneg_dram[h:h + 1, :], ldm,
                waits=[(st, st_mneg), (ahd, h - 1)])

            mm.inc(2)

            def pe_tp(te, h=h, hb=hb, tgt=mm.n, ldh=ld_h, ldp_n=ldp.n):
                _w(te, [(ld, ldh), (ldp, ldp_n), (ahd, h - 1), (aac, 136 * h)])
                te.transpose(psA[:, 0:64], pc64[:, :], ident[0:64, 0:64]
                             ).then_inc(mm.h, 1)
                te.transpose(tps[:, 0:8],
                             arena[0:8, O_KLR + h * 128: O_KLR + (h + 1) * 128],
                             identb[0:8, 0:8]).then_inc(mm.h, 1)
            P.add("tensor", pe_tp) if A5PARTS[4] else None
            mm_tp_h = mm.n
            mm_tp_prev = mm_tp_h

            akv.inc(1)

            def ac_prep(sc, h=h, hb=hb, mmh=mm_tp_h, ldh=ld_h):
                _w(sc, [(mm, mmh), (ld, ldh)])
                sc.activation(poscol[:, hb * 64:(hb + 1) * 64], psA[:, 0:64],
                              AF.Identity)
                sc.activation(arena[:, O_KLT + hb * 8: O_KLT + (hb + 1) * 8],
                              tps[:, 0:8], AF.Identity)
                r = None
                for c in range(1, 8):
                    pc_ = c - 1
                    idx = min(pc_, 7 - pc_) * 2 + (0 if pc_ < 4 else 1)
                    col = A(O_KH[hb] + c * 1024, O_KH[hb] + c * 1024 + 1)
                    r = sc.activation(
                        col,
                        arena[:, O_KLT + hb * 8 + idx: O_KLT + hb * 8 + idx + 1],
                        AF.Identity, bias=col, scale=smsig[:, h:h + 1])
                r.then_inc(akv.h, 1)
            P.add("scalar", ac_prep) if A5PARTS[5] else None

            for ib in range(4):
                g = h * 4 + ib
                dma(P, "sync",
                    A(O_QB + (g % 2) * 512, O_QB + (g % 2 + 1) * 512),
                    q_fm[h * 128:(h + 1) * 128, ib * 512:(ib + 1) * 512], ldq2,
                    waits=[(aib, g - 1), (st, st_q)])
                ldq_n = ldq2.n
                qslice = A(O_QB + (g % 2) * 512, O_QB + (g % 2 + 1) * 512)
                mrow = arena[0:1, O_MNEGB + hb * 2048 + ib * 512:
                             O_MNEGB + hb * 2048 + (ib + 1) * 512]

                def pe_blk(te, h=h, hb=hb, ib=ib, g=g, ldq=ldq_n, qslice=qslice,
                           mrow=mrow, ldm_n=32 + 16 * (h + 1)):
                    rb, t1, psp = R["pe_rb"], R["pe_t1"], R["pe_psp"]
                    _w(te, [(ldq2, ldq), (ldm, ldm_n), (aev, g), (aep, g)])
                    te.matmul(psY0[:, :], onesb[0:1, :], zrowb[:, :],
                              start=True, stop=False)
                    te.matmul(psY1[0:1, :], onesb[0:1, 0:1], zrowb[:, :],
                              start=True, stop=False)
                    te.reg_add(t1, rb, 4)
                    te.wait_ge(asg.h, t1)
                    for s in range(4):
                        te.matmul(SB4[s][:, :], onesb[0:1, :], mrow,
                                  start=True, stop=False)
                        te.matmul(SB4[s][:, :], KSTG[s], qslice,
                                  start=False, stop=True).then_inc(amm.h, 1)
                    te.reg_mov(psp, 1)
                    with dynloop(nc, te, R["pe_c"], R["pe_tr"][ib]):
                        te.reg_add(t1, rb, 8)
                        te.wait_ge(asg.h, t1)
                        with te.If_eq(psp, 0):
                            for s in range(4):
                                te.reg_add(t1, rb, s + 1)
                                te.wait_ge(aac.h, t1)
                                te.matmul(SB4[s][:, :], onesb[0:1, :], mrow,
                                          start=True, stop=False)
                                te.matmul(SB4[s][:, :], KSTG[s], qslice,
                                          start=False, stop=True).then_inc(amm.h, 1)
                                te.matmul(psY0[:, :], VSTG[4 + s], PT[s],
                                          start=False, stop=False)
                                te.matmul(psY1[0:1, :], onescol[:, :], PT[s],
                                          start=False, stop=False
                                          ).then_inc(aen.h, 1)
                            te.reg_mov(psp, 1)
                        with te.Else():
                            for s in range(4):
                                te.reg_add(t1, rb, s + 1)
                                te.wait_ge(aac.h, t1)
                                te.matmul(SB4[s][:, :], onesb[0:1, :], mrow,
                                          start=True, stop=False)
                                te.matmul(SB4[s][:, :], KSTG[4 + s], qslice,
                                          start=False, stop=True).then_inc(amm.h, 1)
                                te.matmul(psY0[:, :], VSTG[s], PT[s],
                                          start=False, stop=False)
                                te.matmul(psY1[0:1, :], onescol[:, :], PT[s],
                                          start=False, stop=False
                                          ).then_inc(aen.h, 1)
                            te.reg_mov(psp, 0)
                        te.reg_add(rb, rb, 4)
                    with te.If_eq(psp, 0):
                        for s in range(4):
                            te.reg_add(t1, rb, s + 1)
                            te.wait_ge(aac.h, t1)
                            te.matmul(psY0[:, :], VSTG[4 + s], PT[s],
                                      start=False, stop=False)
                            te.matmul(psY1[0:1, :], onescol[:, :], PT[s],
                                      start=False, stop=False).then_inc(aen.h, 1)
                    with te.Else():
                        for s in range(4):
                            te.reg_add(t1, rb, s + 1)
                            te.wait_ge(aac.h, t1)
                            te.matmul(psY0[:, :], VSTG[s], PT[s],
                                      start=False, stop=False)
                            te.matmul(psY1[0:1, :], onescol[:, :], PT[s],
                                      start=False, stop=False).then_inc(aen.h, 1)
                    te.reg_add(rb, rb, 4)
                    te.wait_ge(aep.h, g + 1)
                    te.matmul(psY2[:, :], ones_f[:, :], recsb[:, :],
                              start=True, stop=True).then_inc(abc.h, 1)
                    te.nop().then_inc(aib.h, 1)
                P.add("tensor", pe_blk) if A5PARTS[1] else None
                abc.inc(1)
                aib.inc(1)

                def ac_blk(sc, h=h, ib=ib, g=g):
                    ra, t1 = R["ac_ra"], R["ac_t1"]
                    with dynloop(nc, sc, R["ac_c"], R["ac_tr"][ib]):
                        sc.wait_ge(aen.h, ra)
                        for s in range(4):
                            sc.reg_add(t1, ra, s + 1)
                            sc.wait_ge(amm.h, t1)
                            sc.activation(PT[s], SB4[s][:, :], AF.Exp,
                                          bias=pcstg[:, s:s + 1]
                                          ).then_inc(aac.h, 1)
                        sc.reg_add(ra, ra, 4)
                    sc.wait_ge(aen.h, ra)
                    sc.wait_ge(adm.h, 4 * (g + 1))
                    for s in range(4):
                        sc.activation(PT[s], SB4[s][:, :], AF.Exp,
                                      bias=pcstg[:, s:s + 1]).then_inc(aac.h, 1)
                    sc.reg_add(ra, ra, 4)
                P.add("scalar", ac_blk) if A5PARTS[2] else None

                aos.inc(16)

                def o_store(sy, h=h, ib=ib, g=g):
                    _w(sy, [(aev, g + 1)])
                    sy.dma_start(out=o_fm[h * 128:(h + 1) * 128,
                                          ib * 512:(ib + 1) * 512],
                                 in_=ysb[:, 0:512]).then_inc(aos.h, 16)
                P.add("sync", o_store) if A5PARTS[3] else None

            ahd.inc(1)

            def pe_hdone(te):
                te.nop().then_inc(ahd.h, 1)
            P.add("tensor", pe_hdone)
        ahd_all = ahd.n
        aos_all = aos.n

        akv.inc(1)

        def ac_pad(sc):
            sc.nop().then_inc(akv.h, 1)
        P.add("scalar", ac_pad)


        if upto < 6:
            return aos.n if False else st.n
        for de in range(16):
            dma(P, "gpsimd", A(O_WOUT + de * 1024, O_WOUT + (de + 1) * 1024),
                woutT_in[de * 128:(de + 1) * 128, :], wst,
                waits=[(ahd, ahd_all)] if de == 0 else ())
        dma(P, "sync", Af(O_OLNW, 1024), lnw4_in[:, 2048:3072], ldp,
            waits=[(ahd, ahd_all)])
        dma(P, "sync", Af(55296, 1024), lnw4_in[:, 3072:4096], ldp)
        wst_wo = wst.n
        ld_oln = ldp.n
        lxp = [S("lxA6a"), S("lxA6b")]

        lx_o = lx.n
        for hc in range(16):
            dma(P, "sync", ysb[:, 2048:4096],
                p_fm[hc * 128:(hc + 1) * 128, :], ld,
                waits=[(ahd, ahd_all), (aos, aos_all)] + ([(dv, dv.n)] if hc else []))
            dma(P, "scalar", lnt[:, 0:2048], o_fm[hc * 128:(hc + 1) * 128, :], lx,
                waits=[(dv, dv.n)] if hc else [])

            @op(P, "scalar", waits=[ld.now, lx.now], inc=ac)
            def _(s_):
                return s_.activation(ysb[:, 0:2048], ysb[:, 2048:4096], AF.Silu)

            @op(P, "vector", waits=[ac.now], inc=dv)
            def _(v, hc=hc):
                return v.tensor_mul(A(O_GFM + hc * 2048, O_GFM + (hc + 1) * 2048),
                                    ysb[:, 0:2048], lnt[:, 0:2048])
        dv_gfm = dv.n

        st_a6 = [None] * NT
        for tt in range(NT):
            for half in range(2):
                hp2 = psY2 if half == 0 else psY3

                @op(P, "tensor", waits=[(dv, dv_gfm), (wst, wst_wo)] +
                    ([(dv, dv.n)] if (tt or half) else []), inc=mm)
                def _(te, tt=tt, half=half, hp2=hp2):
                    r = None
                    for de in range(16):
                        r = te.matmul(
                            hp2[:, :],
                            A(O_GFM + de * 2048 + tt * 128,
                              O_GFM + de * 2048 + (tt + 1) * 128),
                            A(O_WOUT + de * 1024 + half * 512,
                              O_WOUT + de * 1024 + (half + 1) * 512),
                            start=(de == 0), stop=(de == 15))
                    return r

            @op(P, "vector", waits=[mm.now], inc=dv)
            def _(v):
                v.tensor_copy(Af(O_OSB, 1024)[:, 0:512], psY2[:, :])
                return v.tensor_copy(Af(O_OSB, 1024)[:, 512:1024], psY3[:, :])

            ln_stats(Af(O_OSB, 1024))

            @op(P, "scalar", waits=[dv.now], inc=ac)
            def _(s_):
                return s_.activation(sqt[:, :], Af(O_OSB, 1024), AF.Identity,
                                     bias=stat[:, 9:10], scale=stat[:, 7:8])

            wxx = [(st, st_a6[tt - 2])] if tt >= 2 and st_a6[tt - 2] else []
            dma(P, "scalar", xt[:, (tt % 2) * D:(tt % 2 + 1) * D],
                x_res[tt * 128:(tt + 1) * 128, :], lxp[tt % 2], waits=wxx)

            @op(P, "vector", waits=[ac.now, lxp[tt % 2].now, (ldp, ld_oln)] +
                ([(st, st_a6[tt - 1])] if tt >= 1 and st_a6[tt - 1] else []), inc=dv)
            def _(v, tt=tt):
                v.tensor_mul(accv[:, :], sqt[:, :], Af(O_OLNW, 1024))
                v.tensor_add(accv[:, :], accv[:, :], Af(55296, 1024))
                return v.tensor_add(accv[:, :], accv[:, :],
                                    xt[:, (tt % 2) * D:(tt % 2 + 1) * D])

            dma(P, "sync", dst_dram[tt * 128:(tt + 1) * 128, :], accv[:, :], st,
                waits=[dv.now])
            st_a6[tt] = st.n
        return st.n
    if phases == ("ffn1",):
        final_st = emit_ffn(x_my.ap(), y_my.ap(), first_call=True)
    elif phases == ("ffn1", "attn"):
        f1 = emit_ffn(x_my.ap(), x_res.ap(), first_call=True)
        final_st = emit_attention(x_res.ap(), y_my.ap(), f1, upto=UPTO[0])
    elif phases == ("ffn1", "attn", "ffn2"):
        f1 = emit_ffn(x_my.ap(), x_res.ap(), first_call=True)
        a6 = emit_attention(x_res.ap(), x_res.ap(), f1)
        final_st = emit_ffn(x_res.ap(), y_my.ap(), first_call=False,
                            entry=[(st, a6)])
    else:
        raise NotImplementedError(phases)

    if adbg:
        da_q = nc.dram_tensor("da_q", [2048, T_OWN], BF16, kind="ExternalOutput")
        da_p = nc.dram_tensor("da_p", [2048, T_OWN], BF16, kind="ExternalOutput")
        da_o = nc.dram_tensor("da_o", [2048, T_OWN], BF16, kind="ExternalOutput")
        da_kvy = nc.dram_tensor("da_kvy", [4 * 4224, 2048], BF16,
                                kind="ExternalOutput")
        da_pos = nc.dram_tensor("da_pos", [16, 8192], F32, kind="ExternalOutput")
        da_mneg = nc.dram_tensor("da_mneg", [16, 2048], BF16,
                                 kind="ExternalOutput")
        da_xr = nc.dram_tensor("da_xr", [T_OWN, D], F32, kind="ExternalOutput")
        dbg = S("dbg")
        dma(P, "scalar", da_q.ap(), q_fm.ap(), dbg, waits=[(st, final_st)])
        dma(P, "scalar", da_p.ap(), p_fm.ap(), dbg)
        dma(P, "scalar", da_o.ap(), o_fm.ap(), dbg)
        dma(P, "scalar", da_kvy.ap(), kvy_ag_out.ap(), dbg)
        dma(P, "scalar", da_pos.ap(), pos_dram.ap(), dbg)
        dma(P, "scalar", da_mneg.ap(), mneg_dram.ap(), dbg)
        dma(P, "scalar", da_xr.ap(), x_res.ap(), dbg)

        @op(P, "sync", waits=[dbg.now, (st, final_st)])
        def _(sy):
            pass
    elif debug:
        dbg = S("dbg")
        dma(P, "scalar", d_ln.ap(), ln_ag_out.ap(), dbg, waits=[(st, final_st)])
        dma(P, "scalar", d_logits.ap(), logits[:, :], dbg)
        dma(P, "scalar", d_posg.ap(), posg[:, :], dbg)
        dma(P, "scalar", d_cnt.ap(), cagb[:, :], dbg)
        dma(P, "scalar", d_offs.ap(), offs[:, :], dbg)
        dma(P, "scalar", d_map.ap(), rs_out.ap(), dbg)
        dma(P, "scalar", d_ybin.ap(), yb_ag_in.ap(), dbg)
        dma(P, "scalar", d_earr.ap(), earr[:, :], dbg)
        dma(P, "scalar", d_parr.ap(), parr[:, :], dbg)
        dma(P, "scalar", d_warr.ap(), warr[:, :], dbg)
        dma(P, "scalar", d_ggix.ap(), ggix[:, :], dbg)
        dma(P, "scalar", d_postm.ap(), postm[:, :], dbg)
        dma(P, "scalar", d_buf.ap(), arena[:, O_BUFFM:O_BUFFM + 8 * SLOT_OWN], dbg)
        dma(P, "scalar", d_hfm.ap(), arena[:, O_HFM:O_HFM + 32 * 512], dbg)
        dma(P, "scalar", d_w1.ap(), arena[:, O_W1:O_W1 + 8 * HID], dbg)
        final_st = None

        @op(P, "sync", waits=[dbg.now])
        def _(sy):
            pass
    else:
        @op(P, "sync", waits=[(st, final_st)])
        def _(sy):
            pass

    bb = nc.Block()
    blk = bb.__enter__()
    ctxs.append(bb)
    for name in ENGINES:
        fns = P.ops[name]
        getattr(blk, name)(lambda e, fns=fns: [f(e) for f in fns])

    for c in reversed(ctxs):
        c.__exit__(None, None, None)
    return nc


def _stage_inputs(inputs):
    x = np.asarray(inputs["L0_attn_x"])
    wr = np.ascontiguousarray(np.asarray(inputs["L0_ffn_Wr"]), dtype=np.float32)
    w_in = np.asarray(inputs["L0_attn_W_in"], dtype=np.float32)
    b_in = np.asarray(inputs["L0_attn_b_in"], dtype=np.float32)
    w_out = np.asarray(inputs["L0_attn_W_out"], dtype=np.float32)
    ilw = np.asarray(inputs["L0_attn_in_ln_w"], dtype=np.float32)
    ilb = np.asarray(inputs["L0_attn_in_ln_b"], dtype=np.float32)
    olw = np.asarray(inputs["L0_attn_out_ln_w"], dtype=np.float32)
    olb = np.asarray(inputs["L0_attn_out_ln_b"], dtype=np.float32)
    smear = np.asarray(inputs["L0_attn_smear_factor"], dtype=np.float32)
    lsc = np.asarray(inputs["L0_attn_log_scale"], dtype=np.float32)

    w_in_pad = np.zeros((8320, 1024), np.float32)
    w_in_pad[:8208] = w_in
    b_in_pad = np.zeros(8320, np.float32)
    b_in_pad[:8208] = b_in
    wblk = np.ascontiguousarray(
        w_in_pad.reshape(65, 128, 8, 128).transpose(0, 3, 2, 1).reshape(65, 128, 1024))
    wv = np.ascontiguousarray(
        w_in_pad[4096:6144].T.reshape(8, 128, 2048))
    binc = np.ascontiguousarray(b_in_pad.reshape(65, 128).T)
    binv = np.ascontiguousarray(b_in_pad[4096:6144].reshape(1, 2048))
    woutT = np.ascontiguousarray(w_out.T)
    lnw4 = np.ascontiguousarray(np.broadcast_to(
        np.concatenate([ilw, ilb, olw, olb]), (128, 4096)))
    hpar = np.ascontiguousarray(np.broadcast_to(
        np.concatenate([smear, lsc]), (128, 32)))
    masks = np.zeros((128, 4, 512), np.float32)
    jj = np.arange(128)[:, None]
    ii = np.arange(512)[None, :]
    for k in range(4):
        masks[:, k, :] = np.where(128 * k + jj > ii, -1e10, 0.0)
    masks = np.ascontiguousarray(masks.reshape(128, 2048))
    w1 = np.asarray(inputs["L0_ffn_W1"])
    w2 = np.asarray(inputs["L0_ffn_W2"])
    b1 = np.asarray(inputs["L0_ffn_b1"])
    b2 = np.asarray(inputs["L0_ffn_b2"])
    br = np.asarray(inputs["L0_ffn_br"])

    ident = np.eye(128, dtype=np.float32)
    iota8 = np.broadcast_to(np.arange(E, dtype=np.float32), (128, E)).copy()
    brb = np.broadcast_to(br.astype(np.float32), (128, E)).copy()

    in_maps = []
    for r in range(NCORE):
        b = r // 4
        cA, cB = r % 4, 7 - (r % 4)
        x_mine = np.ascontiguousarray(
            np.concatenate([x[b, cA * 1024:(cA + 1) * 1024],
                            x[b, cB * 1024:(cB + 1) * 1024]], 0), dtype=np.float32)
        sel = np.zeros((16, 2), np.float32)
        for c in range(8):
            rr, off = (b * 4 + min(c, 7 - c)), (0 if c < 4 else 1)
            row = 2 * rr + off
            if c < cA:
                sel[row, 0] = 1.0
            if cA < c < cB:
                sel[row, 1] = 1.0
        consts = np.zeros((128, 4), np.float32)
        consts[:, 0] = b * CAP
        consts[:, 1] = r * T_OWN + 1
        b1c = np.ascontiguousarray(b1[r].reshape(32, 128).T, dtype=np.float32)
        icon = np.zeros((1, 16), np.int32)
        gpos = [cA * 1024, cA * 1024 + 512, cB * 1024, cB * 1024 + 512]
        for i in range(4):
            icon[0, i] = gpos[i] // 512
            icon[0, 8 + i] = gpos[i] // 128 + 4
        icon[0, 4] = cA * 1024
        icon[0, 5] = cB * 1024
        icon[0, 6] = 16 * (sum(gp_ // 128 + 4 for gp_ in gpos) // 4) + 65
        in_maps.append({
            "x_my": x_mine,
            "wr_in": wr,
            "w1_in": np.ascontiguousarray(w1[r], dtype=np.float32),
            "w2_in": np.ascontiguousarray(w2[r], dtype=np.float32),
            "b1c_in": b1c,
            "b2r_in": np.ascontiguousarray(b2[r].reshape(1, D), dtype=np.float32),
            "brb_in": brb,
            "ident_in": ident,
            "iota8_in": iota8,
            "selpref_in": sel,
            "consts_in": consts,
            "wblk_in": wblk,
            "wv_in": wv,
            "binc_in": binc,
            "binv_in": binv,
            "woutT_in": woutT,
            "lnw4_in": lnw4,
            "hpar_in": hpar,
            "masks_in": masks,
            "icon_in": icon,
        })
    return in_maps


def _gather_output(results):
    out = np.zeros((2, 8192, D), np.float32)
    for r in range(NCORE):
        b = r // 4
        cA, cB = r % 4, 7 - (r % 4)
        y = results[r]["y_my"]
        out[b, cA * 1024:(cA + 1) * 1024] = y[0:1024]
        out[b, cB * 1024:(cB + 1) * 1024] = y[1024:2048]
    return out


def kernel(**inputs):
    nc = build_program(phases=("ffn1", "attn", "ffn2"))
    in_maps = _stage_inputs(inputs)
    res = run_bass_kernel_spmd(nc, in_maps, core_ids=list(range(NCORE)))
    return _gather_output(res.results)


def kernel_debug(**inputs):
    nc = build_program(phases=("ffn1",), debug=True)
    in_maps = _stage_inputs(inputs)
    res = run_bass_kernel_spmd(nc, in_maps, core_ids=list(range(NCORE)))
    return res.results


# version 07
# speedup vs baseline: 1.1322x; 1.1322x over previous
import numpy as np
import ml_dtypes
import concourse.bass as bass
import concourse.mybir as mybir
from concourse.bass_utils import run_bass_kernel_spmd

F32 = mybir.dt.float32
BF16 = mybir.dt.bfloat16
I32 = mybir.dt.int32
AF = mybir.ActivationFunctionType
ALU = mybir.AluOpType
AX = mybir.AxisListType
IOA = bass.IndirectOffsetOnAxis

NCORE = 8
D = 1024
T_OWN = 2048
NT = 16
E = 8
CAP = 1280
NSLOT = E * 2 * CAP
SLOT_OWN = 2 * CAP
HID = 4096
LN_EPS = 1e-5

ENGINES = ("sync", "gpsimd", "scalar", "vector", "tensor")
UPTO = [99]
A5PARTS = [True, True, True, True, True, True, True]

from contextlib import contextmanager
from concourse.ordered_engine_set import OrderedEngineSet

_DYN_ID = [0]


@contextmanager
def dynloop(nc, eng, creg, end_sv):
    _DYN_ID[0] += 1
    lname = f"dyn{_DYN_ID[0]}_loop"
    ename = f"dyn{_DYN_ID[0]}_end"
    engset = OrderedEngineSet([eng.engine])
    eng.reg_mov(creg, 0)
    eng.br_lt(creg, end_sv, on_true=lname, on_false=ename)
    with nc.body(lname, valid_engines=engset):
        yield
        eng.reg_add(creg, creg, 1)
        eng.br_lt(creg, end_sv, on_true=lname, on_false=ename)
    nc.switch_bb(ename)



class Sem:
    def __init__(self, h):
        self.h = h
        self.n = 0

    def inc(self, k=1):
        self.n += k
        return self.n

    @property
    def now(self):
        return (self, self.n)


class Prog:
    def __init__(self):
        self.ops = {e: [] for e in ENGINES}

    def add(self, eng, fn):
        self.ops[eng].append(fn)


def _w(e, waits):
    for s, v in waits:
        if v > 0:
            e.wait_ge(s.h, v)


_DVE_COMPUTE = {
    "tensor_copy", "tensor_tensor", "tensor_add", "tensor_sub", "tensor_mul",
    "tensor_max", "tensor_scalar", "tensor_single_scalar", "tensor_scalar_add",
    "tensor_scalar_sub", "tensor_scalar_mul", "tensor_scalar_max",
    "tensor_scalar_min", "tensor_relu", "tensor_reduce", "reduce_max",
    "reduce_sum", "scalar_tensor_tensor", "tensor_tensor_scan", "memset",
    "memzero", "reciprocal", "max", "max_index", "select", "copy_predicated",
    "bn_stats", "bn_aggr", "transpose", "match_replace",
}


class AutoDrain:

    def __init__(self, v):
        object.__setattr__(self, "_v", v)
        object.__setattr__(self, "_n", 0)

    def __getattr__(self, name):
        v = object.__getattribute__(self, "_v")
        fn = getattr(v, name)
        if name in _DVE_COMPUTE:
            def call(*a, **k):
                n = object.__getattribute__(self, "_n")
                if n > 0:
                    v.drain()
                object.__setattr__(self, "_n", n + 1)
                return fn(*a, **k)
            return call
        return fn


def op(P, eng, waits=(), inc=None, inck=1):
    if inc is not None:
        inc.inc(inck)

    def deco(fn):
        def run(e):
            if eng == "vector":
                e = AutoDrain(e)
            _w(e, waits)
            r = fn(e)
            if inc is not None:
                r.then_inc(inc.h, inck)
        P.add(eng, run)
        return None
    return deco


def dma(P, eng, out, in_, sem, waits=(), **kw):
    sem.inc(16)

    def run(e):
        _w(e, waits)
        e.dma_start(out=out, in_=in_, **kw).then_inc(sem.h, 16)
    P.add(eng, run)


def build_program(phases=("ffn1",), debug=False, adbg=False):
    nc = bass.Bass("TRN2", target_bir_lowering=False)

    x_my = nc.dram_tensor("x_my", [T_OWN, D], F32, kind="ExternalInput")
    wr_in = nc.dram_tensor("wr_in", [D, E], F32, kind="ExternalInput")
    w1_in = nc.dram_tensor("w1_in", [D, HID], F32, kind="ExternalInput")
    w2_in = nc.dram_tensor("w2_in", [HID, D], F32, kind="ExternalInput")
    b1c_in = nc.dram_tensor("b1c_in", [128, 32], F32, kind="ExternalInput")
    b2r_in = nc.dram_tensor("b2r_in", [1, D], F32, kind="ExternalInput")
    brb_in = nc.dram_tensor("brb_in", [128, E], F32, kind="ExternalInput")
    ident_in = nc.dram_tensor("ident_in", [128, 128], F32, kind="ExternalInput")
    iota8_in = nc.dram_tensor("iota8_in", [128, E], F32, kind="ExternalInput")
    selpref_in = nc.dram_tensor("selpref_in", [16, 2], F32, kind="ExternalInput")
    consts_in = nc.dram_tensor("consts_in", [128, 4], F32, kind="ExternalInput")
    wblk_in = nc.dram_tensor("wblk_in", [65, 128, 1024], F32, kind="ExternalInput")
    wv_in = nc.dram_tensor("wv_in", [8, 128, 2048], F32, kind="ExternalInput")
    binc_in = nc.dram_tensor("binc_in", [128, 65], F32, kind="ExternalInput")
    binv_in = nc.dram_tensor("binv_in", [1, 2048], F32, kind="ExternalInput")
    woutT_in = nc.dram_tensor("woutT_in", [2048, 1024], F32, kind="ExternalInput")
    lnw4_in = nc.dram_tensor("lnw4_in", [128, 4 * 1024], F32, kind="ExternalInput")
    hpar_in = nc.dram_tensor("hpar_in", [128, 32], F32, kind="ExternalInput")
    masks_in = nc.dram_tensor("masks_in", [128, 4 * 512], F32, kind="ExternalInput")
    icon_in = nc.dram_tensor("icon_in", [1, 16], I32, kind="ExternalInput")
    y_my = nc.dram_tensor("y_my", [T_OWN, D], F32, kind="ExternalOutput")
    if debug:
        d_ln = nc.dram_tensor("d_ln", [NCORE * T_OWN, D], BF16, kind="ExternalOutput")
        d_logits = nc.dram_tensor("d_logits", [128, NT * E], F32, kind="ExternalOutput")
        d_posg = nc.dram_tensor("d_posg", [E, T_OWN], F32, kind="ExternalOutput")
        d_cnt = nc.dram_tensor("d_cnt", [16, E], F32, kind="ExternalOutput")
        d_offs = nc.dram_tensor("d_offs", [E, 4], F32, kind="ExternalOutput")
        d_map = nc.dram_tensor("d_map", [SLOT_OWN, 1], F32, kind="ExternalOutput")
        d_ybin = nc.dram_tensor("d_ybin", [SLOT_OWN, D], BF16, kind="ExternalOutput")
        d_earr = nc.dram_tensor("d_earr", [128, 2 * NT], F32, kind="ExternalOutput")
        d_parr = nc.dram_tensor("d_parr", [128, 2 * NT], F32, kind="ExternalOutput")
        d_warr = nc.dram_tensor("d_warr", [128, 2 * NT], F32, kind="ExternalOutput")
        d_ggix = nc.dram_tensor("d_ggix", [128, 2 * NT], I32, kind="ExternalOutput")
        d_postm = nc.dram_tensor("d_postm", [128, NT * E], F32, kind="ExternalOutput")
        d_buf = nc.dram_tensor("d_buf", [128, 8 * SLOT_OWN], BF16, kind="ExternalOutput")
        d_hfm = nc.dram_tensor("d_hfm", [128, 32 * 512], BF16, kind="ExternalOutput")
        d_w1 = nc.dram_tensor("d_w1", [128, 8 * HID], BF16, kind="ExternalOutput")

    ln_ag_in = nc.dram_tensor("ln_ag_in", [T_OWN, D], BF16)
    ln_ag_out = nc.dram_tensor("ln_ag_out", [NCORE * T_OWN, D], BF16)
    cnt_ag_in = nc.dram_tensor("cnt_ag_in", [2, E], F32)
    cnt_ag_out = nc.dram_tensor("cnt_ag_out", [16, E], F32)
    rs_in = nc.dram_tensor("rs_in", [NSLOT, 1], F32)
    rs_out = nc.dram_tensor("rs_out", [SLOT_OWN, 1], F32)
    yb_ag_in = nc.dram_tensor("yb_ag_in", [SLOT_OWN, D], BF16)
    yb_ag_out = nc.dram_tensor("yb_ag_out", [NSLOT, D], BF16)
    w2bf = nc.dram_tensor("w2bf", [HID, D], BF16)
    x_res = nc.dram_tensor("x_res", [T_OWN, D], F32)
    kvy_ag_in = nc.dram_tensor("kvy_ag_in", [4224, 2048], BF16)
    kvy_ag_out = nc.dram_tensor("kvy_ag_out", [4 * 4224, 2048], BF16)
    q_fm = nc.dram_tensor("q_fm", [2048, T_OWN], BF16)
    p_fm = nc.dram_tensor("p_fm", [2048, T_OWN], BF16)
    o_fm = nc.dram_tensor("o_fm", [2048, T_OWN], BF16)
    pos_dram = nc.dram_tensor("pos_dram", [16, 8192], F32)
    mneg_dram = nc.dram_tensor("mneg_dram", [16, 2048], BF16)

    ctxs = []

    def S(name):
        h = nc.semaphore(name)
        ctxs.append(h)
        return Sem(h.__enter__())

    def SB(name, shape, dt):
        h = nc.sbuf_tensor(name, shape, dt)
        ctxs.append(h)
        return h.__enter__()

    def PSUM(name, shape, dt):
        h = nc.psum_tensor(name, shape, dt)
        ctxs.append(h)
        return h.__enter__()

    ident = SB("ident", [128, 128], F32)
    identb = SB("identb", [128, 128], BF16)
    iota8 = SB("iota8", [128, E], F32)
    brb = SB("brb", [128, E], F32)
    b1c = SB("b1c", [128, 32], F32)
    b2r = SB("b2r", [1, D], F32)
    selpref = SB("selpref", [16, 2], F32)
    cconst = SB("cconst", [128, 4], F32)
    ones_f = SB("ones_f", [1, 128], F32)
    epsc = SB("epsc", [128, 1], F32)
    wrb = SB("wrb", [128, 8 * E], BF16)
    rowval = SB("rowval", [128, NT], F32)
    zed = SB("zed", [128, 160], F32)
    xt = SB("xt", [128, 2 * D], F32)
    sqt = SB("sqt", [128, D], F32)
    lnt = SB("lnt", [128, 2 * D], BF16)
    stat = SB("stat", [128, 16], F32)
    small = SB("small", [128, 16], F32)
    logits = SB("logits", [128, NT * E], F32)
    parr = SB("parr", [128, 2 * NT], F32)
    warr = SB("warr", [128, 2 * NT], F32)
    earr = SB("earr", [128, 2 * NT], F32)
    ohs1 = SB("ohs1", [128, NT * E], F32)
    ohsb = SB("ohsb", [128, E], BF16)
    cagb = SB("cagb", [16, E], F32)
    offs = SB("offs", [E, 4], F32)
    cnt_sb = SB("cnt_sb", [2, E], F32)
    postm = SB("postm", [128, NT * E], F32)
    gidx = SB("gidx", [128, 2 * NT], I32)
    ggix = SB("ggix", [128, 2 * NT], I32)
    mapcol = SB("mapcol", [128, 20], F32)
    mapi = SB("mapi", [128, 20], I32)
    w2sb = SB("w2sb", [128, 2 * 512], BF16)
    ysb = SB("ysb", [128, 4 * D], BF16)
    ygb = SB("ygb", [128, 2 * D], BF16)
    accv = SB("accv", [128, D], F32)
    binc = SB("binc", [128, 65], F32)
    hpar = SB("hpar", [128, 32], F32)
    qscale = SB("qscale", [128, 16], F32)
    smsig = SB("smsig", [128, 16], F32)
    binqs = SB("binqs", [128, 16], F32)
    icon = SB("icon", [1, 16], I32)
    onescol = SB("onescol", [128, 1], BF16)
    zrow = SB("zrow", [1, 512], F32)
    onesb = SB("onesb", [1, 128], BF16)
    zrowb = SB("zrowb", [1, 512], BF16)
    carry = SB("carry", [16, 2], F32)
    poscol = SB("poscol", [128, 128], F32)
    pc64 = SB("pc64", [64, 128], F32)
    pcstg = SB("pcstg", [128, 8], F32)
    densb = SB("densb", [1, 512], F32)
    recsb = SB("recsb", [1, 512], F32)
    recb = SB("recb", [128, 512], F32)

    arena = SB("arena", [128, 73728], BF16)
    O_LNFM, O_BUFFM, O_W1, O_HFM, O_BUFTM = 0, 0, 20480, 53248, 69632
    ohs_fm = arena[0:E, 53248:55296]
    cinc = arena[0:E, 55296:59392].bitcast(F32)
    posg = arena[0:E, 59392:63488].bitcast(F32)

    tps = PSUM("tps", [128, 128], BF16)
    psA = PSUM("psA", [128, 512], F32)
    psB = PSUM("psB", [128, 512], F32)
    psY = [PSUM(f"psY{i}", [128, 512], F32) for i in range(4)]
    psY0, psY1, psY2, psY3 = psY

    ld = S("ld")
    lx = S("lx")
    gp = S("gp")
    cc = S("cc")
    mm = S("mm")
    dv = S("dv")
    ac = S("ac")
    st = S("st")
    wst = S("wst")
    asg = S("asg")
    amm = S("amm")
    aac = S("aac")
    aen = S("aen")
    adm = S("adm")
    aep = S("aep")
    abc = S("abc")
    aev = S("aev")
    aib = S("aib")
    ahd = S("ahd")
    aos = S("aos")
    akv = S("akv")
    ldp = S("ldp")
    ldm = S("ldm")
    ldq2 = S("ldq2")
    lxx = S("lxx")
    wvs = S("wvs")

    P = Prog()
    regs = {}

    def _initregs(g):
        regs["bc_slot"] = g.to_reg(NSLOT - 1)
        regs["bc_row"] = g.to_reg(NCORE * T_OWN - 1)
    P.add("gpsimd", _initregs)

    dma(P, "sync", ident[:, :], ident_in.ap(), ld)
    dma(P, "sync", iota8[:, :], iota8_in.ap(), ld)
    dma(P, "sync", brb[:, :], brb_in.ap(), ld)
    dma(P, "sync", b1c[:, :], b1c_in.ap(), ld)
    dma(P, "sync", b2r[:, :], b2r_in.ap(), ld)
    dma(P, "sync", selpref[:, :], selpref_in.ap(), ld)
    dma(P, "sync", cconst[:, :], consts_in.ap(), ld)
    dma(P, "gpsimd", wrb[:, :].rearrange("p (c e) -> p c e", e=E),
        wr_in.ap().rearrange("(c p) e -> p c e", p=128), gp)

    @op(P, "gpsimd", inc=gp)
    def _(g):
        g.memset(ones_f[:, :], 1.0)
        g.memset(epsc[:, :], LN_EPS)
        g.memset(zed[:, :], 0.0)
        return g.iota(rowval[:, :], [[128, NT]], channel_multiplier=1,
                      allow_small_or_imprecise_dtypes=True)

    @op(P, "vector", waits=[ld.now, gp.now], inc=dv)
    def _(v):
        v.tensor_copy(identb[:, :], ident[:, :])
        return v.tensor_scalar_add(rowval[:, :], rowval[:, :], cconst[:, 1:2])
    dv_setup = dv.n

    def emit_ffn(src_dram, dst_dram, first_call, entry=(), entry_tiles=None):
        xtv = arena[:, 60416:60416 + 4096].bitcast(F32)
        lx0 = lx.n
        ac_ln = [None] * NT
        st_ln = [None] * NT
        mm_tp = [None] * NT
        for t in range(NT):
            s0 = (t % 2) * D
            sl = slice(s0, s0 + D)
            wx = ([(ac, ac_ln[t - 2])] if t >= 2 else []) + \
                (entry_tiles[t] if entry_tiles else
                 (list(entry) if t < 2 else []))
            dma(P, "scalar", xtv[:, sl], src_dram[t * 128:(t + 1) * 128, :],
                lx, waits=wx)

            @op(P, "vector", waits=[(lx, lx0 + 16 * (t + 1)), (dv, dv_setup)], inc=dv)
            def _(v, sl=sl):
                v.reduce_sum(stat[:, 0:1], xtv[:, sl], axis=AX.X)
                return v.tensor_scalar_mul(stat[:, 1:2], stat[:, 0:1], 1.0 / D)

            @op(P, "scalar", waits=[dv.now], inc=ac)
            def _(s, sl=sl):
                return s.activation(sqt[:, :], xtv[:, sl], AF.Square)

            @op(P, "vector", waits=[ac.now], inc=dv)
            def _(v):
                v.reduce_sum(stat[:, 2:3], sqt[:, :], axis=AX.X)
                v.tensor_scalar_mul(stat[:, 3:4], stat[:, 2:3], 1.0 / D)
                v.tensor_mul(stat[:, 4:5], stat[:, 1:2], stat[:, 1:2])
                return v.tensor_sub(stat[:, 5:6], stat[:, 3:4], stat[:, 4:5])

            @op(P, "scalar", waits=[dv.now], inc=ac)
            def _(s):
                return s.activation(stat[:, 6:7], stat[:, 5:6], AF.Sqrt, bias=epsc[:, :])

            @op(P, "vector", waits=[ac.now], inc=dv)
            def _(v):
                v.reciprocal(stat[:, 7:8], stat[:, 6:7])
                v.tensor_mul(stat[:, 8:9], stat[:, 1:2], stat[:, 7:8])
                return v.tensor_scalar_mul(stat[:, 9:10], stat[:, 8:9], -1.0)

            w = [dv.now]
            if t >= 2:
                w += [(st, st_ln[t - 2]), (mm, mm_tp[t - 2])]

            @op(P, "scalar", waits=w, inc=ac)
            def _(s, sl=sl):
                return s.activation(lnt[:, sl], xtv[:, sl], AF.Identity,
                                    bias=stat[:, 9:10], scale=stat[:, 7:8])
            ac_ln[t] = ac.n

            dma(P, "sync", ln_ag_in[t * 128:(t + 1) * 128, :], lnt[:, sl], st,
                waits=[ac.now])
            st_ln[t] = st.n

            for c in range(8):
                @op(P, "tensor", waits=[ac.now, dv.now], inc=mm)
                def _(te, s0=s0, c=c):
                    return te.transpose(
                        tps[:, :], lnt[:, s0 + c * 128: s0 + (c + 1) * 128],
                        identb[:, :])

                @op(P, "vector", waits=[mm.now], inc=dv)
                def _(v, t=t, c=c):
                    return v.tensor_copy(
                        arena[:, O_LNFM + c * T_OWN + t * 128:
                              O_LNFM + c * T_OWN + (t + 1) * 128], tps[:, :])
            mm_tp[t] = mm.n

            @op(P, "tensor", waits=[dv.now], inc=mm)
            def _(te, t=t):
                r = None
                for c in range(8):
                    r = te.matmul(
                        psA[:, 0:E],
                        arena[:, O_LNFM + c * T_OWN + t * 128:
                              O_LNFM + c * T_OWN + (t + 1) * 128],
                        wrb[:, c * E:(c + 1) * E], start=(c == 0), stop=(c == 7))
                return r

            @op(P, "vector", waits=[mm.now], inc=dv)
            def _(v, t=t):
                return v.tensor_add(logits[:, t * E:(t + 1) * E], psA[:, 0:E],
                                    brb[:, :])
        dv_logits_all = dv.n
        st_ln_all = st.n

        mm_ohs = [None] * NT
        for t in range(NT):
            tE = slice(t * E, (t + 1) * E)
            w = [(mm, mm_ohs[t - 1])] if t >= 1 else []

            @op(P, "vector", waits=w, inc=dv)
            def _(v, tE=tE):
                v.max(out=small[:, 0:8], in_=logits[:, tE])
                v.tensor_scalar_mul(small[:, 8:9], small[:, 0:1], -1.0)
                return v.tensor_sub(small[:, 9:10], small[:, 1:2], small[:, 0:1])

            @op(P, "scalar", waits=[dv.now], inc=ac)
            def _(s, tE=tE):
                s.activation(sqt[:, 0:E], logits[:, tE], AF.Exp, bias=small[:, 8:9])
                return s.activation(sqt[:, E:E + 1], small[:, 9:10], AF.Exp)

            @op(P, "vector", waits=[ac.now], inc=dv)
            def _(v, tE=tE, t=t):
                v.reduce_sum(small[:, 10:11], sqt[:, 0:E], axis=AX.X)
                v.reciprocal(parr[:, t:t + 1], small[:, 10:11])
                v.tensor_mul(parr[:, NT + t:NT + t + 1], sqt[:, E:E + 1],
                             parr[:, t:t + 1])
                v.tensor_scalar(small[:, 0:8], logits[:, tE], small[:, 0:1], None,
                                ALU.is_equal)
                v.tensor_scalar(small[:, 0:8], small[:, 0:8], -1.0, 1.0,
                                ALU.mult, ALU.add)
                v.scalar_tensor_tensor(small[:, 0:8], small[:, 0:8], 1e9,
                                       iota8[:, :], ALU.mult, ALU.add)
                v.tensor_reduce(earr[:, t:t + 1], small[:, 0:8], AX.X, ALU.min)
                v.tensor_scalar(ohs1[:, tE], iota8[:, :], earr[:, t:t + 1], None,
                                ALU.is_equal)
                v.max(out=small[:, 0:8], in_=logits[:, tE])
                v.tensor_scalar(small[:, 8:16], logits[:, tE], small[:, 1:2], None,
                                ALU.is_equal)
                v.tensor_scalar(small[:, 8:16], small[:, 8:16], -1.0, 1.0,
                                ALU.mult, ALU.add)
                v.scalar_tensor_tensor(small[:, 8:16], small[:, 8:16], 1e9,
                                       iota8[:, :], ALU.mult, ALU.add)
                v.scalar_tensor_tensor(small[:, 8:16], ohs1[:, tE], 1e9,
                                       small[:, 8:16], ALU.mult, ALU.add)
                v.tensor_reduce(earr[:, NT + t:NT + t + 1], small[:, 8:16],
                                AX.X, ALU.min)
                v.tensor_scalar(small[:, 0:8], iota8[:, :],
                                earr[:, NT + t:NT + t + 1], None, ALU.is_equal)
                v.tensor_add(ohs1[:, tE], ohs1[:, tE], small[:, 0:8])
                return v.tensor_copy(ohsb[:, :], ohs1[:, tE])

            @op(P, "tensor", waits=[dv.now], inc=mm)
            def _(te):
                return te.transpose(tps[0:E, 0:128], ohsb[:, :], identb[:, :])
            mm_ohs[t] = mm.n

            @op(P, "vector", waits=[mm.now], inc=dv)
            def _(v, t=t):
                return v.tensor_copy(ohs_fm[:, t * 128:(t + 1) * 128],
                                     tps[0:E, 0:128])

        @op(P, "vector", inc=dv)
        def _(v):
            v.tensor_tensor_scan(cinc[:, :], ohs_fm[:, :], ohs_fm[:, :], 0.0,
                                 ALU.add, ALU.bypass)
            v.tensor_copy(offs[:, 2:3], cinc[:, 1023:1024])
            return v.tensor_sub(offs[:, 3:4], cinc[:, 2047:2048],
                                cinc[:, 1023:1024])

        @op(P, "tensor", waits=[dv.now], inc=mm)
        def _(te):
            return te.transpose(psB[0:2, 0:E], offs[:, 2:4], ident[0:E, 0:E])

        @op(P, "vector", waits=[mm.now], inc=dv)
        def _(v):
            return v.tensor_copy(cnt_sb[:, :], psB[0:2, 0:E])

        dma(P, "sync", cnt_ag_in.ap(), cnt_sb[:, :], st, waits=[dv.now])
        st_cnt = st.n

        dma(P, "gpsimd", rs_in.ap().rearrange("(a b) o -> a (b o)", b=160),
            zed[:, :], gp)
        gp_zero = gp.n

        @op(P, "gpsimd", waits=[(st, st_cnt)], inc=cc)
        def _(g):
            return g.collective_compute(
                "AllGather", ALU.bypass, replica_groups=[list(range(NCORE))],
                ins=[cnt_ag_in.ap().opt()], outs=[cnt_ag_out.ap().opt()])
        cc_cnt = cc.n

        @op(P, "gpsimd", waits=[(st, st_ln_all)], inc=cc)
        def _(g):
            return g.collective_compute(
                "AllGather", ALU.bypass, replica_groups=[list(range(NCORE))],
                ins=[ln_ag_in.ap().opt()], outs=[ln_ag_out.ap().opt()])
        cc_lnag = cc.n

        dma(P, "sync", cagb[:, :], cnt_ag_out.ap(), ld, waits=[(cc, cc_cnt)])

        @op(P, "tensor", waits=[ld.now], inc=mm)
        def _(te):
            return te.matmul(psA[0:E, 0:2], cagb[:, :], selpref[:, :],
                             start=True, stop=True)

        @op(P, "vector", waits=[mm.now], inc=dv)
        def _(v):
            v.tensor_copy(offs[:, 0:2], psA[0:E, 0:2])
            v.tensor_add(offs[:, 1:2], offs[:, 0:1], offs[:, 1:2])
            v.tensor_sub(cinc[:, :], cinc[:, :], ohs_fm[:, :])
            v.tensor_scalar_add(posg[:, 0:1024], cinc[:, 0:1024], offs[:, 0:1])
            return v.tensor_scalar_add(posg[:, 1024:2048], cinc[:, 1024:2048],
                                       offs[:, 1:2])

        for t in range(NT):
            @op(P, "tensor", waits=[dv.now], inc=mm)
            def _(te, t=t):
                return te.transpose(psB[:, 0:E], posg[:, t * 128:(t + 1) * 128],
                                    ident[0:E, 0:E])

            @op(P, "vector", waits=[mm.now], inc=dv)
            def _(v, t=t):
                return v.tensor_copy(postm[:, t * E:(t + 1) * E], psB[:, 0:E])

        for t in range(NT):
            tE = slice(t * E, (t + 1) * E)

            @op(P, "vector", inc=dv)
            def _(v, tE=tE, t=t):
                r = None
                for k in range(2):
                    col = k * NT + t
                    ek = earr[:, col:col + 1]
                    v.tensor_scalar(small[:, 0:8], iota8[:, :], ek, None,
                                    ALU.is_equal)
                    v.tensor_mul(small[:, 0:8], small[:, 0:8], postm[:, tE])
                    v.reduce_sum(small[:, 8:9], small[:, 0:8], axis=AX.X)
                    v.tensor_scalar(small[:, 9:10], small[:, 8:9], 1280.0, None,
                                    ALU.is_ge)
                    v.tensor_scalar_min(small[:, 10:11], small[:, 8:9], 1279.0)
                    v.tensor_scalar(small[:, 11:12], ek, 2560.0, None, ALU.mult)
                    v.tensor_scalar_add(small[:, 11:12], small[:, 11:12],
                                        cconst[:, 0:1])
                    v.tensor_add(small[:, 11:12], small[:, 11:12], small[:, 10:11])
                    v.tensor_copy(ggix[:, col:col + 1], small[:, 11:12])
                    v.scalar_tensor_tensor(small[:, 12:13], small[:, 9:10], 1e9,
                                           small[:, 11:12], ALU.mult, ALU.add)
                    v.tensor_copy(gidx[:, col:col + 1], small[:, 12:13])
                    v.tensor_scalar(small[:, 13:14], small[:, 9:10], -0.5, 0.5,
                                    ALU.mult, ALU.add)
                    r = v.tensor_mul(warr[:, col:col + 1], parr[:, col:col + 1],
                                     small[:, 13:14])
                return r

            gp.inc(32)
            dv_t = dv.n

            def scat(g, t=t, dv_t=dv_t, gpz=gp_zero):
                _w(g, [(dv, dv_t), (gp, gpz)])
                for k in range(2):
                    col = k * NT + t
                    g.indirect_dma_start(
                        rs_in[:, :], IOA(gidx[:, col:col + 1], 0),
                        rowval[:, t:t + 1], None,
                        bounds_check=regs["bc_slot"], oob_is_err=False,
                    ).then_inc(gp.h, 16)
            P.add("gpsimd", scat)

        @op(P, "gpsimd", waits=[gp.now], inc=cc)
        def _(g):
            return g.collective_compute(
                "ReduceScatter", ALU.add, replica_groups=[list(range(NCORE))],
                ins=[rs_in.ap().opt()], outs=[rs_out.ap().opt()])
        cc_rs = cc.n

        for k in range(20):
            dma(P, "sync", mapcol[:, k:k + 1], rs_out[k * 128:(k + 1) * 128, :],
                ld, waits=[(cc, cc_rs)] if k == 0 else ())

        @op(P, "vector", waits=[ld.now], inc=dv)
        def _(v):
            v.tensor_scalar_add(mapcol[:, :], mapcol[:, :], -1.0)
            return v.tensor_copy(mapi[:, :], mapcol[:, :])
        dv_map = dv.n

        for c in range(8):
            dma(P, "gpsimd", arena[:, O_W1 + c * HID: O_W1 + (c + 1) * HID],
                w1_in[c * 128:(c + 1) * 128, :], wst,
                waits=list(entry) if c == 0 else ())
        wst_w1 = wst.n
        if first_call:
            dma(P, "gpsimd", w2bf.ap(), w2_in.ap(), wst)
        wst_w2 = wst.n

        mm_prev_tp = 0
        st_prev_y = st.n
        for s in range(5):
            @op(P, "gpsimd",
                waits=[(dv, dv_map)] + ([(mm, mm_prev_tp)] if s else []), inc=gp)
            def _(g):
                return g.memset(arena[:, O_BUFTM:O_BUFTM + 4 * D], 0.0)

            for j in range(4):
                gp.inc(16)

                def gath(g, s=s, j=j):
                    g.indirect_dma_start(
                        arena[:, O_BUFTM + j * D: O_BUFTM + (j + 1) * D], None,
                        ln_ag_out[:, :], IOA(mapi[:, 4 * s + j:4 * s + j + 1], 0),
                        bounds_check=regs["bc_row"], oob_is_err=False,
                    ).then_inc(gp.h, 16)
                P.add("gpsimd", gath)
            gp_gath = gp.n

            for j in range(4):
                for c in range(8):
                    @op(P, "tensor",
                        waits=[(gp, gp_gath), (dv, dv_logits_all), (dv, dv.n)],
                        inc=mm)
                    def _(te, j=j, c=c):
                        return te.transpose(
                            tps[:, :],
                            arena[:, O_BUFTM + j * D + c * 128:
                                  O_BUFTM + j * D + (c + 1) * 128],
                            identb[:, :])

                    @op(P, "vector", waits=[mm.now], inc=dv)
                    def _(v, s=s, j=j, c=c):
                        dst = arena[:, O_BUFFM + c * SLOT_OWN + s * 512 + j * 128:
                                    O_BUFFM + c * SLOT_OWN + s * 512 + (j + 1) * 128]
                        return v.tensor_copy(dst, tps[:, :])
            mm_prev_tp = mm.n
            dv_tp_end = dv.n

            ac_h0 = ac.n
            for hc in range(32):
                hp = psA if hc % 2 == 0 else psB
                w = [(dv, dv_tp_end), (wst, wst_w1)]
                if hc >= 2:
                    w.append((ac, ac_h0 + hc - 1))

                @op(P, "tensor", waits=w, inc=mm)
                def _(te, s=s, hc=hc, hp=hp):
                    r = None
                    for c in range(8):
                        r = te.matmul(
                            hp[:, :],
                            arena[:, O_W1 + c * HID + hc * 128:
                                  O_W1 + c * HID + (hc + 1) * 128],
                            arena[:, O_BUFFM + c * SLOT_OWN + s * 512:
                                  O_BUFFM + c * SLOT_OWN + (s + 1) * 512],
                            start=(c == 0), stop=(c == 7))
                    return r

                @op(P, "scalar", waits=[mm.now], inc=ac)
                def _(sc, hc=hc, hp=hp):
                    return sc.activation(
                        arena[:, O_HFM + hc * 512: O_HFM + (hc + 1) * 512],
                        hp[:, :], AF.Relu, bias=b1c[:, hc:hc + 1])
            ac_h_end = ac.n

            for half in range(2):
                wst_h0 = wst.n
                mm_y0 = mm.n
                for hc in range(32):
                    buf = (hc % 2) * 512
                    w = [(mm, mm_y0 + hc)] if hc >= 2 else [(wst, wst_w2), (mm, mm_y0)]
                    dma(P, "sync", w2sb[:, buf:buf + 512],
                        w2bf[hc * 128:(hc + 1) * 128, half * 512:(half + 1) * 512],
                        wst, waits=w)

                @op(P, "tensor",
                    waits=[(ac, ac_h_end), (dv, dv.n), (st, st_prev_y)], inc=mm)
                def _(te, half=half):
                    r = None
                    for st4 in range(4):
                        r = te.matmul(psY[st4][:, :], ones_f[:, :],
                                      b2r[:, half * 512:(half + 1) * 512],
                                      start=True, stop=False)
                    return r

                for hc in range(32):
                    buf = (hc % 2) * 512

                    @op(P, "tensor", waits=[(wst, wst_h0 + 16 * (hc + 1))], inc=mm)
                    def _(te, hc=hc, buf=buf):
                        r = None
                        for st4 in range(4):
                            r = te.matmul(
                                psY[st4][:, :],
                                arena[:, O_HFM + hc * 512 + st4 * 128:
                                      O_HFM + hc * 512 + (st4 + 1) * 128],
                                w2sb[:, buf:buf + 512],
                                start=False, stop=(hc == 31))
                        return r

                for st4 in range(4):
                    @op(P, "vector", waits=[mm.now], inc=dv)
                    def _(v, st4=st4, half=half):
                        return v.tensor_copy(
                            ysb[:, st4 * D + half * 512: st4 * D + (half + 1) * 512],
                            psY[st4][:, :])

            for st4 in range(4):
                dma(P, "sync",
                    yb_ag_in[s * 512 + st4 * 128: s * 512 + (st4 + 1) * 128, :],
                    ysb[:, st4 * D:(st4 + 1) * D], st, waits=[dv.now])
            st_prev_y = st.n
        st_y_end = st.n

        @op(P, "gpsimd", waits=[(st, st_y_end)], inc=cc)
        def _(g):
            return g.collective_compute(
                "AllGather", ALU.bypass, replica_groups=[list(range(NCORE))],
                ins=[yb_ag_in.ap().opt()], outs=[yb_ag_out.ap().opt()])
        cc_yag = cc.n

        lxc = lx.n
        st_comb = [None] * NT
        for t in range(NT):
            wg = [(cc, cc_yag)]
            if t >= 1:
                wg.append((st, st_comb[t - 1]))
            gp.inc(32)

            def gath2(g, t=t, wg=wg, gpn=gp.n):
                _w(g, wg)
                for k in range(2):
                    col = k * NT + t
                    g.indirect_dma_start(
                        ygb[:, k * D:(k + 1) * D], None,
                        yb_ag_out[:, :], IOA(ggix[:, col:col + 1], 0),
                        bounds_check=regs["bc_slot"], oob_is_err=False,
                    ).then_inc(gp.h, 16)
            P.add("gpsimd", gath2)
            gp_g = gp.n

            w = ([(st, st_comb[t - 2])] if t >= 2 and st_comb[t - 2]
                 else [(mm, mm.n)])
            dma(P, "scalar", xtv[:, (t % 2) * D:(t % 2 + 1) * D],
                src_dram[t * 128:(t + 1) * 128, :], lx, waits=w)

            wv = [(gp, gp_g), lx.now]
            if t >= 1:
                wv.append((st, st_comb[t - 1]))

            @op(P, "vector", waits=wv, inc=dv)
            def _(v, t=t):
                a = slice((t % 2) * D, (t % 2 + 1) * D)
                v.scalar_tensor_tensor(accv[:, 0:D], ygb[:, 0:D], warr[:, t:t + 1],
                                       xtv[:, a], ALU.mult, ALU.add)
                return v.scalar_tensor_tensor(accv[:, 0:D], ygb[:, D:2 * D],
                                              warr[:, NT + t:NT + t + 1],
                                              accv[:, 0:D], ALU.mult, ALU.add)

            dma(P, "sync", dst_dram[t * 128:(t + 1) * 128, :],
                accv[:, 0:D], st, waits=[dv.now])
            st_comb[t] = st.n
        return st_comb


    def emit_attention(src_dram, dst_dram, ffn1_tiles, upto=99):
        O_XNFM = 0
        O_KH = [0, 8192]
        O_VH = [16384, 24576]
        O_WV = 32768
        O_WOUT = 32768
        O_WBLK = 49152
        O_LNW = 49152
        O_KRAW = 51200
        O_KSM = 53248
        O_VSB = 55296
        O_YSM = 56320
        O_KSTG = 51200
        O_PT = 53248
        O_QB = 55296
        O_SIG = 51200
        O_POSQ = 55296
        O_BINV = 59392
        O_MROW = 56320
        O_MNEG = 60416
        O_MASK = 68608
        O_KLR = 70656
        O_KLC = 72704
        O_KLT = 72704
        O_GFM = 0
        O_OSB = 51200
        O_OLNW = 53248

        def A(a, b):
            return arena[:, a:b]

        def Af(a, n):
            return arena[:, a:a + 2 * n].bitcast(F32)

        dma(P, "sync", binc[:, :], binc_in.ap(), ld)
        dma(P, "sync", hpar[:, :], hpar_in.ap(), ld)
        dma(P, "sync", icon[:, :], icon_in.ap(), ld)
        dma(P, "gpsimd", A(O_MASK, O_MASK + 2048), masks_in.ap(), gp)

        @op(P, "gpsimd", inc=gp)
        def _(g):
            g.memset(onescol[:, :], 1.0)
            g.memset(onesb[:, :], 1.0)
            g.memset(zrowb[:, :], 0.0)
            return g.memset(zrow[:, :], 0.0)

        @op(P, "scalar", waits=[ld.now, gp.now], inc=ac)
        def _(sc):
            sc.activation(qscale[:, :], hpar[:, 16:32], AF.Exp, scale=-2.0)
            return sc.activation(smsig[:, :], hpar[:, 0:16], AF.Sigmoid)

        @op(P, "vector", waits=[ac.now], inc=dv)
        def _(v):
            v.tensor_scalar_mul(qscale[:, :], qscale[:, :], float(1.0 / np.sqrt(128.0)))
            return v.tensor_mul(binqs[:, :], binc[:, 0:16], qscale[:, :])
        dv_aconst = dv.n

        dma(P, "gpsimd", A(O_LNW, O_LNW + 1024), lnw4_in[:, 0:1024], gp)
        dma(P, "gpsimd", A(O_LNW + 1024, O_LNW + 2048), lnw4_in[:, 1024:2048], gp)
        gp_lnw = gp.n

        tmpb = ysb[:, 0:1024]
        ac_lnA = [None] * NT
        mm_tpA = [None] * NT
        lx0a = lx.n

        def ln_stats(srcap):
            @op(P, "vector", inc=dv)
            def _(v):
                v.reduce_sum(stat[:, 0:1], srcap, axis=AX.X)
                return v.tensor_scalar_mul(stat[:, 1:2], stat[:, 0:1], 1.0 / D)

            @op(P, "scalar", waits=[dv.now], inc=ac)
            def _(s_):
                return s_.activation(sqt[:, :], srcap, AF.Square)

            @op(P, "vector", waits=[ac.now], inc=dv)
            def _(v):
                v.reduce_sum(stat[:, 2:3], sqt[:, :], axis=AX.X)
                v.tensor_scalar_mul(stat[:, 3:4], stat[:, 2:3], 1.0 / D)
                v.tensor_mul(stat[:, 4:5], stat[:, 1:2], stat[:, 1:2])
                return v.tensor_sub(stat[:, 5:6], stat[:, 3:4], stat[:, 4:5])

            @op(P, "scalar", waits=[dv.now], inc=ac)
            def _(s_):
                return s_.activation(stat[:, 6:7], stat[:, 5:6], AF.Sqrt, bias=epsc[:, :])

            @op(P, "vector", waits=[ac.now], inc=dv)
            def _(v):
                v.reciprocal(stat[:, 7:8], stat[:, 6:7])
                v.tensor_mul(stat[:, 8:9], stat[:, 1:2], stat[:, 7:8])
                return v.tensor_scalar_mul(stat[:, 9:10], stat[:, 8:9], -1.0)

        for t in range(NT):
            s0 = (t % 2) * D
            sl = slice(s0, s0 + D)
            wx = ([(ac, ac_lnA[t - 2])] if t >= 2 else []) + \
                [(st, ffn1_tiles[t])]
            dma(P, "scalar", xt[:, sl], src_dram[t * 128:(t + 1) * 128, :],
                lx, waits=wx)

            @op(P, "vector", waits=[(lx, lx0a + 16 * (t + 1)), (dv, dv_aconst)], inc=dv)
            def _(v, sl=sl):
                v.reduce_sum(stat[:, 0:1], xt[:, sl], axis=AX.X)
                return v.tensor_scalar_mul(stat[:, 1:2], stat[:, 0:1], 1.0 / D)

            @op(P, "scalar", waits=[dv.now], inc=ac)
            def _(s_, sl=sl):
                return s_.activation(sqt[:, :], xt[:, sl], AF.Square)

            @op(P, "vector", waits=[ac.now], inc=dv)
            def _(v):
                v.reduce_sum(stat[:, 2:3], sqt[:, :], axis=AX.X)
                v.tensor_scalar_mul(stat[:, 3:4], stat[:, 2:3], 1.0 / D)
                v.tensor_mul(stat[:, 4:5], stat[:, 1:2], stat[:, 1:2])
                return v.tensor_sub(stat[:, 5:6], stat[:, 3:4], stat[:, 4:5])

            @op(P, "scalar", waits=[dv.now], inc=ac)
            def _(s_):
                return s_.activation(stat[:, 6:7], stat[:, 5:6], AF.Sqrt, bias=epsc[:, :])

            @op(P, "vector", waits=[ac.now], inc=dv)
            def _(v):
                v.reciprocal(stat[:, 7:8], stat[:, 6:7])
                v.tensor_mul(stat[:, 8:9], stat[:, 1:2], stat[:, 7:8])
                return v.tensor_scalar_mul(stat[:, 9:10], stat[:, 8:9], -1.0)

            @op(P, "scalar", waits=[dv.now], inc=ac)
            def _(s_, sl=sl):
                return s_.activation(accv[:, :], xt[:, sl], AF.Identity,
                                     bias=stat[:, 9:10], scale=stat[:, 7:8])
            ac_lnA[t] = ac.n

            ln_stats(accv[:, :])

            @op(P, "scalar", waits=[dv.now], inc=ac)
            def _(s_):
                return s_.activation(sqt[:, :], accv[:, :], AF.Identity,
                                     bias=stat[:, 9:10], scale=stat[:, 7:8])

            w = [ac.now, (gp, gp_lnw)]
            if t >= 2:
                w.append((mm, mm_tpA[t - 2]))

            @op(P, "vector", waits=w, inc=dv)
            def _(v, sl=sl):
                v.tensor_mul(tmpb, sqt[:, :], A(O_LNW, O_LNW + 1024))
                return v.tensor_add(lnt[:, sl], tmpb, A(O_LNW + 1024, O_LNW + 2048))

            for c in range(8):
                @op(P, "tensor", waits=[dv.now], inc=mm)
                def _(te, s0=s0, c=c):
                    return te.transpose(
                        tps[:, :], lnt[:, s0 + c * 128: s0 + (c + 1) * 128],
                        identb[:, :])

                @op(P, "vector", waits=[mm.now], inc=dv)
                def _(v, t=t, c=c):
                    return v.tensor_copy(
                        A(O_XNFM + c * T_OWN + t * 128,
                          O_XNFM + c * T_OWN + (t + 1) * 128), tps[:, :])
            mm_tpA[t] = mm.n
        dv_xnfm = dv.n

        if upto < 2:
            return st.n
        for c in range(8):
            dma(P, "gpsimd", A(O_WV + c * 2048, O_WV + (c + 1) * 2048), wv_in[c],
                wvs, waits=[(dv, dv_xnfm)] if c == 0 else ())
        dma(P, "gpsimd", Af(O_BINV, 2048)[0:1, :], binv_in.ap(), wvs)
        wvs_wv = wvs.n
        VBASE = 2048 * 2048
        gp_prev_vscat = 0
        for tt in range(NT):
            dv_ev = []
            for vc in range(4):
                @op(P, "tensor", waits=[(wvs, wvs_wv), (dv, dv.n)], inc=mm)
                def _(te, tt=tt, vc=vc):
                    te.matmul(psY[vc][:, :], ones_f[:, :],
                              Af(O_BINV, 2048)[0:1, vc * 512:(vc + 1) * 512],
                              start=True, stop=False)
                    r = None
                    for c in range(8):
                        r = te.matmul(
                            psY[vc][:, :],
                            A(O_XNFM + c * T_OWN + tt * 128,
                              O_XNFM + c * T_OWN + (tt + 1) * 128),
                            A(O_WV + c * 2048 + vc * 512,
                              O_WV + c * 2048 + (vc + 1) * 512),
                            start=False, stop=(c == 7))
                    return r

                @op(P, "vector", waits=[mm.now] +
                    ([(gp, gp_prev_vscat)] if gp_prev_vscat and vc == 0 else []),
                    inc=dv)
                def _(v, vc=vc):
                    return v.tensor_copy(
                        arena[:, O_VSB + vc * 512: O_VSB + (vc + 1) * 512],
                        psY[vc][:, :])
            gp_prev_vscat = gp.n + 16
            gp.inc(16)

            def vscat(g, tt=tt, dvn=dv.n):
                _w(g, [(dv, dvn)])
                g.dma_start(
                    out=bass.AP(kvy_ag_in, VBASE + tt * 128,
                                [[16 * 128, 128], [128 * 16 * 128, 16], [1, 128]]),
                    in_=arena[:, O_VSB:O_VSB + 2048].rearrange("p (h d) -> p h d", d=128),
                    allow_slow_non_contiguous=True,
                ).then_inc(gp.h, 16)
            P.add("gpsimd", vscat)
        gp_vscat = gp.n

        order1 = list(range(16, 32)) + [64]
        order2 = list(range(0, 16)) + list(range(48, 64))
        wblk_free = [0, 0]
        evb = ysb[:, 1024:2048]
        stq = [S("stqA"), S("stqB")]
        wbsem = [S("wbA"), S("wbB")]

        def a2_block(oi, o):
            buf = oi % 2
            wload_waits = [(mm, wblk_free[buf])] if wblk_free[buf] else [(dv, dv_xnfm)]
            dma(P, "gpsimd", A(O_WBLK + buf * 1024, O_WBLK + (buf + 1) * 1024),
                wblk_in[o], wbsem[buf], waits=wload_waits)
            wst_o = wbsem[buf].n
            for tb in range(4):
                hp = psA if tb % 2 == 0 else psB

                @op(P, "tensor", waits=[(wbsem[buf], wst_o), (ac, ac.n), (dv, dv.n),
                                        (gp, gp_vscat)], inc=mm)
                def _(te, o=o, tb=tb, buf=buf, hp=hp):
                    r = None
                    for c in range(8):
                        r = te.matmul(
                            hp[:, :],
                            A(O_WBLK + buf * 1024 + c * 128,
                              O_WBLK + buf * 1024 + (c + 1) * 128),
                            A(O_XNFM + c * T_OWN + tb * 512,
                              O_XNFM + c * T_OWN + (tb + 1) * 512),
                            start=(c == 0), stop=(c == 7))
                    return r
                if tb == 3:
                    wblk_free[buf] = mm.n

                if 16 <= o < 32:
                    @op(P, "scalar", waits=[mm.now], inc=ac)
                    def _(s_, o=o, tb=tb, hp=hp):
                        return s_.activation(
                            arena[:, O_KRAW + tb * 512: O_KRAW + (tb + 1) * 512],
                            hp[:, :], AF.Identity, bias=binc[:, o:o + 1])
                elif o == 64:
                    @op(P, "scalar", waits=[mm.now], inc=ac)
                    def _(s_, hp=hp):
                        return s_.activation(arena[0:16, O_YSM:O_YSM + 512],
                                             hp[0:16, :], AF.Identity,
                                             bias=binc[0:16, 64:65])
                    dma(P, "sync", kvy_ag_in[4096:4112, tb * 512:(tb + 1) * 512],
                        arena[0:16, O_YSM:O_YSM + 512], st, waits=[ac.now])
                elif o < 16:
                    eoff = 1024 + (tb % 2) * 512

                    sq = stq[tb % 2]

                    @op(P, "scalar", waits=[mm.now] +
                        ([(sq, sq.n)] if sq.n else []), inc=ac)
                    def _(s_, o=o, hp=hp, eoff=eoff):
                        return s_.activation(ysb[:, eoff:eoff + 512], hp[:, :],
                                             AF.Identity, bias=binqs[:, o:o + 1],
                                             scale=qscale[:, o:o + 1])
                    dma(P, "sync", q_fm[o * 128:(o + 1) * 128, tb * 512:(tb + 1) * 512],
                        ysb[:, eoff:eoff + 512], sq, waits=[ac.now])
                else:
                    eoff = 1024 + (tb % 2) * 512

                    sq = stq[tb % 2]

                    @op(P, "scalar", waits=[mm.now] +
                        ([(sq, sq.n)] if sq.n else []), inc=ac)
                    def _(s_, o=o, hp=hp, eoff=eoff):
                        return s_.activation(ysb[:, eoff:eoff + 512], hp[:, :],
                                             AF.Identity, bias=binc[:, o:o + 1])
                    dma(P, "sync",
                        p_fm[(o - 48) * 128:(o - 47) * 128, tb * 512:(tb + 1) * 512],
                        ysb[:, eoff:eoff + 512], sq, waits=[ac.now])

            if 16 <= o < 32:
                h = o - 16
                @op(P, "vector", waits=[ac.now, (st, st.n)], inc=dv)
                def _(v, h=h):
                    v.tensor_copy(arena[:, O_KLC + h: O_KLC + h + 1],
                                  arena[:, O_KRAW + 1023: O_KRAW + 1024])
                    v.tensor_copy(arena[:, O_KLC + 16 + h: O_KLC + 17 + h],
                                  arena[:, O_KRAW + 2047: O_KRAW + 2048])
                    for hb in range(2):
                        b0 = O_KRAW + hb * 1024
                        s0 = O_KSM + hb * 1024
                        v.tensor_sub(tmpb[:, 0:1023],
                                     arena[:, b0:b0 + 1023],
                                     arena[:, b0 + 1:b0 + 1024])
                        v.scalar_tensor_tensor(
                            arena[:, s0 + 1:s0 + 1024], tmpb[:, 0:1023],
                            smsig[:, h:h + 1], arena[:, b0 + 1:b0 + 1024],
                            ALU.mult, ALU.add)
                        v.tensor_scalar(stat[:, 10:11], smsig[:, h:h + 1], -1.0, 1.0,
                                        ALU.mult, ALU.add)
                        v.tensor_mul(arena[:, s0:s0 + 1],
                                     arena[:, b0:b0 + 1], stat[:, 10:11])
                    return v.tensor_copy(stat[:, 11:12], stat[:, 10:11])
                dma(P, "sync", kvy_ag_in[h * 128:(h + 1) * 128, :],
                    arena[:, O_KSM:O_KSM + 2048], st, waits=[dv.now])

        for oi, o in enumerate(order1):
            a2_block(oi, o)
        for sdx in range(2):
            @op(P, "tensor", waits=[(dv, dv.n)], inc=mm)
            def _(te, sdx=sdx):
                return te.transpose(tps[0:16, 0:128],
                                    arena[:, O_KLC + sdx * 16: O_KLC + (sdx + 1) * 16],
                                    identb[:, :])

            @op(P, "vector", waits=[mm.now], inc=dv)
            def _(v, sdx=sdx):
                return v.tensor_copy(ysb[0:16, sdx * 128:(sdx + 1) * 128],
                                     tps[0:16, 0:128])

            def _klst(e, sdx=sdx, w=dv.now):
                _w(e, [w])
                e.dma_start(out=bass.AP(kvy_ag_in, (4112 + sdx) * 2048,
                                        [[128, 16], [1, 128]]),
                            in_=ysb[0:16, sdx * 128:(sdx + 1) * 128]).then_inc(st.h, 16)
            st.inc(16)
            P.add("sync", _klst)
        st_fm_all = st.n

        if upto < 3:
            return st.n
        @op(P, "gpsimd", waits=[(st, st_fm_all), (gp, gp_vscat)], inc=cc)
        def _(g):
            return g.collective_compute(
                "AllGather", ALU.bypass,
                replica_groups=[[0, 1, 2, 3], [4, 5, 6, 7]],
                ins=[kvy_ag_in.ap().opt()], outs=[kvy_ag_out.ap().opt()])
        cc_kvy = cc.n

        for oj, o in enumerate(order2):
            a2_block(len(order1) + oj, o)
        stq_all = [stq[0].n, stq[1].n]
        mm_a2_end = mm.n
        ac_a2_end = ac.n

        if upto < 4:
            return st.n
        @op(P, "vector", inc=dv)
        def _(v):
            return v.memset(carry[:, :], 0.0)
        for qt in range(4):
            for ci in range(2):
                c = 2 * qt + ci
                rl, sl_ = (min(c, 7 - c)), (0 if c < 4 else 1)
                dma(P, "gpsimd",
                    Af(O_SIG, 2048)[0:16, ci * 1024:(ci + 1) * 1024],
                    kvy_ag_out[rl * 4224 + 4096: rl * 4224 + 4112,
                               sl_ * 1024:(sl_ + 1) * 1024],
                    gp, waits=[(cc, cc_kvy)] + ([(st, st.n)] if qt else []))

            @op(P, "scalar", waits=[gp.now], inc=ac)
            def _(s_):
                return s_.activation(Af(O_POSQ, 2048)[0:16, :],
                                     Af(O_SIG, 2048)[0:16, :], AF.Sigmoid)

            @op(P, "vector", waits=[ac.now], inc=dv)
            def _(v, qt=qt):
                v.tensor_tensor_scan(Af(O_SIG, 2048)[0:16, :],
                                     Af(O_POSQ, 2048)[0:16, :],
                                     Af(O_POSQ, 2048)[0:16, :],
                                     0.0 if qt == 0 else carry[:, 0:1],
                                     ALU.add, ALU.bypass)
                v.drain()
                return v.tensor_copy(carry[:, 0:1], Af(O_SIG, 2048)[0:16, 2047:2048])

            dma(P, "sync", pos_dram[:, qt * 2048:(qt + 1) * 2048],
                Af(O_SIG, 2048)[0:16, :], st, waits=[dv.now])
        st_pos = st.n
        st_q = st_pos

        def a45_init_sy(sy):
            R["sy_cA"] = sy.to_reg(sy.value_load(icon[0:1, 4:5]))
            R["sy_cB"] = sy.to_reg(sy.value_load(icon[0:1, 5:6]))
            _w(sy, [(st, st_pos)])
            sy.dma_start(out=Af(O_MROW, 2048)[0:16, 0:1024],
                         in_=bass.AP(pos_dram, R["sy_cA"], [[8192, 16], [1, 1024]])
                         ).then_inc(ldm.h, 16)
            sy.dma_start(out=Af(O_MROW, 2048)[0:16, 1024:2048],
                         in_=bass.AP(pos_dram, R["sy_cB"], [[8192, 16], [1, 1024]])
                         ).then_inc(ldm.h, 16)
        R = {}
        ldm.inc(32)
        P.add("sync", a45_init_sy)

        @op(P, "vector", waits=[ldm.now], inc=dv)
        def _(v):
            return v.tensor_scalar(ysb[0:16, 2048:4096],
                                   Af(O_MROW, 2048)[0:16, :], -1.0, -24.0,
                                   ALU.mult, ALU.add)

        dma(P, "sync", mneg_dram.ap(), ysb[0:16, 2048:4096], st,
            waits=[dv.now])
        st_mneg = st.n

        if upto < 5:
            return st.n
        KSTG = [arena[:, O_KSTG + i * 256: O_KSTG + i * 256 + 128] for i in range(8)]
        VSTG = [arena[:, O_KSTG + i * 256 + 128: O_KSTG + (i + 1) * 256]
                for i in range(8)]
        PT = [arena[:, O_PT + i * 512: O_PT + (i + 1) * 512] for i in range(4)]
        SB4 = [psA, psB, psY3, psY2]
        O_MNEGB = 60416

        def a5_init_pe(te):
            R["pe_rb"] = te.alloc_register(name="pe_rb")
            R["pe_t1"] = te.alloc_register(name="pe_t1")
            R["pe_c"] = te.alloc_register(name="pe_c")
            R["pe_psp"] = te.alloc_register(name="pe_psp")
            te.reg_mov(R["pe_rb"], 0)
            R["pe_tr"] = [te.value_load(icon[0:1, i:i + 1]) for i in range(4)]
        P.add("tensor", a5_init_pe)

        def a5_init_ac(sc):
            R["ac_ra"] = sc.alloc_register(name="ac_ra")
            R["ac_t1"] = sc.alloc_register(name="ac_t1")
            R["ac_c"] = sc.alloc_register(name="ac_c")
            R["ac_psp"] = sc.alloc_register(name="ac_psp")
            sc.reg_mov(R["ac_ra"], 0)
            R["ac_tr"] = [sc.value_load(icon[0:1, i:i + 1]) for i in range(4)]
        P.add("scalar", a5_init_ac)

        def dv_mega(v):
            rj = v.alloc_register(name="m_rj")
            rp = v.alloc_register(name="m_rp")
            rs = v.alloc_register(name="m_rs")
            bl = v.alloc_register(name="m_bl")
            bk1 = v.alloc_register(name="m_bk1")
            ibc = v.alloc_register(name="m_ibc")
            hpar = v.alloc_register(name="m_hp")
            t1 = v.alloc_register(name="m_t1")
            cr = v.alloc_register(name="m_cr")
            sp = v.alloc_register(name="m_sp")
            trips = v.value_load(icon[0:1, 6:7])
            v.reg_mov(rs, 4)
            v.reg_mov(bl, 4)
            v.reg_mov(bk1, 0)
            v.reg_mov(ibc, 3)
            v.reg_mov(hpar, 1)
            v.reg_mov(sp, 0)
            with dynloop(nc, v, cr, trips):
                with v.If_eq(bl, 4):
                    with v.If_ne(bk1, 0):
                        v.reg_add(t1, rs, -4)
                        v.wait_ge(amm.h, t1)
                        for dt in range(3):
                            v.tensor_tensor(
                                SB4[dt][:, :], SB4[dt][:, :],
                                A(O_MASK + dt * 512, O_MASK + (dt + 1) * 512),
                                ALU.add)
                        v.drain()
                        v.tensor_tensor(
                            SB4[3][:, :], SB4[3][:, :],
                            A(O_MASK + 3 * 512, O_MASK + 4 * 512),
                            ALU.add).then_inc(adm.h, 4)
                        v.reg_add(t1, rs, -4)
                        v.wait_ge(aen.h, t1)
                        v.tensor_copy(densb[:, :], psY1[0:1, :])
                        v.drain()
                        v.reciprocal(recsb[:, :], densb[:, :]).then_inc(aep.h, 1)
                        v.wait_ge(abc.h, bk1)
                        v.tensor_copy(recb[:, :], psY2[:, :])
                        v.drain()
                        v.reg_mul(t1, bk1, 16)
                        v.reg_add(t1, t1, -16)
                        v.wait_ge(aos.h, t1)
                        v.tensor_tensor(ysb[:, 0:512], psY0[:, :], recb[:, :],
                                        ALU.mult).then_inc(aev.h, 1)
                    v.reg_add(ibc, ibc, 1)
                    with v.If_eq(ibc, 4):
                        v.reg_mov(ibc, 0)
                        v.reg_mov(t1, 1)
                        v.reg_sub(hpar, t1, hpar)
                    v.reg_add(t1, ibc, 8)
                    v.tensor_copy(mapi[0:1, 19:20],
                                  bass.AP(icon, t1, [[16, 1], [1, 1]]))
                    v.drain()
                    nj_sv = v.value_load(mapi[0:1, 19:20])
                    v.reg_add(bl, nj_sv, 4)
                    v.reg_mul(t1, hpar, 8192)
                    v.reg_add(rj, t1, 0)
                    v.reg_mul(rp, hpar, 64)
                    v.reg_mov(sp, 0)
                    v.reg_add(bk1, bk1, 1)
                    v.reg_add(t1, bk1, 3)
                    v.reg_alu(t1, t1, 2, op=ALU.logical_shift_right)
                    v.wait_ge(akv.h, t1)
                with v.Else():
                    v.reg_add(t1, rs, -8)
                    v.reg_alu(t1, t1, 0, op=ALU.max)
                    v.wait_ge(aen.h, t1)
                    with v.If_eq(sp, 0):
                        v.tensor_copy(pcstg[:, 0:4],
                                      bass.AP(poscol, rp, [[128, 128], [1, 4]]))
                        v.drain()
                        v.tensor_copy(
                            arena[:, O_KSTG: O_KSTG + 1024]
                            .rearrange("p (s kv d) -> p s kv d", s=4, kv=2, d=128),
                            bass.AP(arena, rj,
                                    [[73728, 128], [128, 4], [16384, 2], [1, 128]])
                        ).then_inc(asg.h, 4)
                        v.reg_mov(sp, 1)
                    with v.Else():
                        v.tensor_copy(pcstg[:, 4:8],
                                      bass.AP(poscol, rp, [[128, 128], [1, 4]]))
                        v.drain()
                        v.tensor_copy(
                            arena[:, O_KSTG + 1024: O_KSTG + 2048]
                            .rearrange("p (s kv d) -> p s kv d", s=4, kv=2, d=128),
                            bass.AP(arena, rj,
                                    [[73728, 128], [128, 4], [16384, 2], [1, 128]])
                        ).then_inc(asg.h, 4)
                        v.reg_mov(sp, 0)
                    v.reg_add(rj, rj, 512)
                    v.reg_add(rp, rp, 4)
                    v.reg_add(rs, rs, 4)
                    v.reg_add(bl, bl, -4)
        P.add("vector", dv_mega) if A5PARTS[0] else None
        aep.inc(64)
        aev.inc(64)

        mm_tp_prev = 0
        for h in range(16):
            hb = h % 2
            wh = [(ahd, h - 1), (cc, cc_kvy), (st, st_pos)]
            if h < 2:
                wh.append((mm, mm_a2_end))
            for c in range(8):
                rl, slt = min(c, 7 - c), (0 if c < 4 else 1)
                dma(P, "sync", A(O_KH[hb] + c * 1024, O_KH[hb] + (c + 1) * 1024),
                    kvy_ag_out[rl * 4224 + h * 128: rl * 4224 + (h + 1) * 128,
                               slt * 1024:(slt + 1) * 1024], ld,
                    waits=wh if c == 0 else ())
                voff = (rl * 4224 + 2048) * 2048 + h * 262144 + slt * 8 * 128
                dma(P, "sync", A(O_VH[hb] + c * 1024, O_VH[hb] + (c + 1) * 1024),
                    bass.AP(kvy_ag_out, voff, [[2048, 128], [128, 8], [1, 128]]), ld)
            dma(P, "sync", pc64[:, :],
                bass.AP(pos_dram, h * 8192, [[128, 64], [1, 128]]), ldp,
                waits=[(st, st_pos)] + ([(mm, mm_tp_prev)] if h else []))
            if h == 0:
                for rl in range(4):
                    dma(P, "sync", arena[rl * 2:(rl + 1) * 2, O_KLR:O_KLR + 2048],
                        kvy_ag_out[rl * 4224 + 4112: rl * 4224 + 4114, :], ld)
            ld_h = ld.n

            dma(P, "sync", arena[0:1, O_MNEGB + hb * 2048: O_MNEGB + (hb + 1) * 2048],
                mneg_dram[h:h + 1, :], ldm,
                waits=[(st, st_mneg), (ahd, h - 1)])

            mm.inc(2)

            def pe_tp(te, h=h, hb=hb, tgt=mm.n, ldh=ld_h, ldp_n=ldp.n,
                      w2=([(ac, ac_a2_end)] if h < 2 else [])):
                _w(te, [(ld, ldh), (ldp, ldp_n), (ahd, h - 1), (aac, 136 * h)] + w2)
                te.transpose(psA[:, 0:64], pc64[:, :], ident[0:64, 0:64]
                             ).then_inc(mm.h, 1)
                te.transpose(tps[:, 0:8],
                             arena[0:8, O_KLR + h * 128: O_KLR + (h + 1) * 128],
                             identb[0:8, 0:8]).then_inc(mm.h, 1)
            P.add("tensor", pe_tp) if A5PARTS[4] else None
            mm_tp_h = mm.n
            mm_tp_prev = mm_tp_h

            akv.inc(1)

            def ac_prep(sc, h=h, hb=hb, mmh=mm_tp_h, ldh=ld_h):
                _w(sc, [(mm, mmh), (ld, ldh)])
                sc.activation(poscol[:, hb * 64:(hb + 1) * 64], psA[:, 0:64],
                              AF.Identity)
                sc.activation(arena[:, O_KLT + hb * 8: O_KLT + (hb + 1) * 8],
                              tps[:, 0:8], AF.Identity)
                r = None
                for c in range(1, 8):
                    pc_ = c - 1
                    idx = min(pc_, 7 - pc_) * 2 + (0 if pc_ < 4 else 1)
                    col = A(O_KH[hb] + c * 1024, O_KH[hb] + c * 1024 + 1)
                    r = sc.activation(
                        col,
                        arena[:, O_KLT + hb * 8 + idx: O_KLT + hb * 8 + idx + 1],
                        AF.Identity, bias=col, scale=smsig[:, h:h + 1])
                r.then_inc(akv.h, 1)
            P.add("scalar", ac_prep) if A5PARTS[5] else None

            for ib in range(4):
                g = h * 4 + ib
                dma(P, "sync",
                    A(O_QB + (g % 2) * 512, O_QB + (g % 2 + 1) * 512),
                    q_fm[h * 128:(h + 1) * 128, ib * 512:(ib + 1) * 512], ldq2,
                    waits=[(aib, g - 1), (stq[0], stq_all[0]),
                           (stq[1], stq_all[1]), (st, st_q)])
                ldq_n = ldq2.n
                qslice = A(O_QB + (g % 2) * 512, O_QB + (g % 2 + 1) * 512)
                mrow = arena[0:1, O_MNEGB + hb * 2048 + ib * 512:
                             O_MNEGB + hb * 2048 + (ib + 1) * 512]

                def pe_blk(te, h=h, hb=hb, ib=ib, g=g, ldq=ldq_n, qslice=qslice,
                           mrow=mrow, ldm_n=32 + 16 * (h + 1)):
                    rb, t1, psp = R["pe_rb"], R["pe_t1"], R["pe_psp"]
                    _w(te, [(ldq2, ldq), (ldm, ldm_n), (aev, g), (aep, g)])
                    te.matmul(psY0[:, :], onesb[0:1, :], zrowb[:, :],
                              start=True, stop=False)
                    te.matmul(psY1[0:1, :], onesb[0:1, 0:1], zrowb[:, :],
                              start=True, stop=False)
                    te.reg_add(t1, rb, 4)
                    te.wait_ge(asg.h, t1)
                    for s in range(4):
                        te.matmul(SB4[s][:, :], onesb[0:1, :], mrow,
                                  start=True, stop=False)
                        te.matmul(SB4[s][:, :], KSTG[s], qslice,
                                  start=False, stop=True).then_inc(amm.h, 1)
                    te.reg_mov(psp, 1)
                    with dynloop(nc, te, R["pe_c"], R["pe_tr"][ib]):
                        te.reg_add(t1, rb, 8)
                        te.wait_ge(asg.h, t1)
                        with te.If_eq(psp, 0):
                            for s in range(4):
                                te.reg_add(t1, rb, s + 1)
                                te.wait_ge(aac.h, t1)
                                te.matmul(SB4[s][:, :], onesb[0:1, :], mrow,
                                          start=True, stop=False)
                                te.matmul(SB4[s][:, :], KSTG[s], qslice,
                                          start=False, stop=True).then_inc(amm.h, 1)
                                te.matmul(psY0[:, :], VSTG[4 + s], PT[s],
                                          start=False, stop=False)
                                te.matmul(psY1[0:1, :], onescol[:, :], PT[s],
                                          start=False, stop=False
                                          ).then_inc(aen.h, 1)
                            te.reg_mov(psp, 1)
                        with te.Else():
                            for s in range(4):
                                te.reg_add(t1, rb, s + 1)
                                te.wait_ge(aac.h, t1)
                                te.matmul(SB4[s][:, :], onesb[0:1, :], mrow,
                                          start=True, stop=False)
                                te.matmul(SB4[s][:, :], KSTG[4 + s], qslice,
                                          start=False, stop=True).then_inc(amm.h, 1)
                                te.matmul(psY0[:, :], VSTG[s], PT[s],
                                          start=False, stop=False)
                                te.matmul(psY1[0:1, :], onescol[:, :], PT[s],
                                          start=False, stop=False
                                          ).then_inc(aen.h, 1)
                            te.reg_mov(psp, 0)
                        te.reg_add(rb, rb, 4)
                    with te.If_eq(psp, 0):
                        for s in range(4):
                            te.reg_add(t1, rb, s + 1)
                            te.wait_ge(aac.h, t1)
                            te.matmul(psY0[:, :], VSTG[4 + s], PT[s],
                                      start=False, stop=False)
                            te.matmul(psY1[0:1, :], onescol[:, :], PT[s],
                                      start=False, stop=False).then_inc(aen.h, 1)
                    with te.Else():
                        for s in range(4):
                            te.reg_add(t1, rb, s + 1)
                            te.wait_ge(aac.h, t1)
                            te.matmul(psY0[:, :], VSTG[s], PT[s],
                                      start=False, stop=False)
                            te.matmul(psY1[0:1, :], onescol[:, :], PT[s],
                                      start=False, stop=False).then_inc(aen.h, 1)
                    te.reg_add(rb, rb, 4)
                    te.wait_ge(aep.h, g + 1)
                    te.matmul(psY2[:, :], ones_f[:, :], recsb[:, :],
                              start=True, stop=True).then_inc(abc.h, 1)
                    te.nop().then_inc(aib.h, 1)
                P.add("tensor", pe_blk) if A5PARTS[1] else None
                abc.inc(1)
                aib.inc(1)

                def ac_blk(sc, h=h, ib=ib, g=g):
                    ra, t1, apsp = R["ac_ra"], R["ac_t1"], R["ac_psp"]
                    sc.reg_mov(apsp, 0)
                    with dynloop(nc, sc, R["ac_c"], R["ac_tr"][ib]):
                        sc.wait_ge(aen.h, ra)
                        with sc.If_eq(apsp, 0):
                            for s in range(4):
                                sc.reg_add(t1, ra, s + 1)
                                sc.wait_ge(amm.h, t1)
                                sc.activation(PT[s], SB4[s][:, :], AF.Exp,
                                              bias=pcstg[:, s:s + 1]
                                              ).then_inc(aac.h, 1)
                            sc.reg_mov(apsp, 1)
                        with sc.Else():
                            for s in range(4):
                                sc.reg_add(t1, ra, s + 1)
                                sc.wait_ge(amm.h, t1)
                                sc.activation(PT[s], SB4[s][:, :], AF.Exp,
                                              bias=pcstg[:, 4 + s:5 + s]
                                              ).then_inc(aac.h, 1)
                            sc.reg_mov(apsp, 0)
                        sc.reg_add(ra, ra, 4)
                    sc.wait_ge(aen.h, ra)
                    sc.wait_ge(adm.h, 4 * (g + 1))
                    with sc.If_eq(apsp, 0):
                        for s in range(4):
                            sc.activation(PT[s], SB4[s][:, :], AF.Exp,
                                          bias=pcstg[:, s:s + 1]
                                          ).then_inc(aac.h, 1)
                    with sc.Else():
                        for s in range(4):
                            sc.activation(PT[s], SB4[s][:, :], AF.Exp,
                                          bias=pcstg[:, 4 + s:5 + s]
                                          ).then_inc(aac.h, 1)
                    sc.reg_add(ra, ra, 4)
                P.add("scalar", ac_blk) if A5PARTS[2] else None

                aos.inc(16)

                def o_store(sy, h=h, ib=ib, g=g):
                    _w(sy, [(aev, g + 1)])
                    sy.dma_start(out=o_fm[h * 128:(h + 1) * 128,
                                          ib * 512:(ib + 1) * 512],
                                 in_=ysb[:, 0:512]).then_inc(aos.h, 16)
                P.add("sync", o_store) if A5PARTS[3] else None

            ahd.inc(1)

            def pe_hdone(te):
                te.nop().then_inc(ahd.h, 1)
            P.add("tensor", pe_hdone)
        ahd_all = ahd.n
        aos_all = aos.n

        akv.inc(1)

        def ac_pad(sc):
            sc.nop().then_inc(akv.h, 1)
        P.add("scalar", ac_pad)


        if upto < 6:
            return aos.n if False else st.n
        for de in range(16):
            dma(P, "gpsimd", A(O_WOUT + de * 1024, O_WOUT + (de + 1) * 1024),
                woutT_in[de * 128:(de + 1) * 128, :], wst,
                waits=[(ahd, ahd_all)] if de == 0 else ())
        dma(P, "sync", Af(O_OLNW, 1024), lnw4_in[:, 2048:3072], ldp,
            waits=[(ahd, ahd_all)])
        dma(P, "sync", Af(55296, 1024), lnw4_in[:, 3072:4096], ldp)
        wst_wo = wst.n
        ld_oln = ldp.n
        lxp = [S("lxA6a"), S("lxA6b")]

        lx_o = lx.n
        for hc in range(16):
            dma(P, "sync", ysb[:, 2048:4096],
                p_fm[hc * 128:(hc + 1) * 128, :], ld,
                waits=[(ahd, ahd_all), (aos, aos_all), (stq[0], stq_all[0]),
                       (stq[1], stq_all[1])] + ([(dv, dv.n)] if hc else []))
            dma(P, "scalar", lnt[:, 0:2048], o_fm[hc * 128:(hc + 1) * 128, :], lx,
                waits=[(dv, dv.n)] if hc else [])

            @op(P, "scalar", waits=[ld.now, lx.now], inc=ac)
            def _(s_):
                return s_.activation(ysb[:, 0:2048], ysb[:, 2048:4096], AF.Silu)

            @op(P, "vector", waits=[ac.now], inc=dv)
            def _(v, hc=hc):
                return v.tensor_mul(A(O_GFM + hc * 2048, O_GFM + (hc + 1) * 2048),
                                    ysb[:, 0:2048], lnt[:, 0:2048])
        dv_gfm = dv.n

        st_a6 = [None] * NT
        for tt in range(NT):
            for half in range(2):
                hp2 = psY2 if half == 0 else psY3

                @op(P, "tensor", waits=[(dv, dv_gfm), (wst, wst_wo)] +
                    ([(dv, dv.n)] if (tt or half) else []), inc=mm)
                def _(te, tt=tt, half=half, hp2=hp2):
                    r = None
                    for de in range(16):
                        r = te.matmul(
                            hp2[:, :],
                            A(O_GFM + de * 2048 + tt * 128,
                              O_GFM + de * 2048 + (tt + 1) * 128),
                            A(O_WOUT + de * 1024 + half * 512,
                              O_WOUT + de * 1024 + (half + 1) * 512),
                            start=(de == 0), stop=(de == 15))
                    return r

            @op(P, "vector", waits=[mm.now], inc=dv)
            def _(v):
                v.tensor_copy(Af(O_OSB, 1024)[:, 0:512], psY2[:, :])
                return v.tensor_copy(Af(O_OSB, 1024)[:, 512:1024], psY3[:, :])

            ln_stats(Af(O_OSB, 1024))

            @op(P, "scalar", waits=[dv.now], inc=ac)
            def _(s_):
                return s_.activation(sqt[:, :], Af(O_OSB, 1024), AF.Identity,
                                     bias=stat[:, 9:10], scale=stat[:, 7:8])

            wxx = [(st, st_a6[tt - 2])] if tt >= 2 and st_a6[tt - 2] else []
            dma(P, "scalar", xt[:, (tt % 2) * D:(tt % 2 + 1) * D],
                x_res[tt * 128:(tt + 1) * 128, :], lxp[tt % 2], waits=wxx)

            @op(P, "vector", waits=[ac.now, lxp[tt % 2].now, (ldp, ld_oln)] +
                ([(st, st_a6[tt - 1])] if tt >= 1 and st_a6[tt - 1] else []), inc=dv)
            def _(v, tt=tt):
                v.tensor_mul(accv[:, :], sqt[:, :], Af(O_OLNW, 1024))
                v.tensor_add(accv[:, :], accv[:, :], Af(55296, 1024))
                return v.tensor_add(accv[:, :], accv[:, :],
                                    xt[:, (tt % 2) * D:(tt % 2 + 1) * D])

            dma(P, "sync", dst_dram[tt * 128:(tt + 1) * 128, :], accv[:, :], st,
                waits=[dv.now])
            st_a6[tt] = st.n
        return st_a6
    if phases == ("ffn1",):
        final_st = emit_ffn(x_my.ap(), y_my.ap(), first_call=True)[-1]
    elif phases == ("ffn1", "attn"):
        f1 = emit_ffn(x_my.ap(), x_res.ap(), first_call=True)
        final_st = emit_attention(x_res.ap(), y_my.ap(), f1, upto=UPTO[0])
        final_st = final_st[-1] if isinstance(final_st, list) else final_st
    elif phases == ("ffn1", "attn", "ffn2"):
        f1 = emit_ffn(x_my.ap(), x_res.ap(), first_call=True)
        a6 = emit_attention(x_res.ap(), x_res.ap(), f1)
        final_st = emit_ffn(x_res.ap(), y_my.ap(), first_call=False,
                            entry=[(st, a6[-1])],
                            entry_tiles=[[(st, a6[t])] for t in range(NT)])[-1]
    else:
        raise NotImplementedError(phases)

    if adbg:
        da_q = nc.dram_tensor("da_q", [2048, T_OWN], BF16, kind="ExternalOutput")
        da_p = nc.dram_tensor("da_p", [2048, T_OWN], BF16, kind="ExternalOutput")
        da_o = nc.dram_tensor("da_o", [2048, T_OWN], BF16, kind="ExternalOutput")
        da_kvy = nc.dram_tensor("da_kvy", [4 * 4224, 2048], BF16,
                                kind="ExternalOutput")
        da_pos = nc.dram_tensor("da_pos", [16, 8192], F32, kind="ExternalOutput")
        da_mneg = nc.dram_tensor("da_mneg", [16, 2048], BF16,
                                 kind="ExternalOutput")
        da_xr = nc.dram_tensor("da_xr", [T_OWN, D], F32, kind="ExternalOutput")
        dbg = S("dbg")
        dma(P, "scalar", da_q.ap(), q_fm.ap(), dbg, waits=[(st, final_st)])
        dma(P, "scalar", da_p.ap(), p_fm.ap(), dbg)
        dma(P, "scalar", da_o.ap(), o_fm.ap(), dbg)
        dma(P, "scalar", da_kvy.ap(), kvy_ag_out.ap(), dbg)
        dma(P, "scalar", da_pos.ap(), pos_dram.ap(), dbg)
        dma(P, "scalar", da_mneg.ap(), mneg_dram.ap(), dbg)
        dma(P, "scalar", da_xr.ap(), x_res.ap(), dbg)

        @op(P, "sync", waits=[dbg.now, (st, final_st)])
        def _(sy):
            pass
    elif debug:
        dbg = S("dbg")
        dma(P, "scalar", d_ln.ap(), ln_ag_out.ap(), dbg, waits=[(st, final_st)])
        dma(P, "scalar", d_logits.ap(), logits[:, :], dbg)
        dma(P, "scalar", d_posg.ap(), posg[:, :], dbg)
        dma(P, "scalar", d_cnt.ap(), cagb[:, :], dbg)
        dma(P, "scalar", d_offs.ap(), offs[:, :], dbg)
        dma(P, "scalar", d_map.ap(), rs_out.ap(), dbg)
        dma(P, "scalar", d_ybin.ap(), yb_ag_in.ap(), dbg)
        dma(P, "scalar", d_earr.ap(), earr[:, :], dbg)
        dma(P, "scalar", d_parr.ap(), parr[:, :], dbg)
        dma(P, "scalar", d_warr.ap(), warr[:, :], dbg)
        dma(P, "scalar", d_ggix.ap(), ggix[:, :], dbg)
        dma(P, "scalar", d_postm.ap(), postm[:, :], dbg)
        dma(P, "scalar", d_buf.ap(), arena[:, O_BUFFM:O_BUFFM + 8 * SLOT_OWN], dbg)
        dma(P, "scalar", d_hfm.ap(), arena[:, O_HFM:O_HFM + 32 * 512], dbg)
        dma(P, "scalar", d_w1.ap(), arena[:, O_W1:O_W1 + 8 * HID], dbg)
        final_st = None

        @op(P, "sync", waits=[dbg.now])
        def _(sy):
            pass
    else:
        @op(P, "sync", waits=[(st, final_st)])
        def _(sy):
            pass

    bb = nc.Block()
    blk = bb.__enter__()
    ctxs.append(bb)
    for name in ENGINES:
        fns = P.ops[name]
        getattr(blk, name)(lambda e, fns=fns: [f(e) for f in fns])

    for c in reversed(ctxs):
        c.__exit__(None, None, None)
    return nc


def _stage_inputs(inputs):
    x = np.asarray(inputs["L0_attn_x"])
    wr = np.ascontiguousarray(np.asarray(inputs["L0_ffn_Wr"]), dtype=np.float32)
    w_in = np.asarray(inputs["L0_attn_W_in"], dtype=np.float32)
    b_in = np.asarray(inputs["L0_attn_b_in"], dtype=np.float32)
    w_out = np.asarray(inputs["L0_attn_W_out"], dtype=np.float32)
    ilw = np.asarray(inputs["L0_attn_in_ln_w"], dtype=np.float32)
    ilb = np.asarray(inputs["L0_attn_in_ln_b"], dtype=np.float32)
    olw = np.asarray(inputs["L0_attn_out_ln_w"], dtype=np.float32)
    olb = np.asarray(inputs["L0_attn_out_ln_b"], dtype=np.float32)
    smear = np.asarray(inputs["L0_attn_smear_factor"], dtype=np.float32)
    lsc = np.asarray(inputs["L0_attn_log_scale"], dtype=np.float32)

    w_in_pad = np.zeros((8320, 1024), np.float32)
    w_in_pad[:8208] = w_in
    b_in_pad = np.zeros(8320, np.float32)
    b_in_pad[:8208] = b_in
    wblk = np.ascontiguousarray(
        w_in_pad.reshape(65, 128, 8, 128).transpose(0, 3, 2, 1).reshape(65, 128, 1024))
    wv = np.ascontiguousarray(
        w_in_pad[4096:6144].T.reshape(8, 128, 2048))
    binc = np.ascontiguousarray(b_in_pad.reshape(65, 128).T)
    binv = np.ascontiguousarray(b_in_pad[4096:6144].reshape(1, 2048))
    woutT = np.ascontiguousarray(w_out.T)
    lnw4 = np.ascontiguousarray(np.broadcast_to(
        np.concatenate([ilw, ilb, olw, olb]), (128, 4096)))
    hpar = np.ascontiguousarray(np.broadcast_to(
        np.concatenate([smear, lsc]), (128, 32)))
    masks = np.zeros((128, 4, 512), np.float32)
    jj = np.arange(128)[:, None]
    ii = np.arange(512)[None, :]
    for k in range(4):
        masks[:, k, :] = np.where(128 * k + jj > ii, -1e10, 0.0)
    masks = np.ascontiguousarray(masks.reshape(128, 2048))
    w1 = np.asarray(inputs["L0_ffn_W1"])
    w2 = np.asarray(inputs["L0_ffn_W2"])
    b1 = np.asarray(inputs["L0_ffn_b1"])
    b2 = np.asarray(inputs["L0_ffn_b2"])
    br = np.asarray(inputs["L0_ffn_br"])

    ident = np.eye(128, dtype=np.float32)
    iota8 = np.broadcast_to(np.arange(E, dtype=np.float32), (128, E)).copy()
    brb = np.broadcast_to(br.astype(np.float32), (128, E)).copy()

    in_maps = []
    for r in range(NCORE):
        b = r // 4
        cA, cB = r % 4, 7 - (r % 4)
        x_mine = np.ascontiguousarray(
            np.concatenate([x[b, cA * 1024:(cA + 1) * 1024],
                            x[b, cB * 1024:(cB + 1) * 1024]], 0), dtype=np.float32)
        sel = np.zeros((16, 2), np.float32)
        for c in range(8):
            rr, off = (b * 4 + min(c, 7 - c)), (0 if c < 4 else 1)
            row = 2 * rr + off
            if c < cA:
                sel[row, 0] = 1.0
            if cA < c < cB:
                sel[row, 1] = 1.0
        consts = np.zeros((128, 4), np.float32)
        consts[:, 0] = b * CAP
        consts[:, 1] = r * T_OWN + 1
        b1c = np.ascontiguousarray(b1[r].reshape(32, 128).T, dtype=np.float32)
        icon = np.zeros((1, 16), np.int32)
        gpos = [cA * 1024, cA * 1024 + 512, cB * 1024, cB * 1024 + 512]
        for i in range(4):
            icon[0, i] = gpos[i] // 512
            icon[0, 8 + i] = gpos[i] // 128 + 4
        icon[0, 4] = cA * 1024
        icon[0, 5] = cB * 1024
        icon[0, 6] = 16 * (sum(gp_ // 128 + 4 for gp_ in gpos) // 4) + 65
        in_maps.append({
            "x_my": x_mine,
            "wr_in": wr,
            "w1_in": np.ascontiguousarray(w1[r], dtype=np.float32),
            "w2_in": np.ascontiguousarray(w2[r], dtype=np.float32),
            "b1c_in": b1c,
            "b2r_in": np.ascontiguousarray(b2[r].reshape(1, D), dtype=np.float32),
            "brb_in": brb,
            "ident_in": ident,
            "iota8_in": iota8,
            "selpref_in": sel,
            "consts_in": consts,
            "wblk_in": wblk,
            "wv_in": wv,
            "binc_in": binc,
            "binv_in": binv,
            "woutT_in": woutT,
            "lnw4_in": lnw4,
            "hpar_in": hpar,
            "masks_in": masks,
            "icon_in": icon,
        })
    return in_maps


def _gather_output(results):
    out = np.zeros((2, 8192, D), np.float32)
    for r in range(NCORE):
        b = r // 4
        cA, cB = r % 4, 7 - (r % 4)
        y = results[r]["y_my"]
        out[b, cA * 1024:(cA + 1) * 1024] = y[0:1024]
        out[b, cB * 1024:(cB + 1) * 1024] = y[1024:2048]
    return out


def kernel(**inputs):
    nc = build_program(phases=("ffn1", "attn", "ffn2"))
    in_maps = _stage_inputs(inputs)
    res = run_bass_kernel_spmd(nc, in_maps, core_ids=list(range(NCORE)))
    return _gather_output(res.results)


def kernel_debug(**inputs):
    nc = build_program(phases=("ffn1",), debug=True)
    in_maps = _stage_inputs(inputs)
    res = run_bass_kernel_spmd(nc, in_maps, core_ids=list(range(NCORE)))
    return res.results


# version 08
# speedup vs baseline: 1.1524x; 1.0178x over previous
import numpy as np
import ml_dtypes
import concourse.bass as bass
import concourse.mybir as mybir
from concourse.bass_utils import run_bass_kernel_spmd

F32 = mybir.dt.float32
BF16 = mybir.dt.bfloat16
I32 = mybir.dt.int32
AF = mybir.ActivationFunctionType
ALU = mybir.AluOpType
AX = mybir.AxisListType
IOA = bass.IndirectOffsetOnAxis

NCORE = 8
D = 1024
T_OWN = 2048
NT = 16
E = 8
CAP = 1280
NSLOT = E * 2 * CAP
SLOT_OWN = 2 * CAP
HID = 4096
LN_EPS = 1e-5

ENGINES = ("sync", "gpsimd", "scalar", "vector", "tensor")
UPTO = [99]
A5PARTS = [True, True, True, True, True, True, True]

from contextlib import contextmanager
from concourse.ordered_engine_set import OrderedEngineSet

_DYN_ID = [0]


@contextmanager
def dynloop(nc, eng, creg, end_sv):
    _DYN_ID[0] += 1
    lname = f"dyn{_DYN_ID[0]}_loop"
    ename = f"dyn{_DYN_ID[0]}_end"
    engset = OrderedEngineSet([eng.engine])
    eng.reg_mov(creg, 0)
    eng.br_lt(creg, end_sv, on_true=lname, on_false=ename)
    with nc.body(lname, valid_engines=engset):
        yield
        eng.reg_add(creg, creg, 1)
        eng.br_lt(creg, end_sv, on_true=lname, on_false=ename)
    nc.switch_bb(ename)



class Sem:
    def __init__(self, h):
        self.h = h
        self.n = 0

    def inc(self, k=1):
        self.n += k
        return self.n

    @property
    def now(self):
        return (self, self.n)


class Prog:
    def __init__(self):
        self.ops = {e: [] for e in ENGINES}

    def add(self, eng, fn):
        self.ops[eng].append(fn)


def _w(e, waits):
    for s, v in waits:
        if v > 0:
            e.wait_ge(s.h, v)


_DVE_COMPUTE = {
    "tensor_copy", "tensor_tensor", "tensor_add", "tensor_sub", "tensor_mul",
    "tensor_max", "tensor_scalar", "tensor_single_scalar", "tensor_scalar_add",
    "tensor_scalar_sub", "tensor_scalar_mul", "tensor_scalar_max",
    "tensor_scalar_min", "tensor_relu", "tensor_reduce", "reduce_max",
    "reduce_sum", "scalar_tensor_tensor", "tensor_tensor_scan", "memset",
    "memzero", "reciprocal", "max", "max_index", "select", "copy_predicated",
    "bn_stats", "bn_aggr", "transpose", "match_replace",
}


class AutoDrain:

    def __init__(self, v):
        object.__setattr__(self, "_v", v)
        object.__setattr__(self, "_n", 0)

    def __getattr__(self, name):
        v = object.__getattribute__(self, "_v")
        fn = getattr(v, name)
        if name in _DVE_COMPUTE:
            def call(*a, **k):
                n = object.__getattribute__(self, "_n")
                if n > 0:
                    v.drain()
                object.__setattr__(self, "_n", n + 1)
                return fn(*a, **k)
            return call
        return fn


def op(P, eng, waits=(), inc=None, inck=1):
    if inc is not None:
        inc.inc(inck)

    def deco(fn):
        def run(e):
            if eng == "vector":
                e = AutoDrain(e)
            _w(e, waits)
            r = fn(e)
            if inc is not None:
                r.then_inc(inc.h, inck)
        P.add(eng, run)
        return None
    return deco


def dma(P, eng, out, in_, sem, waits=(), **kw):
    sem.inc(16)

    def run(e):
        _w(e, waits)
        e.dma_start(out=out, in_=in_, **kw).then_inc(sem.h, 16)
    P.add(eng, run)


def build_program(phases=("ffn1",), debug=False, adbg=False):
    nc = bass.Bass("TRN2", target_bir_lowering=False)

    x_my = nc.dram_tensor("x_my", [T_OWN, D], F32, kind="ExternalInput")
    wr_in = nc.dram_tensor("wr_in", [D, E], F32, kind="ExternalInput")
    w1_in = nc.dram_tensor("w1_in", [D, HID], F32, kind="ExternalInput")
    w2_in = nc.dram_tensor("w2_in", [HID, D], F32, kind="ExternalInput")
    b1c_in = nc.dram_tensor("b1c_in", [128, 32], F32, kind="ExternalInput")
    b2r_in = nc.dram_tensor("b2r_in", [1, D], F32, kind="ExternalInput")
    brb_in = nc.dram_tensor("brb_in", [128, E], F32, kind="ExternalInput")
    ident_in = nc.dram_tensor("ident_in", [128, 128], F32, kind="ExternalInput")
    iota8_in = nc.dram_tensor("iota8_in", [128, E], F32, kind="ExternalInput")
    selpref_in = nc.dram_tensor("selpref_in", [16, 2], F32, kind="ExternalInput")
    consts_in = nc.dram_tensor("consts_in", [128, 4], F32, kind="ExternalInput")
    wblk_in = nc.dram_tensor("wblk_in", [65, 128, 1024], F32, kind="ExternalInput")
    wv_in = nc.dram_tensor("wv_in", [8, 128, 2048], F32, kind="ExternalInput")
    binc_in = nc.dram_tensor("binc_in", [128, 65], F32, kind="ExternalInput")
    binv_in = nc.dram_tensor("binv_in", [1, 2048], F32, kind="ExternalInput")
    woutT_in = nc.dram_tensor("woutT_in", [2048, 1024], F32, kind="ExternalInput")
    lnw4_in = nc.dram_tensor("lnw4_in", [128, 4 * 1024], F32, kind="ExternalInput")
    hpar_in = nc.dram_tensor("hpar_in", [128, 32], F32, kind="ExternalInput")
    masks_in = nc.dram_tensor("masks_in", [128, 4 * 512], F32, kind="ExternalInput")
    icon_in = nc.dram_tensor("icon_in", [1, 16], I32, kind="ExternalInput")
    y_my = nc.dram_tensor("y_my", [T_OWN, D], F32, kind="ExternalOutput")
    if debug:
        d_ln = nc.dram_tensor("d_ln", [NCORE * T_OWN, D], BF16, kind="ExternalOutput")
        d_logits = nc.dram_tensor("d_logits", [128, NT * E], F32, kind="ExternalOutput")
        d_posg = nc.dram_tensor("d_posg", [E, T_OWN], F32, kind="ExternalOutput")
        d_cnt = nc.dram_tensor("d_cnt", [16, E], F32, kind="ExternalOutput")
        d_offs = nc.dram_tensor("d_offs", [E, 4], F32, kind="ExternalOutput")
        d_map = nc.dram_tensor("d_map", [SLOT_OWN, 1], F32, kind="ExternalOutput")
        d_ybin = nc.dram_tensor("d_ybin", [SLOT_OWN, D], BF16, kind="ExternalOutput")
        d_earr = nc.dram_tensor("d_earr", [128, 2 * NT], F32, kind="ExternalOutput")
        d_parr = nc.dram_tensor("d_parr", [128, 2 * NT], F32, kind="ExternalOutput")
        d_warr = nc.dram_tensor("d_warr", [128, 2 * NT], F32, kind="ExternalOutput")
        d_ggix = nc.dram_tensor("d_ggix", [128, 2 * NT], I32, kind="ExternalOutput")
        d_postm = nc.dram_tensor("d_postm", [128, NT * E], F32, kind="ExternalOutput")
        d_buf = nc.dram_tensor("d_buf", [128, 8 * SLOT_OWN], BF16, kind="ExternalOutput")
        d_hfm = nc.dram_tensor("d_hfm", [128, 32 * 512], BF16, kind="ExternalOutput")
        d_w1 = nc.dram_tensor("d_w1", [128, 8 * HID], BF16, kind="ExternalOutput")

    ln_ag_in = nc.dram_tensor("ln_ag_in", [T_OWN, D], BF16)
    ln_ag_out = nc.dram_tensor("ln_ag_out", [NCORE * T_OWN, D], BF16)
    cnt_ag_in = nc.dram_tensor("cnt_ag_in", [2, E], F32)
    cnt_ag_out = nc.dram_tensor("cnt_ag_out", [16, E], F32)
    rs_in = nc.dram_tensor("rs_in", [NSLOT, 1], F32)
    rs_out = nc.dram_tensor("rs_out", [SLOT_OWN, 1], F32)
    yb_ag_in = nc.dram_tensor("yb_ag_in", [SLOT_OWN, D], BF16)
    yb_ag_out = nc.dram_tensor("yb_ag_out", [NSLOT, D], BF16)
    w2bf = nc.dram_tensor("w2bf", [HID, D], BF16)
    x_res = nc.dram_tensor("x_res", [T_OWN, D], F32)
    kvy_ag_in = nc.dram_tensor("kvy_ag_in", [4224, 2048], BF16)
    kvy_ag_out = nc.dram_tensor("kvy_ag_out", [4 * 4224, 2048], BF16)
    q_fm = nc.dram_tensor("q_fm", [2048, T_OWN], BF16)
    p_fm = nc.dram_tensor("p_fm", [2048, T_OWN], BF16)
    o_fm = nc.dram_tensor("o_fm", [2048, T_OWN], BF16)
    pos_dram = nc.dram_tensor("pos_dram", [16, 8192], F32)
    mneg_dram = nc.dram_tensor("mneg_dram", [16, 2048], BF16)

    ctxs = []

    def S(name):
        h = nc.semaphore(name)
        ctxs.append(h)
        return Sem(h.__enter__())

    def SB(name, shape, dt):
        h = nc.sbuf_tensor(name, shape, dt)
        ctxs.append(h)
        return h.__enter__()

    def PSUM(name, shape, dt):
        h = nc.psum_tensor(name, shape, dt)
        ctxs.append(h)
        return h.__enter__()

    ident = SB("ident", [128, 128], F32)
    identb = SB("identb", [128, 128], BF16)
    iota8 = SB("iota8", [128, E], F32)
    brb = SB("brb", [128, E], F32)
    b1c = SB("b1c", [128, 32], F32)
    b2r = SB("b2r", [1, D], F32)
    selpref = SB("selpref", [16, 2], F32)
    cconst = SB("cconst", [128, 4], F32)
    ones_f = SB("ones_f", [1, 128], F32)
    epsc = SB("epsc", [128, 1], F32)
    wrb = SB("wrb", [128, 8 * E], BF16)
    rowval = SB("rowval", [128, NT], F32)
    zed = SB("zed", [128, 160], F32)
    xt = SB("xt", [128, 2 * D], F32)
    sqt = SB("sqt", [128, D], F32)
    lnt = SB("lnt", [128, 2 * D], BF16)
    stat = SB("stat", [128, 16], F32)
    small = SB("small", [128, 16], F32)
    logits = SB("logits", [128, NT * E], F32)
    parr = SB("parr", [128, 2 * NT], F32)
    warr = SB("warr", [128, 2 * NT], F32)
    earr = SB("earr", [128, 2 * NT], F32)
    ohs1 = SB("ohs1", [128, NT * E], F32)
    ohsb = SB("ohsb", [128, E], BF16)
    cagb = SB("cagb", [16, E], F32)
    offs = SB("offs", [E, 4], F32)
    cnt_sb = SB("cnt_sb", [2, E], F32)
    postm = SB("postm", [128, NT * E], F32)
    gidx = SB("gidx", [128, 2 * NT], I32)
    ggix = SB("ggix", [128, 2 * NT], I32)
    mapcol = SB("mapcol", [128, 20], F32)
    mapi = SB("mapi", [128, 20], I32)
    w2sb = SB("w2sb", [128, 2 * 512], BF16)
    ysb = SB("ysb", [128, 4 * D], BF16)
    ygb = SB("ygb", [128, 2 * D], BF16)
    accv = SB("accv", [128, D], F32)
    binc = SB("binc", [128, 65], F32)
    hpar = SB("hpar", [128, 32], F32)
    qscale = SB("qscale", [128, 16], F32)
    smsig = SB("smsig", [128, 16], F32)
    binqs = SB("binqs", [128, 16], F32)
    icon = SB("icon", [1, 16], I32)
    onescol = SB("onescol", [128, 1], BF16)
    zrow = SB("zrow", [1, 512], F32)
    onesb = SB("onesb", [1, 128], BF16)
    zrowb = SB("zrowb", [1, 512], BF16)
    carry = SB("carry", [16, 2], F32)
    poscol = SB("poscol", [128, 128], F32)
    pc64 = SB("pc64", [64, 128], F32)
    pcstg = SB("pcstg", [128, 8], F32)
    densb = SB("densb", [1, 512], F32)
    recsb = SB("recsb", [1, 512], F32)
    recb = SB("recb", [128, 512], F32)

    arena = SB("arena", [128, 73728], BF16)
    O_LNFM, O_BUFFM, O_W1, O_HFM, O_BUFTM = 0, 0, 20480, 53248, 69632
    ohs_fm = arena[0:E, 53248:55296]
    cinc = arena[0:E, 55296:59392].bitcast(F32)
    posg = arena[0:E, 59392:63488].bitcast(F32)

    tps = PSUM("tps", [128, 128], BF16)
    psA = PSUM("psA", [128, 512], F32)
    psB = PSUM("psB", [128, 512], F32)
    psY = [PSUM(f"psY{i}", [128, 512], F32) for i in range(4)]
    psY0, psY1, psY2, psY3 = psY

    ld = S("ld")
    lx = S("lx")
    gp = S("gp")
    cc = S("cc")
    mm = S("mm")
    dv = S("dv")
    ac = S("ac")
    st = S("st")
    wst = S("wst")
    asg = S("asg")
    amm = S("amm")
    aac = S("aac")
    aen = S("aen")
    adm = S("adm")
    aep = S("aep")
    abc = S("abc")
    aev = S("aev")
    aib = S("aib")
    ahd = S("ahd")
    aos = S("aos")
    akv = S("akv")
    ldp = S("ldp")
    ldm = S("ldm")
    ldq2 = S("ldq2")
    lxx = S("lxx")
    wvs = S("wvs")

    P = Prog()
    regs = {}

    def _initregs(g):
        regs["bc_slot"] = g.to_reg(NSLOT - 1)
        regs["bc_row"] = g.to_reg(NCORE * T_OWN - 1)
    P.add("gpsimd", _initregs)

    dma(P, "sync", ident[:, :], ident_in.ap(), ld)
    dma(P, "sync", iota8[:, :], iota8_in.ap(), ld)
    dma(P, "sync", brb[:, :], brb_in.ap(), ld)
    dma(P, "sync", b1c[:, :], b1c_in.ap(), ld)
    dma(P, "sync", b2r[:, :], b2r_in.ap(), ld)
    dma(P, "sync", selpref[:, :], selpref_in.ap(), ld)
    dma(P, "sync", cconst[:, :], consts_in.ap(), ld)
    dma(P, "gpsimd", wrb[:, :].rearrange("p (c e) -> p c e", e=E),
        wr_in.ap().rearrange("(c p) e -> p c e", p=128), gp)

    @op(P, "gpsimd", inc=gp)
    def _(g):
        g.memset(ones_f[:, :], 1.0)
        g.memset(epsc[:, :], LN_EPS)
        g.memset(zed[:, :], 0.0)
        return g.iota(rowval[:, :], [[128, NT]], channel_multiplier=1,
                      allow_small_or_imprecise_dtypes=True)

    @op(P, "vector", waits=[ld.now, gp.now], inc=dv)
    def _(v):
        v.tensor_copy(identb[:, :], ident[:, :])
        return v.tensor_scalar_add(rowval[:, :], rowval[:, :], cconst[:, 1:2])
    dv_setup = dv.n

    def emit_ffn(src_dram, dst_dram, first_call, entry=(), entry_tiles=None):
        xtv = arena[:, 60416:60416 + 4096].bitcast(F32)
        lx0 = lx.n
        ac_ln = [None] * NT
        st_ln = [None] * NT
        mm_tp = [None] * NT
        for t in range(NT):
            s0 = (t % 2) * D
            sl = slice(s0, s0 + D)
            wx = ([(ac, ac_ln[t - 2])] if t >= 2 else []) + \
                (entry_tiles[t] if entry_tiles else
                 (list(entry) if t < 2 else []))
            dma(P, "scalar", xtv[:, sl], src_dram[t * 128:(t + 1) * 128, :],
                lx, waits=wx)

            @op(P, "vector", waits=[(lx, lx0 + 16 * (t + 1)), (dv, dv_setup)], inc=dv)
            def _(v, sl=sl):
                v.reduce_sum(stat[:, 0:1], xtv[:, sl], axis=AX.X)
                return v.tensor_scalar_mul(stat[:, 1:2], stat[:, 0:1], 1.0 / D)

            @op(P, "scalar", waits=[dv.now], inc=ac)
            def _(s, sl=sl):
                return s.activation(sqt[:, :], xtv[:, sl], AF.Square)

            @op(P, "vector", waits=[ac.now], inc=dv)
            def _(v):
                v.reduce_sum(stat[:, 2:3], sqt[:, :], axis=AX.X)
                v.tensor_scalar_mul(stat[:, 3:4], stat[:, 2:3], 1.0 / D)
                v.tensor_mul(stat[:, 4:5], stat[:, 1:2], stat[:, 1:2])
                return v.tensor_sub(stat[:, 5:6], stat[:, 3:4], stat[:, 4:5])

            @op(P, "scalar", waits=[dv.now], inc=ac)
            def _(s):
                return s.activation(stat[:, 6:7], stat[:, 5:6], AF.Sqrt, bias=epsc[:, :])

            @op(P, "vector", waits=[ac.now], inc=dv)
            def _(v):
                v.reciprocal(stat[:, 7:8], stat[:, 6:7])
                v.tensor_mul(stat[:, 8:9], stat[:, 1:2], stat[:, 7:8])
                return v.tensor_scalar_mul(stat[:, 9:10], stat[:, 8:9], -1.0)

            w = [dv.now]
            if t >= 2:
                w += [(st, st_ln[t - 2]), (mm, mm_tp[t - 2])]

            @op(P, "scalar", waits=w, inc=ac)
            def _(s, sl=sl):
                return s.activation(lnt[:, sl], xtv[:, sl], AF.Identity,
                                    bias=stat[:, 9:10], scale=stat[:, 7:8])
            ac_ln[t] = ac.n

            dma(P, "sync", ln_ag_in[t * 128:(t + 1) * 128, :], lnt[:, sl], st,
                waits=[ac.now])
            st_ln[t] = st.n

            for c in range(8):
                @op(P, "tensor", waits=[ac.now, dv.now], inc=mm)
                def _(te, s0=s0, c=c):
                    return te.transpose(
                        tps[:, :], lnt[:, s0 + c * 128: s0 + (c + 1) * 128],
                        identb[:, :])

                @op(P, "vector", waits=[mm.now], inc=dv)
                def _(v, t=t, c=c):
                    return v.tensor_copy(
                        arena[:, O_LNFM + c * T_OWN + t * 128:
                              O_LNFM + c * T_OWN + (t + 1) * 128], tps[:, :])
            mm_tp[t] = mm.n

            @op(P, "tensor", waits=[dv.now], inc=mm)
            def _(te, t=t):
                r = None
                for c in range(8):
                    r = te.matmul(
                        psA[:, 0:E],
                        arena[:, O_LNFM + c * T_OWN + t * 128:
                              O_LNFM + c * T_OWN + (t + 1) * 128],
                        wrb[:, c * E:(c + 1) * E], start=(c == 0), stop=(c == 7))
                return r

            @op(P, "vector", waits=[mm.now], inc=dv)
            def _(v, t=t):
                return v.tensor_add(logits[:, t * E:(t + 1) * E], psA[:, 0:E],
                                    brb[:, :])
        dv_logits_all = dv.n
        st_ln_all = st.n

        mm_ohs = [None] * NT
        for t in range(NT):
            tE = slice(t * E, (t + 1) * E)
            w = [(mm, mm_ohs[t - 1])] if t >= 1 else []

            @op(P, "vector", waits=w, inc=dv)
            def _(v, tE=tE):
                v.max(out=small[:, 0:8], in_=logits[:, tE])
                v.tensor_scalar_mul(small[:, 8:9], small[:, 0:1], -1.0)
                return v.tensor_sub(small[:, 9:10], small[:, 1:2], small[:, 0:1])

            @op(P, "scalar", waits=[dv.now], inc=ac)
            def _(s, tE=tE):
                s.activation(sqt[:, 0:E], logits[:, tE], AF.Exp, bias=small[:, 8:9])
                return s.activation(sqt[:, E:E + 1], small[:, 9:10], AF.Exp)

            @op(P, "vector", waits=[ac.now], inc=dv)
            def _(v, tE=tE, t=t):
                v.reduce_sum(small[:, 10:11], sqt[:, 0:E], axis=AX.X)
                v.reciprocal(parr[:, t:t + 1], small[:, 10:11])
                v.tensor_mul(parr[:, NT + t:NT + t + 1], sqt[:, E:E + 1],
                             parr[:, t:t + 1])
                v.tensor_scalar(small[:, 0:8], logits[:, tE], small[:, 0:1], None,
                                ALU.is_equal)
                v.tensor_scalar(small[:, 0:8], small[:, 0:8], -1.0, 1.0,
                                ALU.mult, ALU.add)
                v.scalar_tensor_tensor(small[:, 0:8], small[:, 0:8], 1e9,
                                       iota8[:, :], ALU.mult, ALU.add)
                v.tensor_reduce(earr[:, t:t + 1], small[:, 0:8], AX.X, ALU.min)
                v.tensor_scalar(ohs1[:, tE], iota8[:, :], earr[:, t:t + 1], None,
                                ALU.is_equal)
                v.max(out=small[:, 0:8], in_=logits[:, tE])
                v.tensor_scalar(small[:, 8:16], logits[:, tE], small[:, 1:2], None,
                                ALU.is_equal)
                v.tensor_scalar(small[:, 8:16], small[:, 8:16], -1.0, 1.0,
                                ALU.mult, ALU.add)
                v.scalar_tensor_tensor(small[:, 8:16], small[:, 8:16], 1e9,
                                       iota8[:, :], ALU.mult, ALU.add)
                v.scalar_tensor_tensor(small[:, 8:16], ohs1[:, tE], 1e9,
                                       small[:, 8:16], ALU.mult, ALU.add)
                v.tensor_reduce(earr[:, NT + t:NT + t + 1], small[:, 8:16],
                                AX.X, ALU.min)
                v.tensor_scalar(small[:, 0:8], iota8[:, :],
                                earr[:, NT + t:NT + t + 1], None, ALU.is_equal)
                v.tensor_add(ohs1[:, tE], ohs1[:, tE], small[:, 0:8])
                return v.tensor_copy(ohsb[:, :], ohs1[:, tE])

            @op(P, "tensor", waits=[dv.now], inc=mm)
            def _(te):
                return te.transpose(tps[0:E, 0:128], ohsb[:, :], identb[:, :])
            mm_ohs[t] = mm.n

            @op(P, "vector", waits=[mm.now], inc=dv)
            def _(v, t=t):
                return v.tensor_copy(ohs_fm[:, t * 128:(t + 1) * 128],
                                     tps[0:E, 0:128])

        @op(P, "vector", inc=dv)
        def _(v):
            v.tensor_tensor_scan(cinc[:, :], ohs_fm[:, :], ohs_fm[:, :], 0.0,
                                 ALU.add, ALU.bypass)
            v.tensor_copy(offs[:, 2:3], cinc[:, 1023:1024])
            return v.tensor_sub(offs[:, 3:4], cinc[:, 2047:2048],
                                cinc[:, 1023:1024])

        @op(P, "tensor", waits=[dv.now], inc=mm)
        def _(te):
            return te.transpose(psB[0:2, 0:E], offs[:, 2:4], ident[0:E, 0:E])

        @op(P, "vector", waits=[mm.now], inc=dv)
        def _(v):
            return v.tensor_copy(cnt_sb[:, :], psB[0:2, 0:E])

        dma(P, "sync", cnt_ag_in.ap(), cnt_sb[:, :], st, waits=[dv.now])
        st_cnt = st.n

        dma(P, "gpsimd", rs_in.ap().rearrange("(a b) o -> a (b o)", b=160),
            zed[:, :], gp)
        gp_zero = gp.n

        @op(P, "gpsimd", waits=[(st, st_cnt)], inc=cc)
        def _(g):
            return g.collective_compute(
                "AllGather", ALU.bypass, replica_groups=[list(range(NCORE))],
                ins=[cnt_ag_in.ap().opt()], outs=[cnt_ag_out.ap().opt()])
        cc_cnt = cc.n

        @op(P, "gpsimd", waits=[(st, st_ln_all)], inc=cc)
        def _(g):
            return g.collective_compute(
                "AllGather", ALU.bypass, replica_groups=[list(range(NCORE))],
                ins=[ln_ag_in.ap().opt()], outs=[ln_ag_out.ap().opt()])
        cc_lnag = cc.n

        dma(P, "sync", cagb[:, :], cnt_ag_out.ap(), ld, waits=[(cc, cc_cnt)])

        @op(P, "tensor", waits=[ld.now], inc=mm)
        def _(te):
            return te.matmul(psA[0:E, 0:2], cagb[:, :], selpref[:, :],
                             start=True, stop=True)

        @op(P, "vector", waits=[mm.now], inc=dv)
        def _(v):
            v.tensor_copy(offs[:, 0:2], psA[0:E, 0:2])
            v.tensor_add(offs[:, 1:2], offs[:, 0:1], offs[:, 1:2])
            v.tensor_sub(cinc[:, :], cinc[:, :], ohs_fm[:, :])
            v.tensor_scalar_add(posg[:, 0:1024], cinc[:, 0:1024], offs[:, 0:1])
            return v.tensor_scalar_add(posg[:, 1024:2048], cinc[:, 1024:2048],
                                       offs[:, 1:2])

        for t in range(NT):
            @op(P, "tensor", waits=[dv.now], inc=mm)
            def _(te, t=t):
                return te.transpose(psB[:, 0:E], posg[:, t * 128:(t + 1) * 128],
                                    ident[0:E, 0:E])

            @op(P, "vector", waits=[mm.now], inc=dv)
            def _(v, t=t):
                return v.tensor_copy(postm[:, t * E:(t + 1) * E], psB[:, 0:E])

        for t in range(NT):
            tE = slice(t * E, (t + 1) * E)

            @op(P, "vector", inc=dv)
            def _(v, tE=tE, t=t):
                r = None
                for k in range(2):
                    col = k * NT + t
                    ek = earr[:, col:col + 1]
                    v.tensor_scalar(small[:, 0:8], iota8[:, :], ek, None,
                                    ALU.is_equal)
                    v.tensor_mul(small[:, 0:8], small[:, 0:8], postm[:, tE])
                    v.reduce_sum(small[:, 8:9], small[:, 0:8], axis=AX.X)
                    v.tensor_scalar(small[:, 9:10], small[:, 8:9], 1280.0, None,
                                    ALU.is_ge)
                    v.tensor_scalar_min(small[:, 10:11], small[:, 8:9], 1279.0)
                    v.tensor_scalar(small[:, 11:12], ek, 2560.0, None, ALU.mult)
                    v.tensor_scalar_add(small[:, 11:12], small[:, 11:12],
                                        cconst[:, 0:1])
                    v.tensor_add(small[:, 11:12], small[:, 11:12], small[:, 10:11])
                    v.tensor_copy(ggix[:, col:col + 1], small[:, 11:12])
                    v.scalar_tensor_tensor(small[:, 12:13], small[:, 9:10], 1e9,
                                           small[:, 11:12], ALU.mult, ALU.add)
                    v.tensor_copy(gidx[:, col:col + 1], small[:, 12:13])
                    v.tensor_scalar(small[:, 13:14], small[:, 9:10], -0.5, 0.5,
                                    ALU.mult, ALU.add)
                    r = v.tensor_mul(warr[:, col:col + 1], parr[:, col:col + 1],
                                     small[:, 13:14])
                return r

            gp.inc(32)
            dv_t = dv.n

            def scat(g, t=t, dv_t=dv_t, gpz=gp_zero):
                _w(g, [(dv, dv_t), (gp, gpz)])
                for k in range(2):
                    col = k * NT + t
                    g.indirect_dma_start(
                        rs_in[:, :], IOA(gidx[:, col:col + 1], 0),
                        rowval[:, t:t + 1], None,
                        bounds_check=regs["bc_slot"], oob_is_err=False,
                    ).then_inc(gp.h, 16)
            P.add("gpsimd", scat)

        @op(P, "gpsimd", waits=[gp.now], inc=cc)
        def _(g):
            return g.collective_compute(
                "ReduceScatter", ALU.add, replica_groups=[list(range(NCORE))],
                ins=[rs_in.ap().opt()], outs=[rs_out.ap().opt()])
        cc_rs = cc.n

        for k in range(20):
            dma(P, "sync", mapcol[:, k:k + 1], rs_out[k * 128:(k + 1) * 128, :],
                ld, waits=[(cc, cc_rs)] if k == 0 else ())

        @op(P, "vector", waits=[ld.now], inc=dv)
        def _(v):
            v.tensor_scalar_add(mapcol[:, :], mapcol[:, :], -1.0)
            return v.tensor_copy(mapi[:, :], mapcol[:, :])
        dv_map = dv.n

        for c in range(8):
            dma(P, "gpsimd", arena[:, O_W1 + c * HID: O_W1 + (c + 1) * HID],
                w1_in[c * 128:(c + 1) * 128, :], wst,
                waits=list(entry) if c == 0 else ())
        wst_w1 = wst.n
        if first_call:
            dma(P, "gpsimd", w2bf.ap(), w2_in.ap(), wst)
        wst_w2 = wst.n

        mm_prev_tp = 0
        st_prev_y = st.n
        for s in range(5):
            @op(P, "gpsimd",
                waits=[(dv, dv_map)] + ([(mm, mm_prev_tp)] if s else []), inc=gp)
            def _(g):
                return g.memset(arena[:, O_BUFTM:O_BUFTM + 4 * D], 0.0)

            for j in range(4):
                gp.inc(16)

                def gath(g, s=s, j=j):
                    g.indirect_dma_start(
                        arena[:, O_BUFTM + j * D: O_BUFTM + (j + 1) * D], None,
                        ln_ag_out[:, :], IOA(mapi[:, 4 * s + j:4 * s + j + 1], 0),
                        bounds_check=regs["bc_row"], oob_is_err=False,
                    ).then_inc(gp.h, 16)
                P.add("gpsimd", gath)
            gp_gath = gp.n

            for j in range(4):
                for c in range(8):
                    @op(P, "tensor",
                        waits=[(gp, gp_gath), (dv, dv_logits_all), (dv, dv.n)],
                        inc=mm)
                    def _(te, j=j, c=c):
                        return te.transpose(
                            tps[:, :],
                            arena[:, O_BUFTM + j * D + c * 128:
                                  O_BUFTM + j * D + (c + 1) * 128],
                            identb[:, :])

                    @op(P, "vector", waits=[mm.now], inc=dv)
                    def _(v, s=s, j=j, c=c):
                        dst = arena[:, O_BUFFM + c * SLOT_OWN + s * 512 + j * 128:
                                    O_BUFFM + c * SLOT_OWN + s * 512 + (j + 1) * 128]
                        return v.tensor_copy(dst, tps[:, :])
            mm_prev_tp = mm.n
            dv_tp_end = dv.n

            ac_h0 = ac.n
            for hc in range(32):
                hp = psA if hc % 2 == 0 else psB
                w = [(dv, dv_tp_end), (wst, wst_w1)]
                if hc >= 2:
                    w.append((ac, ac_h0 + hc - 1))

                @op(P, "tensor", waits=w, inc=mm)
                def _(te, s=s, hc=hc, hp=hp):
                    r = None
                    for c in range(8):
                        r = te.matmul(
                            hp[:, :],
                            arena[:, O_W1 + c * HID + hc * 128:
                                  O_W1 + c * HID + (hc + 1) * 128],
                            arena[:, O_BUFFM + c * SLOT_OWN + s * 512:
                                  O_BUFFM + c * SLOT_OWN + (s + 1) * 512],
                            start=(c == 0), stop=(c == 7))
                    return r

                @op(P, "scalar", waits=[mm.now], inc=ac)
                def _(sc, hc=hc, hp=hp):
                    return sc.activation(
                        arena[:, O_HFM + hc * 512: O_HFM + (hc + 1) * 512],
                        hp[:, :], AF.Relu, bias=b1c[:, hc:hc + 1])
            ac_h_end = ac.n

            for half in range(2):
                wst_h0 = wst.n
                mm_y0 = mm.n
                for hc in range(32):
                    buf = (hc % 2) * 512
                    w = [(mm, mm_y0 + hc)] if hc >= 2 else [(wst, wst_w2), (mm, mm_y0)]
                    dma(P, "sync", w2sb[:, buf:buf + 512],
                        w2bf[hc * 128:(hc + 1) * 128, half * 512:(half + 1) * 512],
                        wst, waits=w)

                @op(P, "tensor",
                    waits=[(ac, ac_h_end), (dv, dv.n), (st, st_prev_y)], inc=mm)
                def _(te, half=half):
                    r = None
                    for st4 in range(4):
                        r = te.matmul(psY[st4][:, :], ones_f[:, :],
                                      b2r[:, half * 512:(half + 1) * 512],
                                      start=True, stop=False)
                    return r

                for hc in range(32):
                    buf = (hc % 2) * 512

                    @op(P, "tensor", waits=[(wst, wst_h0 + 16 * (hc + 1))], inc=mm)
                    def _(te, hc=hc, buf=buf):
                        r = None
                        for st4 in range(4):
                            r = te.matmul(
                                psY[st4][:, :],
                                arena[:, O_HFM + hc * 512 + st4 * 128:
                                      O_HFM + hc * 512 + (st4 + 1) * 128],
                                w2sb[:, buf:buf + 512],
                                start=False, stop=(hc == 31))
                        return r

                for st4 in range(4):
                    @op(P, "vector", waits=[mm.now], inc=dv)
                    def _(v, st4=st4, half=half):
                        return v.tensor_copy(
                            ysb[:, st4 * D + half * 512: st4 * D + (half + 1) * 512],
                            psY[st4][:, :])

            for st4 in range(4):
                dma(P, "sync",
                    yb_ag_in[s * 512 + st4 * 128: s * 512 + (st4 + 1) * 128, :],
                    ysb[:, st4 * D:(st4 + 1) * D], st, waits=[dv.now])
            st_prev_y = st.n
        st_y_end = st.n

        @op(P, "gpsimd", waits=[(st, st_y_end)], inc=cc)
        def _(g):
            return g.collective_compute(
                "AllGather", ALU.bypass, replica_groups=[list(range(NCORE))],
                ins=[yb_ag_in.ap().opt()], outs=[yb_ag_out.ap().opt()])
        cc_yag = cc.n

        lxc = lx.n
        st_comb = [None] * NT
        for t in range(NT):
            wg = [(cc, cc_yag)]
            if t >= 1:
                wg.append((st, st_comb[t - 1]))
            gp.inc(32)

            def gath2(g, t=t, wg=wg, gpn=gp.n):
                _w(g, wg)
                for k in range(2):
                    col = k * NT + t
                    g.indirect_dma_start(
                        ygb[:, k * D:(k + 1) * D], None,
                        yb_ag_out[:, :], IOA(ggix[:, col:col + 1], 0),
                        bounds_check=regs["bc_slot"], oob_is_err=False,
                    ).then_inc(gp.h, 16)
            P.add("gpsimd", gath2)
            gp_g = gp.n

            w = ([(st, st_comb[t - 2])] if t >= 2 and st_comb[t - 2]
                 else [(mm, mm.n)])
            dma(P, "scalar", xtv[:, (t % 2) * D:(t % 2 + 1) * D],
                src_dram[t * 128:(t + 1) * 128, :], lx, waits=w)

            wv = [(gp, gp_g), lx.now]
            if t >= 1:
                wv.append((st, st_comb[t - 1]))

            @op(P, "vector", waits=wv, inc=dv)
            def _(v, t=t):
                a = slice((t % 2) * D, (t % 2 + 1) * D)
                v.scalar_tensor_tensor(accv[:, 0:D], ygb[:, 0:D], warr[:, t:t + 1],
                                       xtv[:, a], ALU.mult, ALU.add)
                return v.scalar_tensor_tensor(accv[:, 0:D], ygb[:, D:2 * D],
                                              warr[:, NT + t:NT + t + 1],
                                              accv[:, 0:D], ALU.mult, ALU.add)

            dma(P, "sync", dst_dram[t * 128:(t + 1) * 128, :],
                accv[:, 0:D], st, waits=[dv.now])
            st_comb[t] = st.n
        return st_comb


    def emit_attention(src_dram, dst_dram, ffn1_tiles, upto=99):
        O_XNFM = 0
        O_KH = [0, 8192]
        O_VH = [16384, 24576]
        O_WV = 32768
        O_WOUT = 32768
        O_WBLK = 49152
        O_LNW = 49152
        O_KRAW = 51200
        O_KSM = 53248
        O_VSB = 55296
        O_YSM = 56320
        O_KSTG = 51200
        O_PT = 53248
        O_QB = 55296
        O_SIG = 51200
        O_POSQ = 55296
        O_BINV = 59392
        O_MROW = 56320
        O_MNEG = 60416
        O_MASK = 68608
        O_KLR = 70656
        O_KLC = 72704
        O_KLT = 72704
        O_GFM = 0
        O_OSB = 51200
        O_OLNW = 53248

        def A(a, b):
            return arena[:, a:b]

        def Af(a, n):
            return arena[:, a:a + 2 * n].bitcast(F32)

        dma(P, "sync", binc[:, :], binc_in.ap(), ld)
        dma(P, "sync", hpar[:, :], hpar_in.ap(), ld)
        dma(P, "sync", icon[:, :], icon_in.ap(), ld)
        dma(P, "gpsimd", A(O_MASK, O_MASK + 2048), masks_in.ap(), gp)

        @op(P, "gpsimd", inc=gp)
        def _(g):
            g.memset(onescol[:, :], 1.0)
            g.memset(onesb[:, :], 1.0)
            g.memset(zrowb[:, :], 0.0)
            return g.memset(zrow[:, :], 0.0)

        @op(P, "scalar", waits=[ld.now, gp.now], inc=ac)
        def _(sc):
            sc.activation(qscale[:, :], hpar[:, 16:32], AF.Exp, scale=-2.0)
            return sc.activation(smsig[:, :], hpar[:, 0:16], AF.Sigmoid)

        @op(P, "vector", waits=[ac.now], inc=dv)
        def _(v):
            v.tensor_scalar_mul(qscale[:, :], qscale[:, :], float(1.0 / np.sqrt(128.0)))
            return v.tensor_mul(binqs[:, :], binc[:, 0:16], qscale[:, :])
        dv_aconst = dv.n

        dma(P, "gpsimd", A(O_LNW, O_LNW + 1024), lnw4_in[:, 0:1024], gp)
        dma(P, "gpsimd", A(O_LNW + 1024, O_LNW + 2048), lnw4_in[:, 1024:2048], gp)
        gp_lnw = gp.n

        tmpb = ysb[:, 0:1024]
        ac_lnA = [None] * NT
        mm_tpA = [None] * NT
        lx0a = lx.n

        def ln_stats(srcap):
            @op(P, "vector", inc=dv)
            def _(v):
                v.reduce_sum(stat[:, 0:1], srcap, axis=AX.X)
                return v.tensor_scalar_mul(stat[:, 1:2], stat[:, 0:1], 1.0 / D)

            @op(P, "scalar", waits=[dv.now], inc=ac)
            def _(s_):
                return s_.activation(sqt[:, :], srcap, AF.Square)

            @op(P, "vector", waits=[ac.now], inc=dv)
            def _(v):
                v.reduce_sum(stat[:, 2:3], sqt[:, :], axis=AX.X)
                v.tensor_scalar_mul(stat[:, 3:4], stat[:, 2:3], 1.0 / D)
                v.tensor_mul(stat[:, 4:5], stat[:, 1:2], stat[:, 1:2])
                return v.tensor_sub(stat[:, 5:6], stat[:, 3:4], stat[:, 4:5])

            @op(P, "scalar", waits=[dv.now], inc=ac)
            def _(s_):
                return s_.activation(stat[:, 6:7], stat[:, 5:6], AF.Sqrt, bias=epsc[:, :])

            @op(P, "vector", waits=[ac.now], inc=dv)
            def _(v):
                v.reciprocal(stat[:, 7:8], stat[:, 6:7])
                v.tensor_mul(stat[:, 8:9], stat[:, 1:2], stat[:, 7:8])
                return v.tensor_scalar_mul(stat[:, 9:10], stat[:, 8:9], -1.0)

        for t in range(NT):
            s0 = (t % 2) * D
            sl = slice(s0, s0 + D)
            wx = ([(ac, ac_lnA[t - 2])] if t >= 2 else []) + \
                [(st, ffn1_tiles[t])]
            dma(P, "scalar", xt[:, sl], src_dram[t * 128:(t + 1) * 128, :],
                lx, waits=wx)

            @op(P, "vector", waits=[(lx, lx0a + 16 * (t + 1)), (dv, dv_aconst)], inc=dv)
            def _(v, sl=sl):
                v.reduce_sum(stat[:, 0:1], xt[:, sl], axis=AX.X)
                return v.tensor_scalar_mul(stat[:, 1:2], stat[:, 0:1], 1.0 / D)

            @op(P, "scalar", waits=[dv.now], inc=ac)
            def _(s_, sl=sl):
                return s_.activation(sqt[:, :], xt[:, sl], AF.Square)

            @op(P, "vector", waits=[ac.now], inc=dv)
            def _(v):
                v.reduce_sum(stat[:, 2:3], sqt[:, :], axis=AX.X)
                v.tensor_scalar_mul(stat[:, 3:4], stat[:, 2:3], 1.0 / D)
                v.tensor_mul(stat[:, 4:5], stat[:, 1:2], stat[:, 1:2])
                return v.tensor_sub(stat[:, 5:6], stat[:, 3:4], stat[:, 4:5])

            @op(P, "scalar", waits=[dv.now], inc=ac)
            def _(s_):
                return s_.activation(stat[:, 6:7], stat[:, 5:6], AF.Sqrt, bias=epsc[:, :])

            @op(P, "vector", waits=[ac.now], inc=dv)
            def _(v):
                v.reciprocal(stat[:, 7:8], stat[:, 6:7])
                v.tensor_mul(stat[:, 8:9], stat[:, 1:2], stat[:, 7:8])
                return v.tensor_scalar_mul(stat[:, 9:10], stat[:, 8:9], -1.0)

            @op(P, "scalar", waits=[dv.now], inc=ac)
            def _(s_, sl=sl):
                return s_.activation(accv[:, :], xt[:, sl], AF.Identity,
                                     bias=stat[:, 9:10], scale=stat[:, 7:8])
            ac_lnA[t] = ac.n

            ln_stats(accv[:, :])

            @op(P, "scalar", waits=[dv.now], inc=ac)
            def _(s_):
                return s_.activation(sqt[:, :], accv[:, :], AF.Identity,
                                     bias=stat[:, 9:10], scale=stat[:, 7:8])

            w = [ac.now, (gp, gp_lnw)]
            if t >= 2:
                w.append((mm, mm_tpA[t - 2]))

            @op(P, "vector", waits=w, inc=dv)
            def _(v, sl=sl):
                v.tensor_mul(tmpb, sqt[:, :], A(O_LNW, O_LNW + 1024))
                return v.tensor_add(lnt[:, sl], tmpb, A(O_LNW + 1024, O_LNW + 2048))

            for c in range(8):
                @op(P, "tensor", waits=[dv.now], inc=mm)
                def _(te, s0=s0, c=c):
                    return te.transpose(
                        tps[:, :], lnt[:, s0 + c * 128: s0 + (c + 1) * 128],
                        identb[:, :])

                @op(P, "vector", waits=[mm.now], inc=dv)
                def _(v, t=t, c=c):
                    return v.tensor_copy(
                        A(O_XNFM + c * T_OWN + t * 128,
                          O_XNFM + c * T_OWN + (t + 1) * 128), tps[:, :])
            mm_tpA[t] = mm.n
        dv_xnfm = dv.n

        if upto < 2:
            return st.n
        for c in range(8):
            dma(P, "gpsimd", A(O_WV + c * 2048, O_WV + (c + 1) * 2048), wv_in[c],
                wvs, waits=[(dv, dv_xnfm)] if c == 0 else ())
        dma(P, "gpsimd", Af(O_BINV, 2048)[0:1, :], binv_in.ap(), wvs)
        wvs_wv = wvs.n
        VBASE = 2048 * 2048
        gp_prev_vscat = 0
        for tt in range(NT):
            dv_ev = []
            for vc in range(4):
                @op(P, "tensor", waits=[(wvs, wvs_wv), (dv, dv.n)], inc=mm)
                def _(te, tt=tt, vc=vc):
                    te.matmul(psY[vc][:, :], ones_f[:, :],
                              Af(O_BINV, 2048)[0:1, vc * 512:(vc + 1) * 512],
                              start=True, stop=False)
                    r = None
                    for c in range(8):
                        r = te.matmul(
                            psY[vc][:, :],
                            A(O_XNFM + c * T_OWN + tt * 128,
                              O_XNFM + c * T_OWN + (tt + 1) * 128),
                            A(O_WV + c * 2048 + vc * 512,
                              O_WV + c * 2048 + (vc + 1) * 512),
                            start=False, stop=(c == 7))
                    return r

                @op(P, "vector", waits=[mm.now] +
                    ([(gp, gp_prev_vscat)] if gp_prev_vscat and vc == 0 else []),
                    inc=dv)
                def _(v, vc=vc):
                    return v.tensor_copy(
                        arena[:, O_VSB + vc * 512: O_VSB + (vc + 1) * 512],
                        psY[vc][:, :])
            gp_prev_vscat = gp.n + 16
            gp.inc(16)

            def vscat(g, tt=tt, dvn=dv.n):
                _w(g, [(dv, dvn)])
                g.dma_start(
                    out=bass.AP(kvy_ag_in, VBASE + tt * 128,
                                [[16 * 128, 128], [128 * 16 * 128, 16], [1, 128]]),
                    in_=arena[:, O_VSB:O_VSB + 2048].rearrange("p (h d) -> p h d", d=128),
                    allow_slow_non_contiguous=True,
                ).then_inc(gp.h, 16)
            P.add("gpsimd", vscat)
        gp_vscat = gp.n

        order1 = list(range(16, 32)) + [64]
        order2 = list(range(0, 16)) + list(range(48, 64))
        wblk_free = [0, 0]
        evb = ysb[:, 1024:2048]
        stq = [S("stqA"), S("stqB")]
        wbsem = [S("wbA"), S("wbB")]

        def a2_block(oi, o):
            buf = oi % 2
            wload_waits = [(mm, wblk_free[buf])] if wblk_free[buf] else [(dv, dv_xnfm)]
            dma(P, "gpsimd", A(O_WBLK + buf * 1024, O_WBLK + (buf + 1) * 1024),
                wblk_in[o], wbsem[buf], waits=wload_waits)
            wst_o = wbsem[buf].n
            for tb in range(4):
                hp = psA if tb % 2 == 0 else psB

                @op(P, "tensor", waits=[(wbsem[buf], wst_o), (ac, ac.n), (dv, dv.n),
                                        (gp, gp_vscat)], inc=mm)
                def _(te, o=o, tb=tb, buf=buf, hp=hp):
                    r = None
                    for c in range(8):
                        r = te.matmul(
                            hp[:, :],
                            A(O_WBLK + buf * 1024 + c * 128,
                              O_WBLK + buf * 1024 + (c + 1) * 128),
                            A(O_XNFM + c * T_OWN + tb * 512,
                              O_XNFM + c * T_OWN + (tb + 1) * 512),
                            start=(c == 0), stop=(c == 7))
                    return r
                if tb == 3:
                    wblk_free[buf] = mm.n

                if 16 <= o < 32:
                    @op(P, "scalar", waits=[mm.now], inc=ac)
                    def _(s_, o=o, tb=tb, hp=hp):
                        return s_.activation(
                            arena[:, O_KRAW + tb * 512: O_KRAW + (tb + 1) * 512],
                            hp[:, :], AF.Identity, bias=binc[:, o:o + 1])
                elif o == 64:
                    @op(P, "scalar", waits=[mm.now], inc=ac)
                    def _(s_, hp=hp):
                        return s_.activation(arena[0:16, O_YSM:O_YSM + 512],
                                             hp[0:16, :], AF.Identity,
                                             bias=binc[0:16, 64:65])
                    dma(P, "sync", kvy_ag_in[4096:4112, tb * 512:(tb + 1) * 512],
                        arena[0:16, O_YSM:O_YSM + 512], st, waits=[ac.now])
                elif o < 16:
                    eoff = 1024 + (tb % 2) * 512

                    sq = stq[tb % 2]

                    @op(P, "scalar", waits=[mm.now] +
                        ([(sq, sq.n)] if sq.n else []), inc=ac)
                    def _(s_, o=o, hp=hp, eoff=eoff):
                        return s_.activation(ysb[:, eoff:eoff + 512], hp[:, :],
                                             AF.Identity, bias=binqs[:, o:o + 1],
                                             scale=qscale[:, o:o + 1])
                    dma(P, "sync", q_fm[o * 128:(o + 1) * 128, tb * 512:(tb + 1) * 512],
                        ysb[:, eoff:eoff + 512], sq, waits=[ac.now])
                else:
                    eoff = 1024 + (tb % 2) * 512

                    sq = stq[tb % 2]

                    @op(P, "scalar", waits=[mm.now] +
                        ([(sq, sq.n)] if sq.n else []), inc=ac)
                    def _(s_, o=o, hp=hp, eoff=eoff):
                        return s_.activation(ysb[:, eoff:eoff + 512], hp[:, :],
                                             AF.Identity, bias=binc[:, o:o + 1])
                    dma(P, "sync",
                        p_fm[(o - 48) * 128:(o - 47) * 128, tb * 512:(tb + 1) * 512],
                        ysb[:, eoff:eoff + 512], sq, waits=[ac.now])

            if 16 <= o < 32:
                h = o - 16
                @op(P, "vector", waits=[ac.now, (st, st.n)], inc=dv)
                def _(v, h=h):
                    v.tensor_copy(arena[:, O_KLC + h: O_KLC + h + 1],
                                  arena[:, O_KRAW + 1023: O_KRAW + 1024])
                    v.tensor_copy(arena[:, O_KLC + 16 + h: O_KLC + 17 + h],
                                  arena[:, O_KRAW + 2047: O_KRAW + 2048])
                    for hb in range(2):
                        b0 = O_KRAW + hb * 1024
                        s0 = O_KSM + hb * 1024
                        v.tensor_sub(tmpb[:, 0:1023],
                                     arena[:, b0:b0 + 1023],
                                     arena[:, b0 + 1:b0 + 1024])
                        v.scalar_tensor_tensor(
                            arena[:, s0 + 1:s0 + 1024], tmpb[:, 0:1023],
                            smsig[:, h:h + 1], arena[:, b0 + 1:b0 + 1024],
                            ALU.mult, ALU.add)
                        v.tensor_scalar(stat[:, 10:11], smsig[:, h:h + 1], -1.0, 1.0,
                                        ALU.mult, ALU.add)
                        v.tensor_mul(arena[:, s0:s0 + 1],
                                     arena[:, b0:b0 + 1], stat[:, 10:11])
                    return v.tensor_copy(stat[:, 11:12], stat[:, 10:11])
                dma(P, "sync", kvy_ag_in[h * 128:(h + 1) * 128, :],
                    arena[:, O_KSM:O_KSM + 2048], st, waits=[dv.now])

        for oi, o in enumerate(order1):
            a2_block(oi, o)
        for sdx in range(2):
            @op(P, "tensor", waits=[(dv, dv.n)], inc=mm)
            def _(te, sdx=sdx):
                return te.transpose(tps[0:16, 0:128],
                                    arena[:, O_KLC + sdx * 16: O_KLC + (sdx + 1) * 16],
                                    identb[:, :])

            @op(P, "vector", waits=[mm.now], inc=dv)
            def _(v, sdx=sdx):
                return v.tensor_copy(ysb[0:16, sdx * 128:(sdx + 1) * 128],
                                     tps[0:16, 0:128])

            def _klst(e, sdx=sdx, w=dv.now):
                _w(e, [w])
                e.dma_start(out=bass.AP(kvy_ag_in, (4112 + sdx) * 2048,
                                        [[128, 16], [1, 128]]),
                            in_=ysb[0:16, sdx * 128:(sdx + 1) * 128]).then_inc(st.h, 16)
            st.inc(16)
            P.add("sync", _klst)
        st_fm_all = st.n

        if upto < 3:
            return st.n
        @op(P, "gpsimd", waits=[(st, st_fm_all), (gp, gp_vscat)], inc=cc)
        def _(g):
            return g.collective_compute(
                "AllGather", ALU.bypass,
                replica_groups=[[0, 1, 2, 3], [4, 5, 6, 7]],
                ins=[kvy_ag_in.ap().opt()], outs=[kvy_ag_out.ap().opt()])
        cc_kvy = cc.n

        for oj, o in enumerate(order2):
            a2_block(len(order1) + oj, o)
        stq_all = [stq[0].n, stq[1].n]
        mm_a2_end = mm.n
        ac_a2_end = ac.n

        if upto < 4:
            return st.n
        @op(P, "vector", inc=dv)
        def _(v):
            return v.memset(carry[:, :], 0.0)
        for qt in range(4):
            for ci in range(2):
                c = 2 * qt + ci
                rl, sl_ = (min(c, 7 - c)), (0 if c < 4 else 1)
                dma(P, "gpsimd",
                    Af(O_SIG, 2048)[0:16, ci * 1024:(ci + 1) * 1024],
                    kvy_ag_out[rl * 4224 + 4096: rl * 4224 + 4112,
                               sl_ * 1024:(sl_ + 1) * 1024],
                    gp, waits=[(cc, cc_kvy)] + ([(st, st.n)] if qt else []))

            @op(P, "scalar", waits=[gp.now], inc=ac)
            def _(s_):
                return s_.activation(Af(O_POSQ, 2048)[0:16, :],
                                     Af(O_SIG, 2048)[0:16, :], AF.Sigmoid)

            @op(P, "vector", waits=[ac.now], inc=dv)
            def _(v, qt=qt):
                v.tensor_tensor_scan(Af(O_SIG, 2048)[0:16, :],
                                     Af(O_POSQ, 2048)[0:16, :],
                                     Af(O_POSQ, 2048)[0:16, :],
                                     0.0 if qt == 0 else carry[:, 0:1],
                                     ALU.add, ALU.bypass)
                v.drain()
                return v.tensor_copy(carry[:, 0:1], Af(O_SIG, 2048)[0:16, 2047:2048])

            dma(P, "sync", pos_dram[:, qt * 2048:(qt + 1) * 2048],
                Af(O_SIG, 2048)[0:16, :], st, waits=[dv.now])
        st_pos = st.n
        st_q = st_pos

        def a45_init_sy(sy):
            R["sy_cA"] = sy.to_reg(sy.value_load(icon[0:1, 4:5]))
            R["sy_cB"] = sy.to_reg(sy.value_load(icon[0:1, 5:6]))
            _w(sy, [(st, st_pos)])
            sy.dma_start(out=Af(O_MROW, 2048)[0:16, 0:1024],
                         in_=bass.AP(pos_dram, R["sy_cA"], [[8192, 16], [1, 1024]])
                         ).then_inc(ldm.h, 16)
            sy.dma_start(out=Af(O_MROW, 2048)[0:16, 1024:2048],
                         in_=bass.AP(pos_dram, R["sy_cB"], [[8192, 16], [1, 1024]])
                         ).then_inc(ldm.h, 16)
        R = {}
        ldm.inc(32)
        P.add("sync", a45_init_sy)

        @op(P, "vector", waits=[ldm.now], inc=dv)
        def _(v):
            return v.tensor_scalar(ysb[0:16, 2048:4096],
                                   Af(O_MROW, 2048)[0:16, :], -1.0, -24.0,
                                   ALU.mult, ALU.add)

        dma(P, "sync", mneg_dram.ap(), ysb[0:16, 2048:4096], st,
            waits=[dv.now])
        st_mneg = st.n

        if upto < 5:
            return st.n
        KSTG = [arena[:, O_KSTG + i * 256: O_KSTG + i * 256 + 128] for i in range(8)]
        VSTG = [arena[:, O_KSTG + i * 256 + 128: O_KSTG + (i + 1) * 256]
                for i in range(8)]
        PT = [arena[:, O_PT + i * 512: O_PT + (i + 1) * 512] for i in range(4)]
        SB4 = [psA, psB, psY3, psY2]
        O_MNEGB = 60416

        def a5_init_pe(te):
            R["pe_rb"] = te.alloc_register(name="pe_rb")
            R["pe_t1"] = te.alloc_register(name="pe_t1")
            R["pe_c"] = te.alloc_register(name="pe_c")
            R["pe_psp"] = te.alloc_register(name="pe_psp")
            te.reg_mov(R["pe_rb"], 0)
            R["pe_tr"] = [te.value_load(icon[0:1, i:i + 1]) for i in range(4)]
        P.add("tensor", a5_init_pe)

        def a5_init_ac(sc):
            R["ac_ra"] = sc.alloc_register(name="ac_ra")
            R["ac_t1"] = sc.alloc_register(name="ac_t1")
            R["ac_c"] = sc.alloc_register(name="ac_c")
            R["ac_psp"] = sc.alloc_register(name="ac_psp")
            sc.reg_mov(R["ac_ra"], 0)
            R["ac_tr"] = [sc.value_load(icon[0:1, i:i + 1]) for i in range(4)]
        P.add("scalar", a5_init_ac)

        def dv_mega(v):
            rj = v.alloc_register(name="m_rj")
            rp = v.alloc_register(name="m_rp")
            rs = v.alloc_register(name="m_rs")
            bl = v.alloc_register(name="m_bl")
            bk1 = v.alloc_register(name="m_bk1")
            ibc = v.alloc_register(name="m_ibc")
            hpar = v.alloc_register(name="m_hp")
            t1 = v.alloc_register(name="m_t1")
            cr = v.alloc_register(name="m_cr")
            sp = v.alloc_register(name="m_sp")
            trips = v.value_load(icon[0:1, 6:7])
            v.reg_mov(rs, 4)
            v.reg_mov(bl, 4)
            v.reg_mov(bk1, 0)
            v.reg_mov(ibc, 3)
            v.reg_mov(hpar, 1)
            v.reg_mov(sp, 0)
            with dynloop(nc, v, cr, trips):
                with v.If_eq(bl, 4):
                    with v.If_ne(bk1, 0):
                        v.reg_add(t1, rs, -4)
                        v.wait_ge(amm.h, t1)
                        for dt in range(3):
                            v.tensor_tensor(
                                SB4[dt][:, :], SB4[dt][:, :],
                                A(O_MASK + dt * 512, O_MASK + (dt + 1) * 512),
                                ALU.add)
                        v.drain()
                        v.tensor_tensor(
                            SB4[3][:, :], SB4[3][:, :],
                            A(O_MASK + 3 * 512, O_MASK + 4 * 512),
                            ALU.add).then_inc(adm.h, 4)
                        v.reg_add(t1, rs, -4)
                        v.wait_ge(aen.h, t1)
                        v.tensor_copy(densb[:, :], psY1[0:1, :])
                        v.drain()
                        v.reciprocal(recsb[:, :], densb[:, :]).then_inc(aep.h, 1)
                        v.wait_ge(abc.h, bk1)
                        v.tensor_copy(recb[:, :], psY2[:, :])
                        v.drain()
                        v.reg_mul(t1, bk1, 16)
                        v.reg_add(t1, t1, -16)
                        v.wait_ge(aos.h, t1)
                        v.tensor_tensor(ysb[:, 0:512], psY0[:, :], recb[:, :],
                                        ALU.mult).then_inc(aev.h, 1)
                    v.reg_add(ibc, ibc, 1)
                    with v.If_eq(ibc, 4):
                        v.reg_mov(ibc, 0)
                        v.reg_mov(t1, 1)
                        v.reg_sub(hpar, t1, hpar)
                    v.reg_add(t1, ibc, 8)
                    v.tensor_copy(mapi[0:1, 19:20],
                                  bass.AP(icon, t1, [[16, 1], [1, 1]]))
                    v.drain()
                    nj_sv = v.value_load(mapi[0:1, 19:20])
                    v.reg_add(bl, nj_sv, 4)
                    v.reg_mul(t1, hpar, 8192)
                    v.reg_add(rj, t1, 0)
                    v.reg_mul(rp, hpar, 64)
                    v.reg_mov(sp, 0)
                    v.reg_add(bk1, bk1, 1)
                    v.reg_add(t1, bk1, 3)
                    v.reg_alu(t1, t1, 2, op=ALU.logical_shift_right)
                    v.wait_ge(akv.h, t1)
                with v.Else():
                    v.reg_add(t1, rs, -8)
                    v.reg_alu(t1, t1, 0, op=ALU.max)
                    v.wait_ge(aen.h, t1)
                    with v.If_eq(sp, 0):
                        v.tensor_copy(pcstg[:, 0:4],
                                      bass.AP(poscol, rp, [[128, 128], [1, 4]]))
                        v.drain()
                        v.tensor_copy(
                            arena[:, O_KSTG: O_KSTG + 1024]
                            .rearrange("p (s kv d) -> p s kv d", s=4, kv=2, d=128),
                            bass.AP(arena, rj,
                                    [[73728, 128], [128, 4], [16384, 2], [1, 128]])
                        ).then_inc(asg.h, 4)
                        v.reg_mov(sp, 1)
                    with v.Else():
                        v.tensor_copy(pcstg[:, 4:8],
                                      bass.AP(poscol, rp, [[128, 128], [1, 4]]))
                        v.drain()
                        v.tensor_copy(
                            arena[:, O_KSTG + 1024: O_KSTG + 2048]
                            .rearrange("p (s kv d) -> p s kv d", s=4, kv=2, d=128),
                            bass.AP(arena, rj,
                                    [[73728, 128], [128, 4], [16384, 2], [1, 128]])
                        ).then_inc(asg.h, 4)
                        v.reg_mov(sp, 0)
                    v.reg_add(rj, rj, 512)
                    v.reg_add(rp, rp, 4)
                    v.reg_add(rs, rs, 4)
                    v.reg_add(bl, bl, -4)
        P.add("vector", dv_mega) if A5PARTS[0] else None
        aep.inc(64)
        aev.inc(64)

        mm_tp_prev = 0
        for h in range(16):
            hb = h % 2
            wh = [(ahd, h - 1), (cc, cc_kvy), (st, st_pos)]
            if h < 2:
                wh.append((mm, mm_a2_end))
            for c in range(8):
                rl, slt = min(c, 7 - c), (0 if c < 4 else 1)
                dma(P, "gpsimd", A(O_KH[hb] + c * 1024, O_KH[hb] + (c + 1) * 1024),
                    kvy_ag_out[rl * 4224 + h * 128: rl * 4224 + (h + 1) * 128,
                               slt * 1024:(slt + 1) * 1024], ld,
                    waits=wh if c == 0 else ())
                voff = (rl * 4224 + 2048) * 2048 + h * 262144 + slt * 8 * 128
                dma(P, "gpsimd", A(O_VH[hb] + c * 1024, O_VH[hb] + (c + 1) * 1024),
                    bass.AP(kvy_ag_out, voff, [[2048, 128], [128, 8], [1, 128]]), ld)
            dma(P, "gpsimd", pc64[:, :],
                bass.AP(pos_dram, h * 8192, [[128, 64], [1, 128]]), ldp,
                waits=[(st, st_pos)] + ([(mm, mm_tp_prev)] if h else []))
            if h == 0:
                for rl in range(4):
                    dma(P, "gpsimd", arena[rl * 2:(rl + 1) * 2, O_KLR:O_KLR + 2048],
                        kvy_ag_out[rl * 4224 + 4112: rl * 4224 + 4114, :], ld)
            ld_h = ld.n

            dma(P, "gpsimd", arena[0:1, O_MNEGB + hb * 2048: O_MNEGB + (hb + 1) * 2048],
                mneg_dram[h:h + 1, :], ldm,
                waits=[(st, st_mneg), (ahd, h - 1)])

            mm.inc(2)

            def pe_tp(te, h=h, hb=hb, tgt=mm.n, ldh=ld_h, ldp_n=ldp.n,
                      w2=([(ac, ac_a2_end)] if h < 2 else [])):
                _w(te, [(ld, ldh), (ldp, ldp_n), (ahd, h - 1), (aac, 136 * h)] + w2)
                te.transpose(psA[:, 0:64], pc64[:, :], ident[0:64, 0:64]
                             ).then_inc(mm.h, 1)
                te.transpose(tps[:, 0:8],
                             arena[0:8, O_KLR + h * 128: O_KLR + (h + 1) * 128],
                             identb[0:8, 0:8]).then_inc(mm.h, 1)
            P.add("tensor", pe_tp) if A5PARTS[4] else None
            mm_tp_h = mm.n
            mm_tp_prev = mm_tp_h

            akv.inc(1)

            def ac_prep(sc, h=h, hb=hb, mmh=mm_tp_h, ldh=ld_h):
                _w(sc, [(mm, mmh), (ld, ldh)])
                sc.activation(poscol[:, hb * 64:(hb + 1) * 64], psA[:, 0:64],
                              AF.Identity)
                sc.activation(arena[:, O_KLT + hb * 8: O_KLT + (hb + 1) * 8],
                              tps[:, 0:8], AF.Identity)
                r = None
                for c in range(1, 8):
                    pc_ = c - 1
                    idx = min(pc_, 7 - pc_) * 2 + (0 if pc_ < 4 else 1)
                    col = A(O_KH[hb] + c * 1024, O_KH[hb] + c * 1024 + 1)
                    r = sc.activation(
                        col,
                        arena[:, O_KLT + hb * 8 + idx: O_KLT + hb * 8 + idx + 1],
                        AF.Identity, bias=col, scale=smsig[:, h:h + 1])
                r.then_inc(akv.h, 1)
            P.add("scalar", ac_prep) if A5PARTS[5] else None

            for ib in range(4):
                g = h * 4 + ib
                dma(P, "sync",
                    A(O_QB + (g % 2) * 512, O_QB + (g % 2 + 1) * 512),
                    q_fm[h * 128:(h + 1) * 128, ib * 512:(ib + 1) * 512], ldq2,
                    waits=[(aib, g - 1), (stq[0], stq_all[0]),
                           (stq[1], stq_all[1]), (st, st_q)])
                ldq_n = ldq2.n
                qslice = A(O_QB + (g % 2) * 512, O_QB + (g % 2 + 1) * 512)
                mrow = arena[0:1, O_MNEGB + hb * 2048 + ib * 512:
                             O_MNEGB + hb * 2048 + (ib + 1) * 512]

                def pe_blk(te, h=h, hb=hb, ib=ib, g=g, ldq=ldq_n, qslice=qslice,
                           mrow=mrow, ldm_n=32 + 16 * (h + 1)):
                    rb, t1, psp = R["pe_rb"], R["pe_t1"], R["pe_psp"]
                    _w(te, [(ldq2, ldq), (ldm, ldm_n), (aev, g), (aep, g)])
                    te.matmul(psY0[:, :], onesb[0:1, :], zrowb[:, :],
                              start=True, stop=False)
                    te.matmul(psY1[0:1, :], onesb[0:1, 0:1], zrowb[:, :],
                              start=True, stop=False)
                    te.reg_add(t1, rb, 4)
                    te.wait_ge(asg.h, t1)
                    for s in range(4):
                        te.matmul(SB4[s][:, :], onesb[0:1, :], mrow,
                                  start=True, stop=False)
                        te.matmul(SB4[s][:, :], KSTG[s], qslice,
                                  start=False, stop=True).then_inc(amm.h, 1)
                    te.reg_mov(psp, 1)
                    with dynloop(nc, te, R["pe_c"], R["pe_tr"][ib]):
                        te.reg_add(t1, rb, 8)
                        te.wait_ge(asg.h, t1)
                        with te.If_eq(psp, 0):
                            for s in range(4):
                                te.reg_add(t1, rb, s + 1)
                                te.wait_ge(aac.h, t1)
                                te.matmul(SB4[s][:, :], onesb[0:1, :], mrow,
                                          start=True, stop=False)
                                te.matmul(SB4[s][:, :], KSTG[s], qslice,
                                          start=False, stop=True).then_inc(amm.h, 1)
                                te.matmul(psY0[:, :], VSTG[4 + s], PT[s],
                                          start=False, stop=False)
                                te.matmul(psY1[0:1, :], onescol[:, :], PT[s],
                                          start=False, stop=False
                                          ).then_inc(aen.h, 1)
                            te.reg_mov(psp, 1)
                        with te.Else():
                            for s in range(4):
                                te.reg_add(t1, rb, s + 1)
                                te.wait_ge(aac.h, t1)
                                te.matmul(SB4[s][:, :], onesb[0:1, :], mrow,
                                          start=True, stop=False)
                                te.matmul(SB4[s][:, :], KSTG[4 + s], qslice,
                                          start=False, stop=True).then_inc(amm.h, 1)
                                te.matmul(psY0[:, :], VSTG[s], PT[s],
                                          start=False, stop=False)
                                te.matmul(psY1[0:1, :], onescol[:, :], PT[s],
                                          start=False, stop=False
                                          ).then_inc(aen.h, 1)
                            te.reg_mov(psp, 0)
                        te.reg_add(rb, rb, 4)
                    with te.If_eq(psp, 0):
                        for s in range(4):
                            te.reg_add(t1, rb, s + 1)
                            te.wait_ge(aac.h, t1)
                            te.matmul(psY0[:, :], VSTG[4 + s], PT[s],
                                      start=False, stop=False)
                            te.matmul(psY1[0:1, :], onescol[:, :], PT[s],
                                      start=False, stop=False).then_inc(aen.h, 1)
                    with te.Else():
                        for s in range(4):
                            te.reg_add(t1, rb, s + 1)
                            te.wait_ge(aac.h, t1)
                            te.matmul(psY0[:, :], VSTG[s], PT[s],
                                      start=False, stop=False)
                            te.matmul(psY1[0:1, :], onescol[:, :], PT[s],
                                      start=False, stop=False).then_inc(aen.h, 1)
                    te.reg_add(rb, rb, 4)
                    te.wait_ge(aep.h, g + 1)
                    te.matmul(psY2[:, :], ones_f[:, :], recsb[:, :],
                              start=True, stop=True).then_inc(abc.h, 1)
                    te.nop().then_inc(aib.h, 1)
                P.add("tensor", pe_blk) if A5PARTS[1] else None
                abc.inc(1)
                aib.inc(1)

                def ac_blk(sc, h=h, ib=ib, g=g):
                    ra, t1, apsp = R["ac_ra"], R["ac_t1"], R["ac_psp"]
                    sc.reg_mov(apsp, 0)
                    with dynloop(nc, sc, R["ac_c"], R["ac_tr"][ib]):
                        sc.wait_ge(aen.h, ra)
                        with sc.If_eq(apsp, 0):
                            for s in range(4):
                                sc.reg_add(t1, ra, s + 1)
                                sc.wait_ge(amm.h, t1)
                                sc.activation(PT[s], SB4[s][:, :], AF.Exp,
                                              bias=pcstg[:, s:s + 1]
                                              ).then_inc(aac.h, 1)
                            sc.reg_mov(apsp, 1)
                        with sc.Else():
                            for s in range(4):
                                sc.reg_add(t1, ra, s + 1)
                                sc.wait_ge(amm.h, t1)
                                sc.activation(PT[s], SB4[s][:, :], AF.Exp,
                                              bias=pcstg[:, 4 + s:5 + s]
                                              ).then_inc(aac.h, 1)
                            sc.reg_mov(apsp, 0)
                        sc.reg_add(ra, ra, 4)
                    sc.wait_ge(aen.h, ra)
                    sc.wait_ge(adm.h, 4 * (g + 1))
                    with sc.If_eq(apsp, 0):
                        for s in range(4):
                            sc.activation(PT[s], SB4[s][:, :], AF.Exp,
                                          bias=pcstg[:, s:s + 1]
                                          ).then_inc(aac.h, 1)
                    with sc.Else():
                        for s in range(4):
                            sc.activation(PT[s], SB4[s][:, :], AF.Exp,
                                          bias=pcstg[:, 4 + s:5 + s]
                                          ).then_inc(aac.h, 1)
                    sc.reg_add(ra, ra, 4)
                P.add("scalar", ac_blk) if A5PARTS[2] else None

                aos.inc(16)

                def o_store(sy, h=h, ib=ib, g=g):
                    _w(sy, [(aev, g + 1)])
                    sy.dma_start(out=o_fm[h * 128:(h + 1) * 128,
                                          ib * 512:(ib + 1) * 512],
                                 in_=ysb[:, 0:512]).then_inc(aos.h, 16)
                P.add("sync", o_store) if A5PARTS[3] else None

            ahd.inc(1)

            def pe_hdone(te):
                te.nop().then_inc(ahd.h, 1)
            P.add("tensor", pe_hdone)
        ahd_all = ahd.n
        aos_all = aos.n

        akv.inc(1)

        def ac_pad(sc):
            sc.nop().then_inc(akv.h, 1)
        P.add("scalar", ac_pad)


        if upto < 6:
            return aos.n if False else st.n
        for de in range(16):
            dma(P, "gpsimd", A(O_WOUT + de * 1024, O_WOUT + (de + 1) * 1024),
                woutT_in[de * 128:(de + 1) * 128, :], wst,
                waits=[(ahd, ahd_all)] if de == 0 else ())
        dma(P, "sync", Af(O_OLNW, 1024), lnw4_in[:, 2048:3072], ldp,
            waits=[(ahd, ahd_all)])
        dma(P, "sync", Af(55296, 1024), lnw4_in[:, 3072:4096], ldp)
        wst_wo = wst.n
        ld_oln = ldp.n
        lxp = [S("lxA6a"), S("lxA6b")]

        lx_o = lx.n
        for hc in range(16):
            dma(P, "sync", ysb[:, 2048:4096],
                p_fm[hc * 128:(hc + 1) * 128, :], ld,
                waits=[(ahd, ahd_all), (aos, aos_all), (stq[0], stq_all[0]),
                       (stq[1], stq_all[1])] + ([(dv, dv.n)] if hc else []))
            dma(P, "scalar", lnt[:, 0:2048], o_fm[hc * 128:(hc + 1) * 128, :], lx,
                waits=[(dv, dv.n)] if hc else [])

            @op(P, "scalar", waits=[ld.now, lx.now], inc=ac)
            def _(s_):
                return s_.activation(ysb[:, 0:2048], ysb[:, 2048:4096], AF.Silu)

            @op(P, "vector", waits=[ac.now], inc=dv)
            def _(v, hc=hc):
                return v.tensor_mul(A(O_GFM + hc * 2048, O_GFM + (hc + 1) * 2048),
                                    ysb[:, 0:2048], lnt[:, 0:2048])
        dv_gfm = dv.n

        st_a6 = [None] * NT
        for tt in range(NT):
            for half in range(2):
                hp2 = psY2 if half == 0 else psY3

                @op(P, "tensor", waits=[(dv, dv_gfm), (wst, wst_wo)] +
                    ([(dv, dv.n)] if (tt or half) else []), inc=mm)
                def _(te, tt=tt, half=half, hp2=hp2):
                    r = None
                    for de in range(16):
                        r = te.matmul(
                            hp2[:, :],
                            A(O_GFM + de * 2048 + tt * 128,
                              O_GFM + de * 2048 + (tt + 1) * 128),
                            A(O_WOUT + de * 1024 + half * 512,
                              O_WOUT + de * 1024 + (half + 1) * 512),
                            start=(de == 0), stop=(de == 15))
                    return r

            @op(P, "vector", waits=[mm.now], inc=dv)
            def _(v):
                v.tensor_copy(Af(O_OSB, 1024)[:, 0:512], psY2[:, :])
                return v.tensor_copy(Af(O_OSB, 1024)[:, 512:1024], psY3[:, :])

            ln_stats(Af(O_OSB, 1024))

            @op(P, "scalar", waits=[dv.now], inc=ac)
            def _(s_):
                return s_.activation(sqt[:, :], Af(O_OSB, 1024), AF.Identity,
                                     bias=stat[:, 9:10], scale=stat[:, 7:8])

            wxx = [(st, st_a6[tt - 2])] if tt >= 2 and st_a6[tt - 2] else []
            dma(P, "scalar", xt[:, (tt % 2) * D:(tt % 2 + 1) * D],
                x_res[tt * 128:(tt + 1) * 128, :], lxp[tt % 2], waits=wxx)

            @op(P, "vector", waits=[ac.now, lxp[tt % 2].now, (ldp, ld_oln)] +
                ([(st, st_a6[tt - 1])] if tt >= 1 and st_a6[tt - 1] else []), inc=dv)
            def _(v, tt=tt):
                v.tensor_mul(accv[:, :], sqt[:, :], Af(O_OLNW, 1024))
                v.tensor_add(accv[:, :], accv[:, :], Af(55296, 1024))
                return v.tensor_add(accv[:, :], accv[:, :],
                                    xt[:, (tt % 2) * D:(tt % 2 + 1) * D])

            dma(P, "sync", dst_dram[tt * 128:(tt + 1) * 128, :], accv[:, :], st,
                waits=[dv.now])
            st_a6[tt] = st.n
        return st_a6
    if phases == ("ffn1",):
        final_st = emit_ffn(x_my.ap(), y_my.ap(), first_call=True)[-1]
    elif phases == ("ffn1", "attn"):
        f1 = emit_ffn(x_my.ap(), x_res.ap(), first_call=True)
        final_st = emit_attention(x_res.ap(), y_my.ap(), f1, upto=UPTO[0])
        final_st = final_st[-1] if isinstance(final_st, list) else final_st
    elif phases == ("ffn1", "attn", "ffn2"):
        f1 = emit_ffn(x_my.ap(), x_res.ap(), first_call=True)
        a6 = emit_attention(x_res.ap(), x_res.ap(), f1)
        final_st = emit_ffn(x_res.ap(), y_my.ap(), first_call=False,
                            entry=[(st, a6[-1])],
                            entry_tiles=[[(st, a6[t])] for t in range(NT)])[-1]
    else:
        raise NotImplementedError(phases)

    if adbg:
        da_q = nc.dram_tensor("da_q", [2048, T_OWN], BF16, kind="ExternalOutput")
        da_p = nc.dram_tensor("da_p", [2048, T_OWN], BF16, kind="ExternalOutput")
        da_o = nc.dram_tensor("da_o", [2048, T_OWN], BF16, kind="ExternalOutput")
        da_kvy = nc.dram_tensor("da_kvy", [4 * 4224, 2048], BF16,
                                kind="ExternalOutput")
        da_pos = nc.dram_tensor("da_pos", [16, 8192], F32, kind="ExternalOutput")
        da_mneg = nc.dram_tensor("da_mneg", [16, 2048], BF16,
                                 kind="ExternalOutput")
        da_xr = nc.dram_tensor("da_xr", [T_OWN, D], F32, kind="ExternalOutput")
        dbg = S("dbg")
        dma(P, "scalar", da_q.ap(), q_fm.ap(), dbg, waits=[(st, final_st)])
        dma(P, "scalar", da_p.ap(), p_fm.ap(), dbg)
        dma(P, "scalar", da_o.ap(), o_fm.ap(), dbg)
        dma(P, "scalar", da_kvy.ap(), kvy_ag_out.ap(), dbg)
        dma(P, "scalar", da_pos.ap(), pos_dram.ap(), dbg)
        dma(P, "scalar", da_mneg.ap(), mneg_dram.ap(), dbg)
        dma(P, "scalar", da_xr.ap(), x_res.ap(), dbg)

        @op(P, "sync", waits=[dbg.now, (st, final_st)])
        def _(sy):
            pass
    elif debug:
        dbg = S("dbg")
        dma(P, "scalar", d_ln.ap(), ln_ag_out.ap(), dbg, waits=[(st, final_st)])
        dma(P, "scalar", d_logits.ap(), logits[:, :], dbg)
        dma(P, "scalar", d_posg.ap(), posg[:, :], dbg)
        dma(P, "scalar", d_cnt.ap(), cagb[:, :], dbg)
        dma(P, "scalar", d_offs.ap(), offs[:, :], dbg)
        dma(P, "scalar", d_map.ap(), rs_out.ap(), dbg)
        dma(P, "scalar", d_ybin.ap(), yb_ag_in.ap(), dbg)
        dma(P, "scalar", d_earr.ap(), earr[:, :], dbg)
        dma(P, "scalar", d_parr.ap(), parr[:, :], dbg)
        dma(P, "scalar", d_warr.ap(), warr[:, :], dbg)
        dma(P, "scalar", d_ggix.ap(), ggix[:, :], dbg)
        dma(P, "scalar", d_postm.ap(), postm[:, :], dbg)
        dma(P, "scalar", d_buf.ap(), arena[:, O_BUFFM:O_BUFFM + 8 * SLOT_OWN], dbg)
        dma(P, "scalar", d_hfm.ap(), arena[:, O_HFM:O_HFM + 32 * 512], dbg)
        dma(P, "scalar", d_w1.ap(), arena[:, O_W1:O_W1 + 8 * HID], dbg)
        final_st = None

        @op(P, "sync", waits=[dbg.now])
        def _(sy):
            pass
    else:
        @op(P, "sync", waits=[(st, final_st)])
        def _(sy):
            pass

    bb = nc.Block()
    blk = bb.__enter__()
    ctxs.append(bb)
    for name in ENGINES:
        fns = P.ops[name]
        getattr(blk, name)(lambda e, fns=fns: [f(e) for f in fns])

    for c in reversed(ctxs):
        c.__exit__(None, None, None)
    return nc


def _stage_inputs(inputs):
    x = np.asarray(inputs["L0_attn_x"])
    wr = np.ascontiguousarray(np.asarray(inputs["L0_ffn_Wr"]), dtype=np.float32)
    w_in = np.asarray(inputs["L0_attn_W_in"], dtype=np.float32)
    b_in = np.asarray(inputs["L0_attn_b_in"], dtype=np.float32)
    w_out = np.asarray(inputs["L0_attn_W_out"], dtype=np.float32)
    ilw = np.asarray(inputs["L0_attn_in_ln_w"], dtype=np.float32)
    ilb = np.asarray(inputs["L0_attn_in_ln_b"], dtype=np.float32)
    olw = np.asarray(inputs["L0_attn_out_ln_w"], dtype=np.float32)
    olb = np.asarray(inputs["L0_attn_out_ln_b"], dtype=np.float32)
    smear = np.asarray(inputs["L0_attn_smear_factor"], dtype=np.float32)
    lsc = np.asarray(inputs["L0_attn_log_scale"], dtype=np.float32)

    w_in_pad = np.zeros((8320, 1024), np.float32)
    w_in_pad[:8208] = w_in
    b_in_pad = np.zeros(8320, np.float32)
    b_in_pad[:8208] = b_in
    wblk = np.ascontiguousarray(
        w_in_pad.reshape(65, 128, 8, 128).transpose(0, 3, 2, 1).reshape(65, 128, 1024))
    wv = np.ascontiguousarray(
        w_in_pad[4096:6144].T.reshape(8, 128, 2048))
    binc = np.ascontiguousarray(b_in_pad.reshape(65, 128).T)
    binv = np.ascontiguousarray(b_in_pad[4096:6144].reshape(1, 2048))
    woutT = np.ascontiguousarray(w_out.T)
    lnw4 = np.ascontiguousarray(np.broadcast_to(
        np.concatenate([ilw, ilb, olw, olb]), (128, 4096)))
    hpar = np.ascontiguousarray(np.broadcast_to(
        np.concatenate([smear, lsc]), (128, 32)))
    masks = np.zeros((128, 4, 512), np.float32)
    jj = np.arange(128)[:, None]
    ii = np.arange(512)[None, :]
    for k in range(4):
        masks[:, k, :] = np.where(128 * k + jj > ii, -1e10, 0.0)
    masks = np.ascontiguousarray(masks.reshape(128, 2048))
    w1 = np.asarray(inputs["L0_ffn_W1"])
    w2 = np.asarray(inputs["L0_ffn_W2"])
    b1 = np.asarray(inputs["L0_ffn_b1"])
    b2 = np.asarray(inputs["L0_ffn_b2"])
    br = np.asarray(inputs["L0_ffn_br"])

    ident = np.eye(128, dtype=np.float32)
    iota8 = np.broadcast_to(np.arange(E, dtype=np.float32), (128, E)).copy()
    brb = np.broadcast_to(br.astype(np.float32), (128, E)).copy()

    in_maps = []
    for r in range(NCORE):
        b = r // 4
        cA, cB = r % 4, 7 - (r % 4)
        x_mine = np.ascontiguousarray(
            np.concatenate([x[b, cA * 1024:(cA + 1) * 1024],
                            x[b, cB * 1024:(cB + 1) * 1024]], 0), dtype=np.float32)
        sel = np.zeros((16, 2), np.float32)
        for c in range(8):
            rr, off = (b * 4 + min(c, 7 - c)), (0 if c < 4 else 1)
            row = 2 * rr + off
            if c < cA:
                sel[row, 0] = 1.0
            if cA < c < cB:
                sel[row, 1] = 1.0
        consts = np.zeros((128, 4), np.float32)
        consts[:, 0] = b * CAP
        consts[:, 1] = r * T_OWN + 1
        b1c = np.ascontiguousarray(b1[r].reshape(32, 128).T, dtype=np.float32)
        icon = np.zeros((1, 16), np.int32)
        gpos = [cA * 1024, cA * 1024 + 512, cB * 1024, cB * 1024 + 512]
        for i in range(4):
            icon[0, i] = gpos[i] // 512
            icon[0, 8 + i] = gpos[i] // 128 + 4
        icon[0, 4] = cA * 1024
        icon[0, 5] = cB * 1024
        icon[0, 6] = 16 * (sum(gp_ // 128 + 4 for gp_ in gpos) // 4) + 65
        in_maps.append({
            "x_my": x_mine,
            "wr_in": wr,
            "w1_in": np.ascontiguousarray(w1[r], dtype=np.float32),
            "w2_in": np.ascontiguousarray(w2[r], dtype=np.float32),
            "b1c_in": b1c,
            "b2r_in": np.ascontiguousarray(b2[r].reshape(1, D), dtype=np.float32),
            "brb_in": brb,
            "ident_in": ident,
            "iota8_in": iota8,
            "selpref_in": sel,
            "consts_in": consts,
            "wblk_in": wblk,
            "wv_in": wv,
            "binc_in": binc,
            "binv_in": binv,
            "woutT_in": woutT,
            "lnw4_in": lnw4,
            "hpar_in": hpar,
            "masks_in": masks,
            "icon_in": icon,
        })
    return in_maps


def _gather_output(results):
    out = np.zeros((2, 8192, D), np.float32)
    for r in range(NCORE):
        b = r // 4
        cA, cB = r % 4, 7 - (r % 4)
        y = results[r]["y_my"]
        out[b, cA * 1024:(cA + 1) * 1024] = y[0:1024]
        out[b, cB * 1024:(cB + 1) * 1024] = y[1024:2048]
    return out


def kernel(**inputs):
    nc = build_program(phases=("ffn1", "attn", "ffn2"))
    in_maps = _stage_inputs(inputs)
    res = run_bass_kernel_spmd(nc, in_maps, core_ids=list(range(NCORE)))
    return _gather_output(res.results)


def kernel_debug(**inputs):
    nc = build_program(phases=("ffn1",), debug=True)
    in_maps = _stage_inputs(inputs)
    res = run_bass_kernel_spmd(nc, in_maps, core_ids=list(range(NCORE)))
    return res.results


# version 10 (final)
# speedup vs baseline: 1.1570x; 1.0041x over previous
import numpy as np
import ml_dtypes
import concourse.bass as bass
import concourse.mybir as mybir
from concourse.bass_utils import run_bass_kernel_spmd

F32 = mybir.dt.float32
BF16 = mybir.dt.bfloat16
I32 = mybir.dt.int32
AF = mybir.ActivationFunctionType
ALU = mybir.AluOpType
AX = mybir.AxisListType
IOA = bass.IndirectOffsetOnAxis

NCORE = 8
D = 1024
T_OWN = 2048
NT = 16
E = 8
CAP = 1280
NSLOT = E * 2 * CAP
SLOT_OWN = 2 * CAP
HID = 4096
LN_EPS = 1e-5

ENGINES = ("sync", "gpsimd", "scalar", "vector", "tensor")
UPTO = [99]
A5PARTS = [True, True, True, True, True, True, True]

from contextlib import contextmanager
from concourse.ordered_engine_set import OrderedEngineSet

_DYN_ID = [0]


@contextmanager
def dynloop(nc, eng, creg, end_sv):
    _DYN_ID[0] += 1
    lname = f"dyn{_DYN_ID[0]}_loop"
    ename = f"dyn{_DYN_ID[0]}_end"
    engset = OrderedEngineSet([eng.engine])
    eng.reg_mov(creg, 0)
    eng.br_lt(creg, end_sv, on_true=lname, on_false=ename)
    with nc.body(lname, valid_engines=engset):
        yield
        eng.reg_add(creg, creg, 1)
        eng.br_lt(creg, end_sv, on_true=lname, on_false=ename)
    nc.switch_bb(ename)



class Sem:
    def __init__(self, h):
        self.h = h
        self.n = 0

    def inc(self, k=1):
        self.n += k
        return self.n

    @property
    def now(self):
        return (self, self.n)


class Prog:
    def __init__(self):
        self.ops = {e: [] for e in ENGINES}

    def add(self, eng, fn):
        self.ops[eng].append(fn)


def _w(e, waits):
    for s, v in waits:
        if v > 0:
            e.wait_ge(s.h, v)


_DVE_COMPUTE = {
    "tensor_copy", "tensor_tensor", "tensor_add", "tensor_sub", "tensor_mul",
    "tensor_max", "tensor_scalar", "tensor_single_scalar", "tensor_scalar_add",
    "tensor_scalar_sub", "tensor_scalar_mul", "tensor_scalar_max",
    "tensor_scalar_min", "tensor_relu", "tensor_reduce", "reduce_max",
    "reduce_sum", "scalar_tensor_tensor", "tensor_tensor_scan", "memset",
    "memzero", "reciprocal", "max", "max_index", "select", "copy_predicated",
    "bn_stats", "bn_aggr", "transpose", "match_replace",
}


class AutoDrain:

    def __init__(self, v):
        object.__setattr__(self, "_v", v)
        object.__setattr__(self, "_n", 0)

    def __getattr__(self, name):
        v = object.__getattribute__(self, "_v")
        fn = getattr(v, name)
        if name in _DVE_COMPUTE:
            def call(*a, **k):
                n = object.__getattribute__(self, "_n")
                if n > 0:
                    v.drain()
                object.__setattr__(self, "_n", n + 1)
                return fn(*a, **k)
            return call
        return fn


def op(P, eng, waits=(), inc=None, inck=1):
    if inc is not None:
        inc.inc(inck)

    def deco(fn):
        def run(e):
            if eng == "vector":
                e = AutoDrain(e)
            _w(e, waits)
            r = fn(e)
            if inc is not None:
                r.then_inc(inc.h, inck)
        P.add(eng, run)
        return None
    return deco


def dma(P, eng, out, in_, sem, waits=(), **kw):
    sem.inc(16)

    def run(e):
        _w(e, waits)
        e.dma_start(out=out, in_=in_, **kw).then_inc(sem.h, 16)
    P.add(eng, run)


def build_program(phases=("ffn1",), debug=False, adbg=False):
    nc = bass.Bass("TRN2", target_bir_lowering=False)

    x_my = nc.dram_tensor("x_my", [T_OWN, D], F32, kind="ExternalInput")
    wr_in = nc.dram_tensor("wr_in", [D, E], F32, kind="ExternalInput")
    w1_in = nc.dram_tensor("w1_in", [D, HID], F32, kind="ExternalInput")
    w2_in = nc.dram_tensor("w2_in", [HID, D], F32, kind="ExternalInput")
    b1c_in = nc.dram_tensor("b1c_in", [128, 32], F32, kind="ExternalInput")
    b2r_in = nc.dram_tensor("b2r_in", [1, D], F32, kind="ExternalInput")
    brb_in = nc.dram_tensor("brb_in", [128, E], F32, kind="ExternalInput")
    ident_in = nc.dram_tensor("ident_in", [128, 128], F32, kind="ExternalInput")
    iota8_in = nc.dram_tensor("iota8_in", [128, E], F32, kind="ExternalInput")
    selpref_in = nc.dram_tensor("selpref_in", [16, 2], F32, kind="ExternalInput")
    consts_in = nc.dram_tensor("consts_in", [128, 4], F32, kind="ExternalInput")
    wblk_in = nc.dram_tensor("wblk_in", [65, 128, 1024], F32, kind="ExternalInput")
    wv_in = nc.dram_tensor("wv_in", [8, 128, 2048], F32, kind="ExternalInput")
    binc_in = nc.dram_tensor("binc_in", [128, 65], F32, kind="ExternalInput")
    binv_in = nc.dram_tensor("binv_in", [1, 2048], F32, kind="ExternalInput")
    woutT_in = nc.dram_tensor("woutT_in", [2048, 1024], F32, kind="ExternalInput")
    lnw4_in = nc.dram_tensor("lnw4_in", [128, 4 * 1024], F32, kind="ExternalInput")
    hpar_in = nc.dram_tensor("hpar_in", [128, 32], F32, kind="ExternalInput")
    masks_in = nc.dram_tensor("masks_in", [128, 4 * 512], F32, kind="ExternalInput")
    icon_in = nc.dram_tensor("icon_in", [1, 16], I32, kind="ExternalInput")
    y_my = nc.dram_tensor("y_my", [T_OWN, D], F32, kind="ExternalOutput")
    if debug:
        d_ln = nc.dram_tensor("d_ln", [NCORE * T_OWN, D], BF16, kind="ExternalOutput")
        d_logits = nc.dram_tensor("d_logits", [128, NT * E], F32, kind="ExternalOutput")
        d_posg = nc.dram_tensor("d_posg", [E, T_OWN], F32, kind="ExternalOutput")
        d_cnt = nc.dram_tensor("d_cnt", [16, E], F32, kind="ExternalOutput")
        d_offs = nc.dram_tensor("d_offs", [E, 4], F32, kind="ExternalOutput")
        d_map = nc.dram_tensor("d_map", [SLOT_OWN, 1], F32, kind="ExternalOutput")
        d_ybin = nc.dram_tensor("d_ybin", [SLOT_OWN, D], BF16, kind="ExternalOutput")
        d_earr = nc.dram_tensor("d_earr", [128, 2 * NT], F32, kind="ExternalOutput")
        d_parr = nc.dram_tensor("d_parr", [128, 2 * NT], F32, kind="ExternalOutput")
        d_warr = nc.dram_tensor("d_warr", [128, 2 * NT], F32, kind="ExternalOutput")
        d_ggix = nc.dram_tensor("d_ggix", [128, 2 * NT], I32, kind="ExternalOutput")
        d_postm = nc.dram_tensor("d_postm", [128, NT * E], F32, kind="ExternalOutput")
        d_buf = nc.dram_tensor("d_buf", [128, 8 * SLOT_OWN], BF16, kind="ExternalOutput")
        d_hfm = nc.dram_tensor("d_hfm", [128, 32 * 512], BF16, kind="ExternalOutput")
        d_w1 = nc.dram_tensor("d_w1", [128, 8 * HID], BF16, kind="ExternalOutput")

    ln_ag_in = nc.dram_tensor("ln_ag_in", [T_OWN, D], BF16)
    ln_ag_out = nc.dram_tensor("ln_ag_out", [NCORE * T_OWN, D], BF16)
    cnt_ag_in = nc.dram_tensor("cnt_ag_in", [2, E], F32)
    cnt_ag_out = nc.dram_tensor("cnt_ag_out", [16, E], F32)
    rs_in = nc.dram_tensor("rs_in", [NSLOT, 1], F32)
    rs_out = nc.dram_tensor("rs_out", [SLOT_OWN, 1], F32)
    yb_ag_in = nc.dram_tensor("yb_ag_in", [SLOT_OWN, D], BF16)
    yb_ag_out = nc.dram_tensor("yb_ag_out", [NSLOT, D], BF16)
    w2bf = nc.dram_tensor("w2bf", [HID, D], BF16)
    x_res = nc.dram_tensor("x_res", [T_OWN, D], F32)
    kvy_ag_in = nc.dram_tensor("kvy_ag_in", [4224, 2048], BF16)
    kvy_ag_out = nc.dram_tensor("kvy_ag_out", [4 * 4224, 2048], BF16)
    q_fm = nc.dram_tensor("q_fm", [2048, T_OWN], BF16)
    p_fm = nc.dram_tensor("p_fm", [2048, T_OWN], BF16)
    o_fm = nc.dram_tensor("o_fm", [2048, T_OWN], BF16)
    pos_dram = nc.dram_tensor("pos_dram", [16, 8192], F32)
    mneg_dram = nc.dram_tensor("mneg_dram", [16, 2048], BF16)

    ctxs = []

    def S(name):
        h = nc.semaphore(name)
        ctxs.append(h)
        return Sem(h.__enter__())

    def SB(name, shape, dt):
        h = nc.sbuf_tensor(name, shape, dt)
        ctxs.append(h)
        return h.__enter__()

    def PSUM(name, shape, dt):
        h = nc.psum_tensor(name, shape, dt)
        ctxs.append(h)
        return h.__enter__()

    ident = SB("ident", [128, 128], F32)
    identb = SB("identb", [128, 128], BF16)
    iota8 = SB("iota8", [128, E], F32)
    brb = SB("brb", [128, E], F32)
    b1c = SB("b1c", [128, 32], F32)
    b2r = SB("b2r", [1, D], F32)
    selpref = SB("selpref", [16, 2], F32)
    cconst = SB("cconst", [128, 4], F32)
    ones_f = SB("ones_f", [1, 128], F32)
    epsc = SB("epsc", [128, 1], F32)
    wrb = SB("wrb", [128, 8 * E], BF16)
    rowval = SB("rowval", [128, NT], F32)
    zed = SB("zed", [128, 160], F32)
    xt = SB("xt", [128, 2 * D], F32)
    sqt = SB("sqt", [128, D], F32)
    lnt = SB("lnt", [128, 2 * D], BF16)
    stat = SB("stat", [128, 16], F32)
    small = SB("small", [128, 16], F32)
    logits = SB("logits", [128, NT * E], F32)
    parr = SB("parr", [128, 2 * NT], F32)
    warr = SB("warr", [128, 2 * NT], F32)
    earr = SB("earr", [128, 2 * NT], F32)
    ohs1 = SB("ohs1", [128, NT * E], F32)
    ohsb = SB("ohsb", [128, E], BF16)
    cagb = SB("cagb", [16, E], F32)
    offs = SB("offs", [E, 4], F32)
    cnt_sb = SB("cnt_sb", [2, E], F32)
    postm = SB("postm", [128, NT * E], F32)
    gidx = SB("gidx", [128, 2 * NT], I32)
    ggix = SB("ggix", [128, 2 * NT], I32)
    mapcol = SB("mapcol", [128, 20], F32)
    mapi = SB("mapi", [128, 20], I32)
    w2sb = SB("w2sb", [128, 2 * 512], BF16)
    ysb = SB("ysb", [128, 4 * D], BF16)
    ygb = SB("ygb", [128, 2 * D], BF16)
    accv = SB("accv", [128, D], F32)
    binc = SB("binc", [128, 65], F32)
    hpar = SB("hpar", [128, 32], F32)
    qscale = SB("qscale", [128, 16], F32)
    smsig = SB("smsig", [128, 16], F32)
    binqs = SB("binqs", [128, 16], F32)
    icon = SB("icon", [1, 16], I32)
    onescol = SB("onescol", [128, 1], BF16)
    zrow = SB("zrow", [1, 512], F32)
    onesb = SB("onesb", [1, 128], BF16)
    zrowb = SB("zrowb", [1, 512], BF16)
    carry = SB("carry", [16, 2], F32)
    poscol = SB("poscol", [128, 128], F32)
    pc64 = SB("pc64", [64, 128], F32)
    pcstg = SB("pcstg", [128, 8], F32)
    densb = SB("densb", [1, 512], F32)
    recsb = SB("recsb", [1, 512], F32)
    recb = SB("recb", [128, 512], F32)

    arena = SB("arena", [128, 73728], BF16)
    O_LNFM, O_BUFFM, O_W1, O_HFM, O_BUFTM = 0, 0, 20480, 53248, 69632
    ohs_fm = arena[0:E, 53248:55296]
    cinc = arena[0:E, 55296:59392].bitcast(F32)
    posg = arena[0:E, 59392:63488].bitcast(F32)

    tps = PSUM("tps", [128, 128], BF16)
    psA = PSUM("psA", [128, 512], F32)
    psB = PSUM("psB", [128, 512], F32)
    psY = [PSUM(f"psY{i}", [128, 512], F32) for i in range(4)]
    psY0, psY1, psY2, psY3 = psY

    ld = S("ld")
    lx = S("lx")
    gp = S("gp")
    cc = S("cc")
    mm = S("mm")
    dv = S("dv")
    ac = S("ac")
    st = S("st")
    wst = S("wst")
    asg = S("asg")
    amm = S("amm")
    aac = S("aac")
    aen = S("aen")
    adm = S("adm")
    aep = S("aep")
    abc = S("abc")
    aev = S("aev")
    aib = S("aib")
    ahd = S("ahd")
    aos = S("aos")
    akv = S("akv")
    ldp = S("ldp")
    ldm = S("ldm")
    ldq2 = S("ldq2")
    lxx = S("lxx")
    wvs = S("wvs")

    P = Prog()
    regs = {}

    def _initregs(g):
        regs["bc_slot"] = g.to_reg(NSLOT - 1)
        regs["bc_row"] = g.to_reg(NCORE * T_OWN - 1)
    P.add("gpsimd", _initregs)

    dma(P, "sync", ident[:, :], ident_in.ap(), ld)
    dma(P, "sync", iota8[:, :], iota8_in.ap(), ld)
    dma(P, "sync", brb[:, :], brb_in.ap(), ld)
    dma(P, "sync", b1c[:, :], b1c_in.ap(), ld)
    dma(P, "sync", b2r[:, :], b2r_in.ap(), ld)
    dma(P, "sync", selpref[:, :], selpref_in.ap(), ld)
    dma(P, "sync", cconst[:, :], consts_in.ap(), ld)
    dma(P, "gpsimd", wrb[:, :].rearrange("p (c e) -> p c e", e=E),
        wr_in.ap().rearrange("(c p) e -> p c e", p=128), gp)

    @op(P, "gpsimd", inc=gp)
    def _(g):
        g.memset(ones_f[:, :], 1.0)
        g.memset(epsc[:, :], LN_EPS)
        g.memset(zed[:, :], 0.0)
        return g.iota(rowval[:, :], [[128, NT]], channel_multiplier=1,
                      allow_small_or_imprecise_dtypes=True)

    @op(P, "vector", waits=[ld.now, gp.now], inc=dv)
    def _(v):
        v.tensor_copy(identb[:, :], ident[:, :])
        return v.tensor_scalar_add(rowval[:, :], rowval[:, :], cconst[:, 1:2])
    dv_setup = dv.n

    def emit_ffn(src_dram, dst_dram, first_call, entry=(), entry_tiles=None):
        xtv = arena[:, 60416:60416 + 4096].bitcast(F32)
        lx0 = lx.n
        ac_ln = [None] * NT
        st_ln = [None] * NT
        mm_tp = [None] * NT
        for t in range(NT):
            s0 = (t % 2) * D
            sl = slice(s0, s0 + D)
            wx = ([(ac, ac_ln[t - 2])] if t >= 2 else []) + \
                (entry_tiles[t] if entry_tiles else
                 (list(entry) if t < 2 else []))
            dma(P, "scalar", xtv[:, sl], src_dram[t * 128:(t + 1) * 128, :],
                lx, waits=wx)

            @op(P, "vector", waits=[(lx, lx0 + 16 * (t + 1)), (dv, dv_setup)], inc=dv)
            def _(v, sl=sl):
                v.reduce_sum(stat[:, 0:1], xtv[:, sl], axis=AX.X)
                return v.tensor_scalar_mul(stat[:, 1:2], stat[:, 0:1], 1.0 / D)

            @op(P, "scalar", waits=[dv.now], inc=ac)
            def _(s, sl=sl):
                return s.activation(sqt[:, :], xtv[:, sl], AF.Square)

            @op(P, "vector", waits=[ac.now], inc=dv)
            def _(v):
                v.reduce_sum(stat[:, 2:3], sqt[:, :], axis=AX.X)
                v.tensor_scalar_mul(stat[:, 3:4], stat[:, 2:3], 1.0 / D)
                v.tensor_mul(stat[:, 4:5], stat[:, 1:2], stat[:, 1:2])
                return v.tensor_sub(stat[:, 5:6], stat[:, 3:4], stat[:, 4:5])

            @op(P, "scalar", waits=[dv.now], inc=ac)
            def _(s):
                return s.activation(stat[:, 6:7], stat[:, 5:6], AF.Sqrt, bias=epsc[:, :])

            @op(P, "vector", waits=[ac.now], inc=dv)
            def _(v):
                v.reciprocal(stat[:, 7:8], stat[:, 6:7])
                v.tensor_mul(stat[:, 8:9], stat[:, 1:2], stat[:, 7:8])
                return v.tensor_scalar_mul(stat[:, 9:10], stat[:, 8:9], -1.0)

            w = [dv.now]
            if t >= 2:
                w += [(st, st_ln[t - 2]), (mm, mm_tp[t - 2])]

            @op(P, "scalar", waits=w, inc=ac)
            def _(s, sl=sl):
                return s.activation(lnt[:, sl], xtv[:, sl], AF.Identity,
                                    bias=stat[:, 9:10], scale=stat[:, 7:8])
            ac_ln[t] = ac.n

            dma(P, "sync", ln_ag_in[t * 128:(t + 1) * 128, :], lnt[:, sl], st,
                waits=[ac.now])
            st_ln[t] = st.n

            for c in range(8):
                @op(P, "tensor", waits=[ac.now, dv.now], inc=mm)
                def _(te, s0=s0, c=c):
                    return te.transpose(
                        tps[:, :], lnt[:, s0 + c * 128: s0 + (c + 1) * 128],
                        identb[:, :])

                @op(P, "vector", waits=[mm.now], inc=dv)
                def _(v, t=t, c=c):
                    return v.tensor_copy(
                        arena[:, O_LNFM + c * T_OWN + t * 128:
                              O_LNFM + c * T_OWN + (t + 1) * 128], tps[:, :])
            mm_tp[t] = mm.n

            @op(P, "tensor", waits=[dv.now], inc=mm)
            def _(te, t=t):
                r = None
                for c in range(8):
                    r = te.matmul(
                        psA[:, 0:E],
                        arena[:, O_LNFM + c * T_OWN + t * 128:
                              O_LNFM + c * T_OWN + (t + 1) * 128],
                        wrb[:, c * E:(c + 1) * E], start=(c == 0), stop=(c == 7))
                return r

            @op(P, "vector", waits=[mm.now], inc=dv)
            def _(v, t=t):
                return v.tensor_add(logits[:, t * E:(t + 1) * E], psA[:, 0:E],
                                    brb[:, :])
        dv_logits_all = dv.n
        st_ln_all = st.n

        mm_ohs = [None] * NT
        for t in range(NT):
            tE = slice(t * E, (t + 1) * E)
            w = [(mm, mm_ohs[t - 1])] if t >= 1 else []

            @op(P, "vector", waits=w, inc=dv)
            def _(v, tE=tE):
                v.max(out=small[:, 0:8], in_=logits[:, tE])
                v.tensor_scalar_mul(small[:, 8:9], small[:, 0:1], -1.0)
                return v.tensor_sub(small[:, 9:10], small[:, 1:2], small[:, 0:1])

            @op(P, "scalar", waits=[dv.now], inc=ac)
            def _(s, tE=tE):
                s.activation(sqt[:, 0:E], logits[:, tE], AF.Exp, bias=small[:, 8:9])
                return s.activation(sqt[:, E:E + 1], small[:, 9:10], AF.Exp)

            @op(P, "vector", waits=[ac.now], inc=dv)
            def _(v, tE=tE, t=t):
                v.reduce_sum(small[:, 10:11], sqt[:, 0:E], axis=AX.X)
                v.reciprocal(parr[:, t:t + 1], small[:, 10:11])
                v.tensor_mul(parr[:, NT + t:NT + t + 1], sqt[:, E:E + 1],
                             parr[:, t:t + 1])
                v.tensor_scalar(small[:, 0:8], logits[:, tE], small[:, 0:1], None,
                                ALU.is_equal)
                v.tensor_scalar(small[:, 0:8], small[:, 0:8], -1.0, 1.0,
                                ALU.mult, ALU.add)
                v.scalar_tensor_tensor(small[:, 0:8], small[:, 0:8], 1e9,
                                       iota8[:, :], ALU.mult, ALU.add)
                v.tensor_reduce(earr[:, t:t + 1], small[:, 0:8], AX.X, ALU.min)
                v.tensor_scalar(ohs1[:, tE], iota8[:, :], earr[:, t:t + 1], None,
                                ALU.is_equal)
                v.max(out=small[:, 0:8], in_=logits[:, tE])
                v.tensor_scalar(small[:, 8:16], logits[:, tE], small[:, 1:2], None,
                                ALU.is_equal)
                v.tensor_scalar(small[:, 8:16], small[:, 8:16], -1.0, 1.0,
                                ALU.mult, ALU.add)
                v.scalar_tensor_tensor(small[:, 8:16], small[:, 8:16], 1e9,
                                       iota8[:, :], ALU.mult, ALU.add)
                v.scalar_tensor_tensor(small[:, 8:16], ohs1[:, tE], 1e9,
                                       small[:, 8:16], ALU.mult, ALU.add)
                v.tensor_reduce(earr[:, NT + t:NT + t + 1], small[:, 8:16],
                                AX.X, ALU.min)
                v.tensor_scalar(small[:, 0:8], iota8[:, :],
                                earr[:, NT + t:NT + t + 1], None, ALU.is_equal)
                v.tensor_add(ohs1[:, tE], ohs1[:, tE], small[:, 0:8])
                return v.tensor_copy(ohsb[:, :], ohs1[:, tE])

            @op(P, "tensor", waits=[dv.now], inc=mm)
            def _(te):
                return te.transpose(tps[0:E, 0:128], ohsb[:, :], identb[:, :])
            mm_ohs[t] = mm.n

            @op(P, "vector", waits=[mm.now], inc=dv)
            def _(v, t=t):
                return v.tensor_copy(ohs_fm[:, t * 128:(t + 1) * 128],
                                     tps[0:E, 0:128])

        @op(P, "vector", inc=dv)
        def _(v):
            v.tensor_tensor_scan(cinc[:, :], ohs_fm[:, :], ohs_fm[:, :], 0.0,
                                 ALU.add, ALU.bypass)
            v.tensor_copy(offs[:, 2:3], cinc[:, 1023:1024])
            return v.tensor_sub(offs[:, 3:4], cinc[:, 2047:2048],
                                cinc[:, 1023:1024])

        @op(P, "tensor", waits=[dv.now], inc=mm)
        def _(te):
            return te.transpose(psB[0:2, 0:E], offs[:, 2:4], ident[0:E, 0:E])

        @op(P, "vector", waits=[mm.now], inc=dv)
        def _(v):
            return v.tensor_copy(cnt_sb[:, :], psB[0:2, 0:E])

        dma(P, "sync", cnt_ag_in.ap(), cnt_sb[:, :], st, waits=[dv.now])
        st_cnt = st.n

        dma(P, "gpsimd", rs_in.ap().rearrange("(a b) o -> a (b o)", b=160),
            zed[:, :], gp)
        gp_zero = gp.n

        @op(P, "gpsimd", waits=[(st, st_cnt)], inc=cc)
        def _(g):
            return g.collective_compute(
                "AllGather", ALU.bypass, replica_groups=[list(range(NCORE))],
                ins=[cnt_ag_in.ap().opt()], outs=[cnt_ag_out.ap().opt()])
        cc_cnt = cc.n

        @op(P, "gpsimd", waits=[(st, st_ln_all)], inc=cc)
        def _(g):
            return g.collective_compute(
                "AllGather", ALU.bypass, replica_groups=[list(range(NCORE))],
                ins=[ln_ag_in.ap().opt()], outs=[ln_ag_out.ap().opt()])
        cc_lnag = cc.n

        dma(P, "sync", cagb[:, :], cnt_ag_out.ap(), ld, waits=[(cc, cc_cnt)])

        @op(P, "tensor", waits=[ld.now], inc=mm)
        def _(te):
            return te.matmul(psA[0:E, 0:2], cagb[:, :], selpref[:, :],
                             start=True, stop=True)

        @op(P, "vector", waits=[mm.now], inc=dv)
        def _(v):
            v.tensor_copy(offs[:, 0:2], psA[0:E, 0:2])
            v.tensor_add(offs[:, 1:2], offs[:, 0:1], offs[:, 1:2])
            v.tensor_sub(cinc[:, :], cinc[:, :], ohs_fm[:, :])
            v.tensor_scalar_add(posg[:, 0:1024], cinc[:, 0:1024], offs[:, 0:1])
            return v.tensor_scalar_add(posg[:, 1024:2048], cinc[:, 1024:2048],
                                       offs[:, 1:2])

        for t in range(NT):
            @op(P, "tensor", waits=[dv.now], inc=mm)
            def _(te, t=t):
                return te.transpose(psB[:, 0:E], posg[:, t * 128:(t + 1) * 128],
                                    ident[0:E, 0:E])

            @op(P, "vector", waits=[mm.now], inc=dv)
            def _(v, t=t):
                return v.tensor_copy(postm[:, t * E:(t + 1) * E], psB[:, 0:E])

        for t in range(NT):
            tE = slice(t * E, (t + 1) * E)

            @op(P, "vector", inc=dv)
            def _(v, tE=tE, t=t):
                r = None
                for k in range(2):
                    col = k * NT + t
                    ek = earr[:, col:col + 1]
                    v.tensor_scalar(small[:, 0:8], iota8[:, :], ek, None,
                                    ALU.is_equal)
                    v.tensor_mul(small[:, 0:8], small[:, 0:8], postm[:, tE])
                    v.reduce_sum(small[:, 8:9], small[:, 0:8], axis=AX.X)
                    v.tensor_scalar(small[:, 9:10], small[:, 8:9], 1280.0, None,
                                    ALU.is_ge)
                    v.tensor_scalar_min(small[:, 10:11], small[:, 8:9], 1279.0)
                    v.tensor_scalar(small[:, 11:12], ek, 2560.0, None, ALU.mult)
                    v.tensor_scalar_add(small[:, 11:12], small[:, 11:12],
                                        cconst[:, 0:1])
                    v.tensor_add(small[:, 11:12], small[:, 11:12], small[:, 10:11])
                    v.tensor_copy(ggix[:, col:col + 1], small[:, 11:12])
                    v.scalar_tensor_tensor(small[:, 12:13], small[:, 9:10], 1e9,
                                           small[:, 11:12], ALU.mult, ALU.add)
                    v.tensor_copy(gidx[:, col:col + 1], small[:, 12:13])
                    v.tensor_scalar(small[:, 13:14], small[:, 9:10], -0.5, 0.5,
                                    ALU.mult, ALU.add)
                    r = v.tensor_mul(warr[:, col:col + 1], parr[:, col:col + 1],
                                     small[:, 13:14])
                return r

            gp.inc(32)
            dv_t = dv.n

            def scat(g, t=t, dv_t=dv_t, gpz=gp_zero):
                _w(g, [(dv, dv_t), (gp, gpz)])
                for k in range(2):
                    col = k * NT + t
                    g.indirect_dma_start(
                        rs_in[:, :], IOA(gidx[:, col:col + 1], 0),
                        rowval[:, t:t + 1], None,
                        bounds_check=regs["bc_slot"], oob_is_err=False,
                    ).then_inc(gp.h, 16)
            P.add("gpsimd", scat)

        @op(P, "gpsimd", waits=[gp.now], inc=cc)
        def _(g):
            return g.collective_compute(
                "ReduceScatter", ALU.add, replica_groups=[list(range(NCORE))],
                ins=[rs_in.ap().opt()], outs=[rs_out.ap().opt()])
        cc_rs = cc.n

        for k in range(20):
            dma(P, "sync", mapcol[:, k:k + 1], rs_out[k * 128:(k + 1) * 128, :],
                ld, waits=[(cc, cc_rs)] if k == 0 else ())

        @op(P, "vector", waits=[ld.now], inc=dv)
        def _(v):
            v.tensor_scalar_add(mapcol[:, :], mapcol[:, :], -1.0)
            return v.tensor_copy(mapi[:, :], mapcol[:, :])
        dv_map = dv.n

        for c in range(8):
            dma(P, "gpsimd", arena[:, O_W1 + c * HID: O_W1 + (c + 1) * HID],
                w1_in[c * 128:(c + 1) * 128, :], wst,
                waits=list(entry) if c == 0 else ())
        wst_w1 = wst.n
        if first_call:
            dma(P, "gpsimd", w2bf.ap(), w2_in.ap(), wst)
        wst_w2 = wst.n

        mm_prev_tp = 0
        st_prev_y = st.n
        for s in range(5):
            @op(P, "gpsimd",
                waits=[(dv, dv_map)] + ([(mm, mm_prev_tp)] if s else []), inc=gp)
            def _(g):
                return g.memset(arena[:, O_BUFTM:O_BUFTM + 4 * D], 0.0)

            for j in range(4):
                gp.inc(16)

                def gath(g, s=s, j=j):
                    g.indirect_dma_start(
                        arena[:, O_BUFTM + j * D: O_BUFTM + (j + 1) * D], None,
                        ln_ag_out[:, :], IOA(mapi[:, 4 * s + j:4 * s + j + 1], 0),
                        bounds_check=regs["bc_row"], oob_is_err=False,
                    ).then_inc(gp.h, 16)
                P.add("gpsimd", gath)
            gp_gath = gp.n

            for j in range(4):
                for c in range(8):
                    @op(P, "tensor",
                        waits=[(gp, gp_gath), (dv, dv_logits_all), (dv, dv.n)],
                        inc=mm)
                    def _(te, j=j, c=c):
                        return te.transpose(
                            tps[:, :],
                            arena[:, O_BUFTM + j * D + c * 128:
                                  O_BUFTM + j * D + (c + 1) * 128],
                            identb[:, :])

                    @op(P, "vector", waits=[mm.now], inc=dv)
                    def _(v, s=s, j=j, c=c):
                        dst = arena[:, O_BUFFM + c * SLOT_OWN + s * 512 + j * 128:
                                    O_BUFFM + c * SLOT_OWN + s * 512 + (j + 1) * 128]
                        return v.tensor_copy(dst, tps[:, :])
            mm_prev_tp = mm.n
            dv_tp_end = dv.n

            ac_h0 = ac.n
            for hc in range(32):
                hp = psA if hc % 2 == 0 else psB
                w = [(dv, dv_tp_end), (wst, wst_w1)]
                if hc >= 2:
                    w.append((ac, ac_h0 + hc - 1))

                @op(P, "tensor", waits=w, inc=mm)
                def _(te, s=s, hc=hc, hp=hp):
                    r = None
                    for c in range(8):
                        r = te.matmul(
                            hp[:, :],
                            arena[:, O_W1 + c * HID + hc * 128:
                                  O_W1 + c * HID + (hc + 1) * 128],
                            arena[:, O_BUFFM + c * SLOT_OWN + s * 512:
                                  O_BUFFM + c * SLOT_OWN + (s + 1) * 512],
                            start=(c == 0), stop=(c == 7))
                    return r

                @op(P, "scalar", waits=[mm.now], inc=ac)
                def _(sc, hc=hc, hp=hp):
                    return sc.activation(
                        arena[:, O_HFM + hc * 512: O_HFM + (hc + 1) * 512],
                        hp[:, :], AF.Relu, bias=b1c[:, hc:hc + 1])
            ac_h_end = ac.n

            for half in range(2):
                wst_h0 = wst.n
                mm_y0 = mm.n
                for hc in range(32):
                    buf = (hc % 2) * 512
                    w = [(mm, mm_y0 + hc)] if hc >= 2 else [(wst, wst_w2), (mm, mm_y0)]
                    dma(P, "sync", w2sb[:, buf:buf + 512],
                        w2bf[hc * 128:(hc + 1) * 128, half * 512:(half + 1) * 512],
                        wst, waits=w)

                @op(P, "tensor",
                    waits=[(ac, ac_h_end), (dv, dv.n), (st, st_prev_y)], inc=mm)
                def _(te, half=half):
                    r = None
                    for st4 in range(4):
                        r = te.matmul(psY[st4][:, :], ones_f[:, :],
                                      b2r[:, half * 512:(half + 1) * 512],
                                      start=True, stop=False)
                    return r

                for hc in range(32):
                    buf = (hc % 2) * 512

                    @op(P, "tensor", waits=[(wst, wst_h0 + 16 * (hc + 1))], inc=mm)
                    def _(te, hc=hc, buf=buf):
                        r = None
                        for st4 in range(4):
                            r = te.matmul(
                                psY[st4][:, :],
                                arena[:, O_HFM + hc * 512 + st4 * 128:
                                      O_HFM + hc * 512 + (st4 + 1) * 128],
                                w2sb[:, buf:buf + 512],
                                start=False, stop=(hc == 31))
                        return r

                for st4 in range(4):
                    @op(P, "vector", waits=[mm.now], inc=dv)
                    def _(v, st4=st4, half=half):
                        return v.tensor_copy(
                            ysb[:, st4 * D + half * 512: st4 * D + (half + 1) * 512],
                            psY[st4][:, :])

            for st4 in range(4):
                dma(P, "sync",
                    yb_ag_in[s * 512 + st4 * 128: s * 512 + (st4 + 1) * 128, :],
                    ysb[:, st4 * D:(st4 + 1) * D], st, waits=[dv.now])
            st_prev_y = st.n
        st_y_end = st.n

        @op(P, "gpsimd", waits=[(st, st_y_end)], inc=cc)
        def _(g):
            return g.collective_compute(
                "AllGather", ALU.bypass, replica_groups=[list(range(NCORE))],
                ins=[yb_ag_in.ap().opt()], outs=[yb_ag_out.ap().opt()])
        cc_yag = cc.n

        lxc = lx.n
        st_comb = [None] * NT
        for t in range(NT):
            wg = [(cc, cc_yag)]
            if t >= 1:
                wg.append((st, st_comb[t - 1]))
            gp.inc(32)

            def gath2(g, t=t, wg=wg, gpn=gp.n):
                _w(g, wg)
                for k in range(2):
                    col = k * NT + t
                    g.indirect_dma_start(
                        ygb[:, k * D:(k + 1) * D], None,
                        yb_ag_out[:, :], IOA(ggix[:, col:col + 1], 0),
                        bounds_check=regs["bc_slot"], oob_is_err=False,
                    ).then_inc(gp.h, 16)
            P.add("gpsimd", gath2)
            gp_g = gp.n

            w = ([(st, st_comb[t - 2])] if t >= 2 and st_comb[t - 2]
                 else [(mm, mm.n)])
            dma(P, "scalar", xtv[:, (t % 2) * D:(t % 2 + 1) * D],
                src_dram[t * 128:(t + 1) * 128, :], lx, waits=w)

            wv = [(gp, gp_g), lx.now]
            if t >= 1:
                wv.append((st, st_comb[t - 1]))

            @op(P, "vector", waits=wv, inc=dv)
            def _(v, t=t):
                a = slice((t % 2) * D, (t % 2 + 1) * D)
                v.scalar_tensor_tensor(accv[:, 0:D], ygb[:, 0:D], warr[:, t:t + 1],
                                       xtv[:, a], ALU.mult, ALU.add)
                return v.scalar_tensor_tensor(accv[:, 0:D], ygb[:, D:2 * D],
                                              warr[:, NT + t:NT + t + 1],
                                              accv[:, 0:D], ALU.mult, ALU.add)

            dma(P, "sync", dst_dram[t * 128:(t + 1) * 128, :],
                accv[:, 0:D], st, waits=[dv.now])
            st_comb[t] = st.n
        return st_comb


    def emit_attention(src_dram, dst_dram, ffn1_tiles, upto=99):
        O_XNFM = 0
        O_KH = [0, 8192]
        O_VH = [16384, 24576]
        O_WV = 32768
        O_WOUT = 32768
        O_WBLK = 49152
        O_LNW = 49152
        O_KRAW = 51200
        O_KSM = 53248
        O_VSB = 55296
        O_YSM = 56320
        O_KSTG = 51200
        O_PT = 53248
        O_QB = 55296
        O_SIG = 51200
        O_POSQ = 55296
        O_BINV = 59392
        O_MROW = 56320
        O_MNEG = 60416
        O_MASK = 68608
        O_KLR = 70656
        O_KLC = 72704
        O_KLT = 72704
        O_GFM = 0
        O_OSB = 51200
        O_OLNW = 53248

        def A(a, b):
            return arena[:, a:b]

        def Af(a, n):
            return arena[:, a:a + 2 * n].bitcast(F32)

        dma(P, "sync", binc[:, :], binc_in.ap(), ld)
        dma(P, "sync", hpar[:, :], hpar_in.ap(), ld)
        dma(P, "sync", icon[:, :], icon_in.ap(), ld)
        dma(P, "gpsimd", A(O_MASK, O_MASK + 2048), masks_in.ap(), gp)

        @op(P, "gpsimd", inc=gp)
        def _(g):
            g.memset(onescol[:, :], 1.0)
            g.memset(onesb[:, :], 1.0)
            g.memset(zrowb[:, :], 0.0)
            return g.memset(zrow[:, :], 0.0)

        @op(P, "scalar", waits=[ld.now, gp.now], inc=ac)
        def _(sc):
            sc.activation(qscale[:, :], hpar[:, 16:32], AF.Exp, scale=-2.0)
            return sc.activation(smsig[:, :], hpar[:, 0:16], AF.Sigmoid)

        @op(P, "vector", waits=[ac.now], inc=dv)
        def _(v):
            v.tensor_scalar_mul(qscale[:, :], qscale[:, :], float(1.0 / np.sqrt(128.0)))
            return v.tensor_mul(binqs[:, :], binc[:, 0:16], qscale[:, :])
        dv_aconst = dv.n

        dma(P, "gpsimd", A(O_LNW, O_LNW + 1024), lnw4_in[:, 0:1024], gp)
        dma(P, "gpsimd", A(O_LNW + 1024, O_LNW + 2048), lnw4_in[:, 1024:2048], gp)
        gp_lnw = gp.n

        tmpb = ysb[:, 0:1024]
        ac_lnA = [None] * NT
        mm_tpA = [None] * NT
        lx0a = lx.n

        def ln_stats(srcap):
            @op(P, "vector", inc=dv)
            def _(v):
                v.reduce_sum(stat[:, 0:1], srcap, axis=AX.X)
                return v.tensor_scalar_mul(stat[:, 1:2], stat[:, 0:1], 1.0 / D)

            @op(P, "scalar", waits=[dv.now], inc=ac)
            def _(s_):
                return s_.activation(sqt[:, :], srcap, AF.Square)

            @op(P, "vector", waits=[ac.now], inc=dv)
            def _(v):
                v.reduce_sum(stat[:, 2:3], sqt[:, :], axis=AX.X)
                v.tensor_scalar_mul(stat[:, 3:4], stat[:, 2:3], 1.0 / D)
                v.tensor_mul(stat[:, 4:5], stat[:, 1:2], stat[:, 1:2])
                return v.tensor_sub(stat[:, 5:6], stat[:, 3:4], stat[:, 4:5])

            @op(P, "scalar", waits=[dv.now], inc=ac)
            def _(s_):
                return s_.activation(stat[:, 6:7], stat[:, 5:6], AF.Sqrt, bias=epsc[:, :])

            @op(P, "vector", waits=[ac.now], inc=dv)
            def _(v):
                v.reciprocal(stat[:, 7:8], stat[:, 6:7])
                v.tensor_mul(stat[:, 8:9], stat[:, 1:2], stat[:, 7:8])
                return v.tensor_scalar_mul(stat[:, 9:10], stat[:, 8:9], -1.0)

        for t in range(NT):
            s0 = (t % 2) * D
            sl = slice(s0, s0 + D)
            wx = ([(ac, ac_lnA[t - 2])] if t >= 2 else []) + \
                [(st, ffn1_tiles[t])]
            dma(P, "scalar", xt[:, sl], src_dram[t * 128:(t + 1) * 128, :],
                lx, waits=wx)

            @op(P, "vector", waits=[(lx, lx0a + 16 * (t + 1)), (dv, dv_aconst)], inc=dv)
            def _(v, sl=sl):
                v.reduce_sum(stat[:, 0:1], xt[:, sl], axis=AX.X)
                return v.tensor_scalar_mul(stat[:, 1:2], stat[:, 0:1], 1.0 / D)

            @op(P, "scalar", waits=[dv.now], inc=ac)
            def _(s_, sl=sl):
                return s_.activation(sqt[:, :], xt[:, sl], AF.Square)

            @op(P, "vector", waits=[ac.now], inc=dv)
            def _(v):
                v.reduce_sum(stat[:, 2:3], sqt[:, :], axis=AX.X)
                v.tensor_scalar_mul(stat[:, 3:4], stat[:, 2:3], 1.0 / D)
                v.tensor_mul(stat[:, 4:5], stat[:, 1:2], stat[:, 1:2])
                return v.tensor_sub(stat[:, 5:6], stat[:, 3:4], stat[:, 4:5])

            @op(P, "scalar", waits=[dv.now], inc=ac)
            def _(s_):
                return s_.activation(stat[:, 6:7], stat[:, 5:6], AF.Sqrt, bias=epsc[:, :])

            @op(P, "vector", waits=[ac.now], inc=dv)
            def _(v):
                v.reciprocal(stat[:, 7:8], stat[:, 6:7])
                v.tensor_mul(stat[:, 8:9], stat[:, 1:2], stat[:, 7:8])
                return v.tensor_scalar_mul(stat[:, 9:10], stat[:, 8:9], -1.0)

            @op(P, "scalar", waits=[dv.now], inc=ac)
            def _(s_, sl=sl):
                return s_.activation(accv[:, :], xt[:, sl], AF.Identity,
                                     bias=stat[:, 9:10], scale=stat[:, 7:8])
            ac_lnA[t] = ac.n

            ln_stats(accv[:, :])

            @op(P, "scalar", waits=[dv.now], inc=ac)
            def _(s_):
                return s_.activation(sqt[:, :], accv[:, :], AF.Identity,
                                     bias=stat[:, 9:10], scale=stat[:, 7:8])

            w = [ac.now, (gp, gp_lnw)]
            if t >= 2:
                w.append((mm, mm_tpA[t - 2]))

            @op(P, "vector", waits=w, inc=dv)
            def _(v, sl=sl):
                v.tensor_mul(tmpb, sqt[:, :], A(O_LNW, O_LNW + 1024))
                return v.tensor_add(lnt[:, sl], tmpb, A(O_LNW + 1024, O_LNW + 2048))

            for c in range(8):
                @op(P, "tensor", waits=[dv.now], inc=mm)
                def _(te, s0=s0, c=c):
                    return te.transpose(
                        tps[:, :], lnt[:, s0 + c * 128: s0 + (c + 1) * 128],
                        identb[:, :])

                @op(P, "vector", waits=[mm.now], inc=dv)
                def _(v, t=t, c=c):
                    return v.tensor_copy(
                        A(O_XNFM + c * T_OWN + t * 128,
                          O_XNFM + c * T_OWN + (t + 1) * 128), tps[:, :])
            mm_tpA[t] = mm.n
        dv_xnfm = dv.n

        if upto < 2:
            return st.n
        for c in range(8):
            dma(P, "gpsimd", A(O_WV + c * 2048, O_WV + (c + 1) * 2048), wv_in[c],
                wvs, waits=[(dv, dv_xnfm)] if c == 0 else ())
        dma(P, "gpsimd", Af(O_BINV, 2048)[0:1, :], binv_in.ap(), wvs)
        wvs_wv = wvs.n
        VBASE = 2048 * 2048
        gp_prev_vscat = 0
        for tt in range(NT):
            dv_ev = []
            for vc in range(4):
                @op(P, "tensor", waits=[(wvs, wvs_wv), (dv, dv.n)], inc=mm)
                def _(te, tt=tt, vc=vc):
                    te.matmul(psY[vc][:, :], ones_f[:, :],
                              Af(O_BINV, 2048)[0:1, vc * 512:(vc + 1) * 512],
                              start=True, stop=False)
                    r = None
                    for c in range(8):
                        r = te.matmul(
                            psY[vc][:, :],
                            A(O_XNFM + c * T_OWN + tt * 128,
                              O_XNFM + c * T_OWN + (tt + 1) * 128),
                            A(O_WV + c * 2048 + vc * 512,
                              O_WV + c * 2048 + (vc + 1) * 512),
                            start=False, stop=(c == 7))
                    return r

                @op(P, "vector", waits=[mm.now] +
                    ([(gp, gp_prev_vscat)] if gp_prev_vscat and vc == 0 else []),
                    inc=dv)
                def _(v, vc=vc):
                    return v.tensor_copy(
                        arena[:, O_VSB + vc * 512: O_VSB + (vc + 1) * 512],
                        psY[vc][:, :])
            gp_prev_vscat = gp.n + 16
            gp.inc(16)

            def vscat(g, tt=tt, dvn=dv.n):
                _w(g, [(dv, dvn)])
                g.dma_start(
                    out=bass.AP(kvy_ag_in, VBASE + tt * 128,
                                [[16 * 128, 128], [128 * 16 * 128, 16], [1, 128]]),
                    in_=arena[:, O_VSB:O_VSB + 2048].rearrange("p (h d) -> p h d", d=128),
                    allow_slow_non_contiguous=True,
                ).then_inc(gp.h, 16)
            P.add("gpsimd", vscat)
        gp_vscat = gp.n

        order1 = list(range(16, 32)) + [64]
        order2 = list(range(0, 16)) + list(range(48, 64))
        wblk_free = [0, 0]
        evb = ysb[:, 1024:2048]
        stq = [S("stqA"), S("stqB")]
        wbsem = [S("wbA"), S("wbB")]

        def a2_block(oi, o):
            buf = oi % 2
            wload_waits = [(mm, wblk_free[buf])] if wblk_free[buf] else [(dv, dv_xnfm)]
            dma(P, "gpsimd", A(O_WBLK + buf * 1024, O_WBLK + (buf + 1) * 1024),
                wblk_in[o], wbsem[buf], waits=wload_waits)
            wst_o = wbsem[buf].n
            for tb in range(4):
                hp = psA if tb % 2 == 0 else psB

                @op(P, "tensor", waits=[(wbsem[buf], wst_o), (ac, ac.n), (dv, dv.n),
                                        (gp, gp_vscat)], inc=mm)
                def _(te, o=o, tb=tb, buf=buf, hp=hp):
                    r = None
                    for c in range(8):
                        r = te.matmul(
                            hp[:, :],
                            A(O_WBLK + buf * 1024 + c * 128,
                              O_WBLK + buf * 1024 + (c + 1) * 128),
                            A(O_XNFM + c * T_OWN + tb * 512,
                              O_XNFM + c * T_OWN + (tb + 1) * 512),
                            start=(c == 0), stop=(c == 7))
                    return r
                if tb == 3:
                    wblk_free[buf] = mm.n

                if 16 <= o < 32:
                    @op(P, "scalar", waits=[mm.now], inc=ac)
                    def _(s_, o=o, tb=tb, hp=hp):
                        return s_.activation(
                            arena[:, O_KRAW + tb * 512: O_KRAW + (tb + 1) * 512],
                            hp[:, :], AF.Identity, bias=binc[:, o:o + 1])
                elif o == 64:
                    @op(P, "scalar", waits=[mm.now], inc=ac)
                    def _(s_, hp=hp):
                        return s_.activation(arena[0:16, O_YSM:O_YSM + 512],
                                             hp[0:16, :], AF.Identity,
                                             bias=binc[0:16, 64:65])
                    dma(P, "sync", kvy_ag_in[4096:4112, tb * 512:(tb + 1) * 512],
                        arena[0:16, O_YSM:O_YSM + 512], st, waits=[ac.now])
                elif o < 16:
                    eoff = 1024 + (tb % 2) * 512

                    sq = stq[tb % 2]

                    @op(P, "scalar", waits=[mm.now] +
                        ([(sq, sq.n)] if sq.n else []), inc=ac)
                    def _(s_, o=o, hp=hp, eoff=eoff):
                        return s_.activation(ysb[:, eoff:eoff + 512], hp[:, :],
                                             AF.Identity, bias=binqs[:, o:o + 1],
                                             scale=qscale[:, o:o + 1])
                    dma(P, "sync", q_fm[o * 128:(o + 1) * 128, tb * 512:(tb + 1) * 512],
                        ysb[:, eoff:eoff + 512], sq, waits=[ac.now])
                else:
                    eoff = 1024 + (tb % 2) * 512

                    sq = stq[tb % 2]

                    @op(P, "scalar", waits=[mm.now] +
                        ([(sq, sq.n)] if sq.n else []), inc=ac)
                    def _(s_, o=o, hp=hp, eoff=eoff):
                        return s_.activation(ysb[:, eoff:eoff + 512], hp[:, :],
                                             AF.Identity, bias=binc[:, o:o + 1])
                    dma(P, "sync",
                        p_fm[(o - 48) * 128:(o - 47) * 128, tb * 512:(tb + 1) * 512],
                        ysb[:, eoff:eoff + 512], sq, waits=[ac.now])

            if 16 <= o < 32:
                h = o - 16
                @op(P, "vector", waits=[ac.now, (st, st.n)], inc=dv)
                def _(v, h=h):
                    v.tensor_copy(arena[:, O_KLC + h: O_KLC + h + 1],
                                  arena[:, O_KRAW + 1023: O_KRAW + 1024])
                    v.tensor_copy(arena[:, O_KLC + 16 + h: O_KLC + 17 + h],
                                  arena[:, O_KRAW + 2047: O_KRAW + 2048])
                    for hb in range(2):
                        b0 = O_KRAW + hb * 1024
                        s0 = O_KSM + hb * 1024
                        v.tensor_sub(tmpb[:, 0:1023],
                                     arena[:, b0:b0 + 1023],
                                     arena[:, b0 + 1:b0 + 1024])
                        v.scalar_tensor_tensor(
                            arena[:, s0 + 1:s0 + 1024], tmpb[:, 0:1023],
                            smsig[:, h:h + 1], arena[:, b0 + 1:b0 + 1024],
                            ALU.mult, ALU.add)
                        v.tensor_scalar(stat[:, 10:11], smsig[:, h:h + 1], -1.0, 1.0,
                                        ALU.mult, ALU.add)
                        v.tensor_mul(arena[:, s0:s0 + 1],
                                     arena[:, b0:b0 + 1], stat[:, 10:11])
                    return v.tensor_copy(stat[:, 11:12], stat[:, 10:11])
                dma(P, "sync", kvy_ag_in[h * 128:(h + 1) * 128, :],
                    arena[:, O_KSM:O_KSM + 2048], st, waits=[dv.now])

        for oi, o in enumerate(order1):
            a2_block(oi, o)
        for sdx in range(2):
            @op(P, "tensor", waits=[(dv, dv.n)], inc=mm)
            def _(te, sdx=sdx):
                return te.transpose(tps[0:16, 0:128],
                                    arena[:, O_KLC + sdx * 16: O_KLC + (sdx + 1) * 16],
                                    identb[:, :])

            @op(P, "vector", waits=[mm.now], inc=dv)
            def _(v, sdx=sdx):
                return v.tensor_copy(ysb[0:16, sdx * 128:(sdx + 1) * 128],
                                     tps[0:16, 0:128])

            def _klst(e, sdx=sdx, w=dv.now):
                _w(e, [w])
                e.dma_start(out=bass.AP(kvy_ag_in, (4112 + sdx) * 2048,
                                        [[128, 16], [1, 128]]),
                            in_=ysb[0:16, sdx * 128:(sdx + 1) * 128]).then_inc(st.h, 16)
            st.inc(16)
            P.add("sync", _klst)
        st_fm_all = st.n

        if upto < 3:
            return st.n
        @op(P, "gpsimd", waits=[(st, st_fm_all), (gp, gp_vscat)], inc=cc)
        def _(g):
            return g.collective_compute(
                "AllGather", ALU.bypass,
                replica_groups=[[0, 1, 2, 3], [4, 5, 6, 7]],
                ins=[kvy_ag_in.ap().opt()], outs=[kvy_ag_out.ap().opt()])
        cc_kvy = cc.n

        for oj, o in enumerate(order2):
            a2_block(len(order1) + oj, o)
        stq_all = [stq[0].n, stq[1].n]
        mm_a2_end = mm.n
        ac_a2_end = ac.n

        if upto < 4:
            return st.n
        @op(P, "vector", inc=dv)
        def _(v):
            return v.memset(carry[:, :], 0.0)
        for qt in range(4):
            for ci in range(2):
                c = 2 * qt + ci
                rl, sl_ = (min(c, 7 - c)), (0 if c < 4 else 1)
                dma(P, "gpsimd",
                    Af(O_SIG, 2048)[0:16, ci * 1024:(ci + 1) * 1024],
                    kvy_ag_out[rl * 4224 + 4096: rl * 4224 + 4112,
                               sl_ * 1024:(sl_ + 1) * 1024],
                    gp, waits=[(cc, cc_kvy)] + ([(st, st.n)] if qt else []))

            @op(P, "scalar", waits=[gp.now], inc=ac)
            def _(s_):
                return s_.activation(Af(O_POSQ, 2048)[0:16, :],
                                     Af(O_SIG, 2048)[0:16, :], AF.Sigmoid)

            @op(P, "vector", waits=[ac.now], inc=dv)
            def _(v, qt=qt):
                v.tensor_tensor_scan(Af(O_SIG, 2048)[0:16, :],
                                     Af(O_POSQ, 2048)[0:16, :],
                                     Af(O_POSQ, 2048)[0:16, :],
                                     0.0 if qt == 0 else carry[:, 0:1],
                                     ALU.add, ALU.bypass)
                v.drain()
                return v.tensor_copy(carry[:, 0:1], Af(O_SIG, 2048)[0:16, 2047:2048])

            dma(P, "sync", pos_dram[:, qt * 2048:(qt + 1) * 2048],
                Af(O_SIG, 2048)[0:16, :], st, waits=[dv.now])
        st_pos = st.n
        st_q = st_pos

        def a45_init_sy(sy):
            R["sy_cA"] = sy.to_reg(sy.value_load(icon[0:1, 4:5]))
            R["sy_cB"] = sy.to_reg(sy.value_load(icon[0:1, 5:6]))
            _w(sy, [(st, st_pos)])
            sy.dma_start(out=Af(O_MROW, 2048)[0:16, 0:1024],
                         in_=bass.AP(pos_dram, R["sy_cA"], [[8192, 16], [1, 1024]])
                         ).then_inc(ldm.h, 16)
            sy.dma_start(out=Af(O_MROW, 2048)[0:16, 1024:2048],
                         in_=bass.AP(pos_dram, R["sy_cB"], [[8192, 16], [1, 1024]])
                         ).then_inc(ldm.h, 16)
        R = {}
        ldm.inc(32)
        P.add("sync", a45_init_sy)

        @op(P, "vector", waits=[ldm.now], inc=dv)
        def _(v):
            return v.tensor_scalar(ysb[0:16, 2048:4096],
                                   Af(O_MROW, 2048)[0:16, :], -1.0, -24.0,
                                   ALU.mult, ALU.add)

        dma(P, "sync", mneg_dram.ap(), ysb[0:16, 2048:4096], st,
            waits=[dv.now])
        st_mneg = st.n

        if upto < 5:
            return st.n
        KSTG = [arena[:, O_KSTG + i * 256: O_KSTG + i * 256 + 128] for i in range(8)]
        VSTG = [arena[:, O_KSTG + i * 256 + 128: O_KSTG + (i + 1) * 256]
                for i in range(8)]
        PT = [arena[:, O_PT + i * 512: O_PT + (i + 1) * 512] for i in range(4)]
        SB4 = [psA, psB, psY3, psY2]
        O_MNEGB = 60416

        def a5_init_pe(te):
            R["pe_rb"] = te.alloc_register(name="pe_rb")
            R["pe_t1"] = te.alloc_register(name="pe_t1")
            R["pe_c"] = te.alloc_register(name="pe_c")
            R["pe_psp"] = te.alloc_register(name="pe_psp")
            te.reg_mov(R["pe_rb"], 0)
            R["pe_tr"] = [te.value_load(icon[0:1, i:i + 1]) for i in range(4)]
        P.add("tensor", a5_init_pe)

        def a5_init_ac(sc):
            R["ac_ra"] = sc.alloc_register(name="ac_ra")
            R["ac_t1"] = sc.alloc_register(name="ac_t1")
            R["ac_c"] = sc.alloc_register(name="ac_c")
            R["ac_psp"] = sc.alloc_register(name="ac_psp")
            sc.reg_mov(R["ac_ra"], 0)
            R["ac_tr"] = [sc.value_load(icon[0:1, i:i + 1]) for i in range(4)]
        P.add("scalar", a5_init_ac)

        def dv_mega(v):
            rj = v.alloc_register(name="m_rj")
            rp = v.alloc_register(name="m_rp")
            rs = v.alloc_register(name="m_rs")
            bl = v.alloc_register(name="m_bl")
            bk1 = v.alloc_register(name="m_bk1")
            ibc = v.alloc_register(name="m_ibc")
            hpar = v.alloc_register(name="m_hp")
            t1 = v.alloc_register(name="m_t1")
            cr = v.alloc_register(name="m_cr")
            sp = v.alloc_register(name="m_sp")
            trips = v.value_load(icon[0:1, 6:7])
            v.reg_mov(rs, 4)
            v.reg_mov(bl, 4)
            v.reg_mov(bk1, 0)
            v.reg_mov(ibc, 3)
            v.reg_mov(hpar, 1)
            v.reg_mov(sp, 0)
            with dynloop(nc, v, cr, trips):
                with v.If_eq(bl, 4):
                    with v.If_ne(bk1, 0):
                        v.reg_add(t1, rs, -4)
                        v.wait_ge(amm.h, t1)
                        for dt in range(3):
                            v.tensor_tensor(
                                SB4[dt][:, :], SB4[dt][:, :],
                                A(O_MASK + dt * 512, O_MASK + (dt + 1) * 512),
                                ALU.add)
                        v.drain()
                        v.tensor_tensor(
                            SB4[3][:, :], SB4[3][:, :],
                            A(O_MASK + 3 * 512, O_MASK + 4 * 512),
                            ALU.add).then_inc(adm.h, 4)
                        v.reg_add(t1, rs, -4)
                        v.wait_ge(aen.h, t1)
                        v.tensor_copy(densb[:, :], psY1[0:1, :])
                        v.drain()
                        v.reciprocal(recsb[:, :], densb[:, :]).then_inc(aep.h, 1)
                        v.wait_ge(abc.h, bk1)
                        v.tensor_copy(recb[:, :], psY2[:, :])
                        v.drain()
                        v.reg_mul(t1, bk1, 16)
                        v.reg_add(t1, t1, -16)
                        v.wait_ge(aos.h, t1)
                        v.tensor_tensor(ysb[:, 0:512], psY0[:, :], recb[:, :],
                                        ALU.mult).then_inc(aev.h, 1)
                    v.reg_add(ibc, ibc, 1)
                    with v.If_eq(ibc, 4):
                        v.reg_mov(ibc, 0)
                        v.reg_mov(t1, 1)
                        v.reg_sub(hpar, t1, hpar)
                    v.reg_add(t1, ibc, 8)
                    v.tensor_copy(mapi[0:1, 19:20],
                                  bass.AP(icon, t1, [[16, 1], [1, 1]]))
                    v.drain()
                    nj_sv = v.value_load(mapi[0:1, 19:20])
                    v.reg_add(bl, nj_sv, 4)
                    v.reg_mul(t1, hpar, 8192)
                    v.reg_add(rj, t1, 0)
                    v.reg_mul(rp, hpar, 64)
                    v.reg_mov(sp, 0)
                    v.reg_add(bk1, bk1, 1)
                    v.reg_add(t1, bk1, 3)
                    v.reg_alu(t1, t1, 2, op=ALU.logical_shift_right)
                    v.wait_ge(akv.h, t1)
                with v.Else():
                    v.reg_add(t1, rs, -8)
                    v.reg_alu(t1, t1, 0, op=ALU.max)
                    v.wait_ge(aen.h, t1)
                    with v.If_eq(sp, 0):
                        v.tensor_copy(pcstg[:, 0:4],
                                      bass.AP(poscol, rp, [[128, 128], [1, 4]]))
                        v.drain()
                        v.tensor_copy(
                            arena[:, O_KSTG: O_KSTG + 1024]
                            .rearrange("p (s kv d) -> p s kv d", s=4, kv=2, d=128),
                            bass.AP(arena, rj,
                                    [[73728, 128], [128, 4], [16384, 2], [1, 128]])
                        ).then_inc(asg.h, 4)
                        v.reg_mov(sp, 1)
                    with v.Else():
                        v.tensor_copy(pcstg[:, 4:8],
                                      bass.AP(poscol, rp, [[128, 128], [1, 4]]))
                        v.drain()
                        v.tensor_copy(
                            arena[:, O_KSTG + 1024: O_KSTG + 2048]
                            .rearrange("p (s kv d) -> p s kv d", s=4, kv=2, d=128),
                            bass.AP(arena, rj,
                                    [[73728, 128], [128, 4], [16384, 2], [1, 128]])
                        ).then_inc(asg.h, 4)
                        v.reg_mov(sp, 0)
                    v.reg_add(rj, rj, 512)
                    v.reg_add(rp, rp, 4)
                    v.reg_add(rs, rs, 4)
                    v.reg_add(bl, bl, -4)
        P.add("vector", dv_mega) if A5PARTS[0] else None
        aep.inc(64)
        aev.inc(64)

        mm_tp_prev = 0
        PENDING_OS = [None]
        for h in range(16):
            hb = h % 2
            wh = [(ahd, h - 1), (cc, cc_kvy), (st, st_pos)]
            if h < 2:
                wh.append((mm, mm_a2_end))
            for c in range(8):
                rl, slt = min(c, 7 - c), (0 if c < 4 else 1)
                dma(P, "gpsimd", A(O_KH[hb] + c * 1024, O_KH[hb] + (c + 1) * 1024),
                    kvy_ag_out[rl * 4224 + h * 128: rl * 4224 + (h + 1) * 128,
                               slt * 1024:(slt + 1) * 1024], ld,
                    waits=wh if c == 0 else ())
                voff = (rl * 4224 + 2048) * 2048 + h * 262144 + slt * 8 * 128
                dma(P, "gpsimd", A(O_VH[hb] + c * 1024, O_VH[hb] + (c + 1) * 1024),
                    bass.AP(kvy_ag_out, voff, [[2048, 128], [128, 8], [1, 128]]), ld)
            dma(P, "gpsimd", pc64[:, :],
                bass.AP(pos_dram, h * 8192, [[128, 64], [1, 128]]), ldp,
                waits=[(st, st_pos)] + ([(mm, mm_tp_prev)] if h else []))
            if h == 0:
                for rl in range(4):
                    dma(P, "gpsimd", arena[rl * 2:(rl + 1) * 2, O_KLR:O_KLR + 2048],
                        kvy_ag_out[rl * 4224 + 4112: rl * 4224 + 4114, :], ld)
            ld_h = ld.n

            dma(P, "gpsimd", arena[0:1, O_MNEGB + hb * 2048: O_MNEGB + (hb + 1) * 2048],
                mneg_dram[h:h + 1, :], ldm,
                waits=[(st, st_mneg), (ahd, h - 1)])

            mm.inc(2)

            def pe_tp(te, h=h, hb=hb, tgt=mm.n, ldh=ld_h, ldp_n=ldp.n,
                      w2=([(ac, ac_a2_end)] if h < 2 else [])):
                _w(te, [(ld, ldh), (ldp, ldp_n), (ahd, h - 1), (aac, 136 * h)] + w2)
                te.transpose(psA[:, 0:64], pc64[:, :], ident[0:64, 0:64]
                             ).then_inc(mm.h, 1)
                te.transpose(tps[:, 0:8],
                             arena[0:8, O_KLR + h * 128: O_KLR + (h + 1) * 128],
                             identb[0:8, 0:8]).then_inc(mm.h, 1)
            P.add("tensor", pe_tp) if A5PARTS[4] else None
            mm_tp_h = mm.n
            mm_tp_prev = mm_tp_h

            akv.inc(1)

            def ac_prep(sc, h=h, hb=hb, mmh=mm_tp_h, ldh=ld_h):
                _w(sc, [(mm, mmh), (ld, ldh)])
                sc.activation(poscol[:, hb * 64:(hb + 1) * 64], psA[:, 0:64],
                              AF.Identity)
                sc.activation(arena[:, O_KLT + hb * 8: O_KLT + (hb + 1) * 8],
                              tps[:, 0:8], AF.Identity)
                r = None
                for c in range(1, 8):
                    pc_ = c - 1
                    idx = min(pc_, 7 - pc_) * 2 + (0 if pc_ < 4 else 1)
                    col = A(O_KH[hb] + c * 1024, O_KH[hb] + c * 1024 + 1)
                    r = sc.activation(
                        col,
                        arena[:, O_KLT + hb * 8 + idx: O_KLT + hb * 8 + idx + 1],
                        AF.Identity, bias=col, scale=smsig[:, h:h + 1])
                r.then_inc(akv.h, 1)
            P.add("scalar", ac_prep) if A5PARTS[5] else None

            for ib in range(4):
                g = h * 4 + ib
                dma(P, "sync",
                    A(O_QB + (g % 2) * 512, O_QB + (g % 2 + 1) * 512),
                    q_fm[h * 128:(h + 1) * 128, ib * 512:(ib + 1) * 512], ldq2,
                    waits=[(aib, g - 1), (stq[0], stq_all[0]),
                           (stq[1], stq_all[1]), (st, st_q)])
                if PENDING_OS[0] is not None:
                    P.add("sync", PENDING_OS[0])
                    PENDING_OS[0] = None
                ldq_n = ldq2.n
                qslice = A(O_QB + (g % 2) * 512, O_QB + (g % 2 + 1) * 512)
                mrow = arena[0:1, O_MNEGB + hb * 2048 + ib * 512:
                             O_MNEGB + hb * 2048 + (ib + 1) * 512]

                def pe_blk(te, h=h, hb=hb, ib=ib, g=g, ldq=ldq_n, qslice=qslice,
                           mrow=mrow, ldm_n=32 + 16 * (h + 1)):
                    rb, t1, psp = R["pe_rb"], R["pe_t1"], R["pe_psp"]
                    _w(te, [(ldq2, ldq), (ldm, ldm_n), (aev, g), (aep, g)])
                    te.matmul(psY0[:, :], onesb[0:1, :], zrowb[:, :],
                              start=True, stop=False)
                    te.matmul(psY1[0:1, :], onesb[0:1, 0:1], zrowb[:, :],
                              start=True, stop=False)
                    te.reg_add(t1, rb, 4)
                    te.wait_ge(asg.h, t1)
                    for s in range(4):
                        te.matmul(SB4[s][:, :], onesb[0:1, :], mrow,
                                  start=True, stop=False)
                        te.matmul(SB4[s][:, :], KSTG[s], qslice,
                                  start=False, stop=True).then_inc(amm.h, 1)
                    te.reg_mov(psp, 1)
                    with dynloop(nc, te, R["pe_c"], R["pe_tr"][ib]):
                        te.reg_add(t1, rb, 8)
                        te.wait_ge(asg.h, t1)
                        with te.If_eq(psp, 0):
                            for s in range(4):
                                te.reg_add(t1, rb, s + 1)
                                te.wait_ge(aac.h, t1)
                                te.matmul(SB4[s][:, :], onesb[0:1, :], mrow,
                                          start=True, stop=False)
                                te.matmul(SB4[s][:, :], KSTG[s], qslice,
                                          start=False, stop=True).then_inc(amm.h, 1)
                                te.matmul(psY0[:, :], VSTG[4 + s], PT[s],
                                          start=False, stop=False)
                                te.matmul(psY1[0:1, :], onescol[:, :], PT[s],
                                          start=False, stop=False
                                          ).then_inc(aen.h, 1)
                            te.reg_mov(psp, 1)
                        with te.Else():
                            for s in range(4):
                                te.reg_add(t1, rb, s + 1)
                                te.wait_ge(aac.h, t1)
                                te.matmul(SB4[s][:, :], onesb[0:1, :], mrow,
                                          start=True, stop=False)
                                te.matmul(SB4[s][:, :], KSTG[4 + s], qslice,
                                          start=False, stop=True).then_inc(amm.h, 1)
                                te.matmul(psY0[:, :], VSTG[s], PT[s],
                                          start=False, stop=False)
                                te.matmul(psY1[0:1, :], onescol[:, :], PT[s],
                                          start=False, stop=False
                                          ).then_inc(aen.h, 1)
                            te.reg_mov(psp, 0)
                        te.reg_add(rb, rb, 4)
                    with te.If_eq(psp, 0):
                        for s in range(4):
                            te.reg_add(t1, rb, s + 1)
                            te.wait_ge(aac.h, t1)
                            te.matmul(psY0[:, :], VSTG[4 + s], PT[s],
                                      start=False, stop=False)
                            te.matmul(psY1[0:1, :], onescol[:, :], PT[s],
                                      start=False, stop=False).then_inc(aen.h, 1)
                    with te.Else():
                        for s in range(4):
                            te.reg_add(t1, rb, s + 1)
                            te.wait_ge(aac.h, t1)
                            te.matmul(psY0[:, :], VSTG[s], PT[s],
                                      start=False, stop=False)
                            te.matmul(psY1[0:1, :], onescol[:, :], PT[s],
                                      start=False, stop=False).then_inc(aen.h, 1)
                    te.reg_add(rb, rb, 4)
                    te.wait_ge(aep.h, g + 1)
                    te.matmul(psY2[:, :], ones_f[:, :], recsb[:, :],
                              start=True, stop=True).then_inc(abc.h, 1)
                    te.nop().then_inc(aib.h, 1)
                P.add("tensor", pe_blk) if A5PARTS[1] else None
                abc.inc(1)
                aib.inc(1)

                def ac_blk(sc, h=h, ib=ib, g=g):
                    ra, t1, apsp = R["ac_ra"], R["ac_t1"], R["ac_psp"]
                    sc.reg_mov(apsp, 0)
                    with dynloop(nc, sc, R["ac_c"], R["ac_tr"][ib]):
                        sc.wait_ge(aen.h, ra)
                        with sc.If_eq(apsp, 0):
                            for s in range(4):
                                sc.reg_add(t1, ra, s + 1)
                                sc.wait_ge(amm.h, t1)
                                sc.activation(PT[s], SB4[s][:, :], AF.Exp,
                                              bias=pcstg[:, s:s + 1]
                                              ).then_inc(aac.h, 1)
                            sc.reg_mov(apsp, 1)
                        with sc.Else():
                            for s in range(4):
                                sc.reg_add(t1, ra, s + 1)
                                sc.wait_ge(amm.h, t1)
                                sc.activation(PT[s], SB4[s][:, :], AF.Exp,
                                              bias=pcstg[:, 4 + s:5 + s]
                                              ).then_inc(aac.h, 1)
                            sc.reg_mov(apsp, 0)
                        sc.reg_add(ra, ra, 4)
                    sc.wait_ge(aen.h, ra)
                    sc.wait_ge(adm.h, 4 * (g + 1))
                    with sc.If_eq(apsp, 0):
                        for s in range(4):
                            sc.activation(PT[s], SB4[s][:, :], AF.Exp,
                                          bias=pcstg[:, s:s + 1]
                                          ).then_inc(aac.h, 1)
                    with sc.Else():
                        for s in range(4):
                            sc.activation(PT[s], SB4[s][:, :], AF.Exp,
                                          bias=pcstg[:, 4 + s:5 + s]
                                          ).then_inc(aac.h, 1)
                    sc.reg_add(ra, ra, 4)
                P.add("scalar", ac_blk) if A5PARTS[2] else None

                aos.inc(16)

                def o_store(sy, h=h, ib=ib, g=g):
                    _w(sy, [(aev, g + 1)])
                    sy.dma_start(out=o_fm[h * 128:(h + 1) * 128,
                                          ib * 512:(ib + 1) * 512],
                                 in_=ysb[:, 0:512]).then_inc(aos.h, 16)
                PENDING_OS[0] = o_store if A5PARTS[3] else None

            ahd.inc(1)

            def pe_hdone(te):
                te.nop().then_inc(ahd.h, 1)
            P.add("tensor", pe_hdone)
        if PENDING_OS[0] is not None:
            P.add("sync", PENDING_OS[0])
            PENDING_OS[0] = None
        ahd_all = ahd.n
        aos_all = aos.n

        akv.inc(1)

        def ac_pad(sc):
            sc.nop().then_inc(akv.h, 1)
        P.add("scalar", ac_pad)


        if upto < 6:
            return aos.n if False else st.n
        for de in range(16):
            dma(P, "gpsimd", A(O_WOUT + de * 1024, O_WOUT + (de + 1) * 1024),
                woutT_in[de * 128:(de + 1) * 128, :], wst,
                waits=[(ahd, ahd_all)] if de == 0 else ())
        dma(P, "sync", Af(O_OLNW, 1024), lnw4_in[:, 2048:3072], ldp,
            waits=[(ahd, ahd_all)])
        dma(P, "sync", Af(55296, 1024), lnw4_in[:, 3072:4096], ldp)
        wst_wo = wst.n
        ld_oln = ldp.n
        lxp = [S("lxA6a"), S("lxA6b")]

        lx_o = lx.n
        for hc in range(16):
            dma(P, "sync", ysb[:, 2048:4096],
                p_fm[hc * 128:(hc + 1) * 128, :], ld,
                waits=[(ahd, ahd_all), (aos, aos_all), (stq[0], stq_all[0]),
                       (stq[1], stq_all[1])] + ([(dv, dv.n)] if hc else []))
            dma(P, "scalar", lnt[:, 0:2048], o_fm[hc * 128:(hc + 1) * 128, :], lx,
                waits=[(dv, dv.n)] if hc else [])

            @op(P, "scalar", waits=[ld.now, lx.now], inc=ac)
            def _(s_):
                return s_.activation(ysb[:, 0:2048], ysb[:, 2048:4096], AF.Silu)

            @op(P, "vector", waits=[ac.now], inc=dv)
            def _(v, hc=hc):
                return v.tensor_mul(A(O_GFM + hc * 2048, O_GFM + (hc + 1) * 2048),
                                    ysb[:, 0:2048], lnt[:, 0:2048])
        dv_gfm = dv.n

        st_a6 = [None] * NT
        for tt in range(NT):
            for half in range(2):
                hp2 = psY2 if half == 0 else psY3

                @op(P, "tensor", waits=[(dv, dv_gfm), (wst, wst_wo)] +
                    ([(dv, dv.n)] if (tt or half) else []), inc=mm)
                def _(te, tt=tt, half=half, hp2=hp2):
                    r = None
                    for de in range(16):
                        r = te.matmul(
                            hp2[:, :],
                            A(O_GFM + de * 2048 + tt * 128,
                              O_GFM + de * 2048 + (tt + 1) * 128),
                            A(O_WOUT + de * 1024 + half * 512,
                              O_WOUT + de * 1024 + (half + 1) * 512),
                            start=(de == 0), stop=(de == 15))
                    return r

            @op(P, "vector", waits=[mm.now], inc=dv)
            def _(v):
                v.tensor_copy(Af(O_OSB, 1024)[:, 0:512], psY2[:, :])
                return v.tensor_copy(Af(O_OSB, 1024)[:, 512:1024], psY3[:, :])

            ln_stats(Af(O_OSB, 1024))

            @op(P, "scalar", waits=[dv.now], inc=ac)
            def _(s_):
                return s_.activation(sqt[:, :], Af(O_OSB, 1024), AF.Identity,
                                     bias=stat[:, 9:10], scale=stat[:, 7:8])

            wxx = [(st, st_a6[tt - 2])] if tt >= 2 and st_a6[tt - 2] else []
            dma(P, "scalar", xt[:, (tt % 2) * D:(tt % 2 + 1) * D],
                x_res[tt * 128:(tt + 1) * 128, :], lxp[tt % 2], waits=wxx)

            @op(P, "vector", waits=[ac.now, lxp[tt % 2].now, (ldp, ld_oln)] +
                ([(st, st_a6[tt - 1])] if tt >= 1 and st_a6[tt - 1] else []), inc=dv)
            def _(v, tt=tt):
                v.tensor_mul(accv[:, :], sqt[:, :], Af(O_OLNW, 1024))
                v.tensor_add(accv[:, :], accv[:, :], Af(55296, 1024))
                return v.tensor_add(accv[:, :], accv[:, :],
                                    xt[:, (tt % 2) * D:(tt % 2 + 1) * D])

            dma(P, "sync", dst_dram[tt * 128:(tt + 1) * 128, :], accv[:, :], st,
                waits=[dv.now])
            st_a6[tt] = st.n
        return st_a6
    if phases == ("ffn1",):
        final_st = emit_ffn(x_my.ap(), y_my.ap(), first_call=True)[-1]
    elif phases == ("ffn1", "attn"):
        f1 = emit_ffn(x_my.ap(), x_res.ap(), first_call=True)
        final_st = emit_attention(x_res.ap(), y_my.ap(), f1, upto=UPTO[0])
        final_st = final_st[-1] if isinstance(final_st, list) else final_st
    elif phases == ("ffn1", "attn", "ffn2"):
        f1 = emit_ffn(x_my.ap(), x_res.ap(), first_call=True)
        a6 = emit_attention(x_res.ap(), x_res.ap(), f1)
        final_st = emit_ffn(x_res.ap(), y_my.ap(), first_call=False,
                            entry=[(st, a6[-1])],
                            entry_tiles=[[(st, a6[t])] for t in range(NT)])[-1]
    else:
        raise NotImplementedError(phases)

    if adbg:
        da_q = nc.dram_tensor("da_q", [2048, T_OWN], BF16, kind="ExternalOutput")
        da_p = nc.dram_tensor("da_p", [2048, T_OWN], BF16, kind="ExternalOutput")
        da_o = nc.dram_tensor("da_o", [2048, T_OWN], BF16, kind="ExternalOutput")
        da_kvy = nc.dram_tensor("da_kvy", [4 * 4224, 2048], BF16,
                                kind="ExternalOutput")
        da_pos = nc.dram_tensor("da_pos", [16, 8192], F32, kind="ExternalOutput")
        da_mneg = nc.dram_tensor("da_mneg", [16, 2048], BF16,
                                 kind="ExternalOutput")
        da_xr = nc.dram_tensor("da_xr", [T_OWN, D], F32, kind="ExternalOutput")
        dbg = S("dbg")
        dma(P, "scalar", da_q.ap(), q_fm.ap(), dbg, waits=[(st, final_st)])
        dma(P, "scalar", da_p.ap(), p_fm.ap(), dbg)
        dma(P, "scalar", da_o.ap(), o_fm.ap(), dbg)
        dma(P, "scalar", da_kvy.ap(), kvy_ag_out.ap(), dbg)
        dma(P, "scalar", da_pos.ap(), pos_dram.ap(), dbg)
        dma(P, "scalar", da_mneg.ap(), mneg_dram.ap(), dbg)
        dma(P, "scalar", da_xr.ap(), x_res.ap(), dbg)

        @op(P, "sync", waits=[dbg.now, (st, final_st)])
        def _(sy):
            pass
    elif debug:
        dbg = S("dbg")
        dma(P, "scalar", d_ln.ap(), ln_ag_out.ap(), dbg, waits=[(st, final_st)])
        dma(P, "scalar", d_logits.ap(), logits[:, :], dbg)
        dma(P, "scalar", d_posg.ap(), posg[:, :], dbg)
        dma(P, "scalar", d_cnt.ap(), cagb[:, :], dbg)
        dma(P, "scalar", d_offs.ap(), offs[:, :], dbg)
        dma(P, "scalar", d_map.ap(), rs_out.ap(), dbg)
        dma(P, "scalar", d_ybin.ap(), yb_ag_in.ap(), dbg)
        dma(P, "scalar", d_earr.ap(), earr[:, :], dbg)
        dma(P, "scalar", d_parr.ap(), parr[:, :], dbg)
        dma(P, "scalar", d_warr.ap(), warr[:, :], dbg)
        dma(P, "scalar", d_ggix.ap(), ggix[:, :], dbg)
        dma(P, "scalar", d_postm.ap(), postm[:, :], dbg)
        dma(P, "scalar", d_buf.ap(), arena[:, O_BUFFM:O_BUFFM + 8 * SLOT_OWN], dbg)
        dma(P, "scalar", d_hfm.ap(), arena[:, O_HFM:O_HFM + 32 * 512], dbg)
        dma(P, "scalar", d_w1.ap(), arena[:, O_W1:O_W1 + 8 * HID], dbg)
        final_st = None

        @op(P, "sync", waits=[dbg.now])
        def _(sy):
            pass
    else:
        @op(P, "sync", waits=[(st, final_st)])
        def _(sy):
            pass

    bb = nc.Block()
    blk = bb.__enter__()
    ctxs.append(bb)
    for name in ENGINES:
        fns = P.ops[name]
        getattr(blk, name)(lambda e, fns=fns: [f(e) for f in fns])

    for c in reversed(ctxs):
        c.__exit__(None, None, None)
    return nc


def _stage_inputs(inputs):
    x = np.asarray(inputs["L0_attn_x"])
    wr = np.ascontiguousarray(np.asarray(inputs["L0_ffn_Wr"]), dtype=np.float32)
    w_in = np.asarray(inputs["L0_attn_W_in"], dtype=np.float32)
    b_in = np.asarray(inputs["L0_attn_b_in"], dtype=np.float32)
    w_out = np.asarray(inputs["L0_attn_W_out"], dtype=np.float32)
    ilw = np.asarray(inputs["L0_attn_in_ln_w"], dtype=np.float32)
    ilb = np.asarray(inputs["L0_attn_in_ln_b"], dtype=np.float32)
    olw = np.asarray(inputs["L0_attn_out_ln_w"], dtype=np.float32)
    olb = np.asarray(inputs["L0_attn_out_ln_b"], dtype=np.float32)
    smear = np.asarray(inputs["L0_attn_smear_factor"], dtype=np.float32)
    lsc = np.asarray(inputs["L0_attn_log_scale"], dtype=np.float32)

    w_in_pad = np.zeros((8320, 1024), np.float32)
    w_in_pad[:8208] = w_in
    b_in_pad = np.zeros(8320, np.float32)
    b_in_pad[:8208] = b_in
    wblk = np.ascontiguousarray(
        w_in_pad.reshape(65, 128, 8, 128).transpose(0, 3, 2, 1).reshape(65, 128, 1024))
    wv = np.ascontiguousarray(
        w_in_pad[4096:6144].T.reshape(8, 128, 2048))
    binc = np.ascontiguousarray(b_in_pad.reshape(65, 128).T)
    binv = np.ascontiguousarray(b_in_pad[4096:6144].reshape(1, 2048))
    woutT = np.ascontiguousarray(w_out.T)
    lnw4 = np.ascontiguousarray(np.broadcast_to(
        np.concatenate([ilw, ilb, olw, olb]), (128, 4096)))
    hpar = np.ascontiguousarray(np.broadcast_to(
        np.concatenate([smear, lsc]), (128, 32)))
    masks = np.zeros((128, 4, 512), np.float32)
    jj = np.arange(128)[:, None]
    ii = np.arange(512)[None, :]
    for k in range(4):
        masks[:, k, :] = np.where(128 * k + jj > ii, -1e10, 0.0)
    masks = np.ascontiguousarray(masks.reshape(128, 2048))
    w1 = np.asarray(inputs["L0_ffn_W1"])
    w2 = np.asarray(inputs["L0_ffn_W2"])
    b1 = np.asarray(inputs["L0_ffn_b1"])
    b2 = np.asarray(inputs["L0_ffn_b2"])
    br = np.asarray(inputs["L0_ffn_br"])

    ident = np.eye(128, dtype=np.float32)
    iota8 = np.broadcast_to(np.arange(E, dtype=np.float32), (128, E)).copy()
    brb = np.broadcast_to(br.astype(np.float32), (128, E)).copy()

    in_maps = []
    for r in range(NCORE):
        b = r // 4
        cA, cB = r % 4, 7 - (r % 4)
        x_mine = np.ascontiguousarray(
            np.concatenate([x[b, cA * 1024:(cA + 1) * 1024],
                            x[b, cB * 1024:(cB + 1) * 1024]], 0), dtype=np.float32)
        sel = np.zeros((16, 2), np.float32)
        for c in range(8):
            rr, off = (b * 4 + min(c, 7 - c)), (0 if c < 4 else 1)
            row = 2 * rr + off
            if c < cA:
                sel[row, 0] = 1.0
            if cA < c < cB:
                sel[row, 1] = 1.0
        consts = np.zeros((128, 4), np.float32)
        consts[:, 0] = b * CAP
        consts[:, 1] = r * T_OWN + 1
        b1c = np.ascontiguousarray(b1[r].reshape(32, 128).T, dtype=np.float32)
        icon = np.zeros((1, 16), np.int32)
        gpos = [cA * 1024, cA * 1024 + 512, cB * 1024, cB * 1024 + 512]
        for i in range(4):
            icon[0, i] = gpos[i] // 512
            icon[0, 8 + i] = gpos[i] // 128 + 4
        icon[0, 4] = cA * 1024
        icon[0, 5] = cB * 1024
        icon[0, 6] = 16 * (sum(gp_ // 128 + 4 for gp_ in gpos) // 4) + 65
        in_maps.append({
            "x_my": x_mine,
            "wr_in": wr,
            "w1_in": np.ascontiguousarray(w1[r], dtype=np.float32),
            "w2_in": np.ascontiguousarray(w2[r], dtype=np.float32),
            "b1c_in": b1c,
            "b2r_in": np.ascontiguousarray(b2[r].reshape(1, D), dtype=np.float32),
            "brb_in": brb,
            "ident_in": ident,
            "iota8_in": iota8,
            "selpref_in": sel,
            "consts_in": consts,
            "wblk_in": wblk,
            "wv_in": wv,
            "binc_in": binc,
            "binv_in": binv,
            "woutT_in": woutT,
            "lnw4_in": lnw4,
            "hpar_in": hpar,
            "masks_in": masks,
            "icon_in": icon,
        })
    return in_maps


def _gather_output(results):
    out = np.zeros((2, 8192, D), np.float32)
    for r in range(NCORE):
        b = r // 4
        cA, cB = r % 4, 7 - (r % 4)
        y = results[r]["y_my"]
        out[b, cA * 1024:(cA + 1) * 1024] = y[0:1024]
        out[b, cB * 1024:(cB + 1) * 1024] = y[1024:2048]
    return out


def kernel(**inputs):
    nc = build_program(phases=("ffn1", "attn", "ffn2"))
    in_maps = _stage_inputs(inputs)
    res = run_bass_kernel_spmd(nc, in_maps, core_ids=list(range(NCORE)))
    return _gather_output(res.results)


def kernel_debug(**inputs):
    nc = build_program(phases=("ffn1",), debug=True)
    in_maps = _stage_inputs(inputs)
    res = run_bass_kernel_spmd(nc, in_maps, core_ids=list(range(NCORE)))
    return res.results
